# version 07
# baseline (speedup 1.0000x reference)
import numpy as np
import ml_dtypes
from contextlib import ExitStack

import concourse.bass as bass
import concourse.mybir as mybir
from concourse.bass_utils import run_bass_kernel_spmd

F32 = mybir.dt.float32
BF16 = mybir.dt.bfloat16
I32 = mybir.dt.int32
AF = mybir.ActivationFunctionType
ALU = mybir.AluOpType
AX = mybir.AxisListType
BFNP = ml_dtypes.bfloat16

D = 1024
TOK = 4096
TILE = 512
NT = TOK // TILE
G = TILE // 128
DC = D // 128
H = 16
DH = 64
FFH = 4096
HC = FFH // 128
NEG = -1.0e30
SEQ_TILES = 2048 // TILE

ENGS = ("sync", "tensor", "vector", "scalar", "gpsimd")


class Prog:

    def __init__(self, cnt):
        self.ops = {e: [] for e in ENGS}
        self.cnt = cnt
        self.last_wait = {e: {} for e in ENGS}
        self.fence_snap = {}

    def emit(self, eng, fn, waits=(), inc=None, n=1):
        own = _OWN.get(eng)
        merged = dict(self.fence_snap)
        if own in ("V", "S"):
            if inc is None:
                inc = own
            merged[own] = self.cnt.get(own, 0)
        for s, v in waits:
            if v > merged.get(s, 0):
                merged[s] = v
        w = []
        for s, v in merged.items():
            if s == "T" and eng == "tensor":
                continue
            if v > self.last_wait[eng].get(s, 0):
                self.last_wait[eng][s] = v
                w.append((s, v))
        self.ops[eng].append((fn, w, inc, n))
        if inc is not None:
            self.cnt[inc] = self.cnt.get(inc, 0) + n
            return self.cnt[inc]
        return None

    def fence(self):
        self.fence_snap = {s: self.cnt.get(s, 0) for s in ("T", "V", "S")}

    def replay(self, block, sems):
        for ename in ENGS:
            ops = self.ops[ename]

            def body(eng, ops=ops):
                for fn, w, inc, n in ops:
                    for s, v in w:
                        eng.wait_ge(sems[s], v)
                    instr = fn(eng)
                    if inc is not None:
                        instr.then_inc(sems[inc], n)
            getattr(block, ename)(body)


_OWN = {"tensor": "T", "vector": "V", "scalar": "S"}


DBG = False


def build_program(dbg=False):
    nc = bass.Bass("TRN2", target_bir_lowering=False, detect_race_conditions=False)

    d_x = nc.dram_tensor("x", [TOK, D], F32, kind="ExternalInput").ap()
    d_gamma = nc.dram_tensor("gamma", [D], F32, kind="ExternalInput").ap()
    d_beta = nc.dram_tensor("beta", [D], F32, kind="ExternalInput").ap()
    d_wqkv = nc.dram_tensor("wqkv", [D, 3 * D], F32, kind="ExternalInput").ap()
    d_wout = nc.dram_tensor("wout", [D, D], F32, kind="ExternalInput").ap()
    d_wfc = nc.dram_tensor("wfc", [FFH, D], F32, kind="ExternalInput").ap()
    d_bfc = nc.dram_tensor("bfc", [FFH], F32, kind="ExternalInput").ap()
    d_afc = nc.dram_tensor("afc", [16, D], F32, kind="ExternalInput").ap()
    d_bfcB = nc.dram_tensor("bfcB", [FFH, 16], F32, kind="ExternalInput").ap()
    d_wpr = nc.dram_tensor("wpr", [D, FFH], F32, kind="ExternalInput").ap()
    d_bpr = nc.dram_tensor("bpr", [D], F32, kind="ExternalInput").ap()
    d_apr = nc.dram_tensor("apr", [16, FFH], F32, kind="ExternalInput").ap()
    d_bprB = nc.dram_tensor("bprB", [D, 16], F32, kind="ExternalInput").ap()
    d_identf = nc.dram_tensor("identf", [128, 128], F32, kind="ExternalInput").ap()
    d_mask = nc.dram_tensor("maskm", [128, 256], BF16, kind="ExternalInput").ap()
    d_mask0 = nc.dram_tensor("mask0", [128, 128], BF16, kind="ExternalInput").ap()
    d_onesf = nc.dram_tensor("onesf", [1, 128], F32, kind="ExternalInput").ap()

    d_y = nc.dram_tensor("y", [TOK, D], F32, kind="ExternalOutput").ap()
    d_dbg = {}
    if dbg:
        for name, shape, dt in [("wall", [128, 24576], BF16), ("bw", [128, 16], F32),
                                ("bwv", [1, 1024], F32), ("scl", [128, 8], F32),
                                ("btfc", [16, FFH], BF16), ("a2fc", [16, D], BF16),
                                ("stat", [128, 64], F32), ("xnfm", [128, 4096], BF16),
                                ("qd", [128, 4096], BF16), ("kd", [128, 4096], BF16),
                                ("vd", [128, 4096], BF16), ("ao", [128, 4096], BF16),
                                ("x1", [128, 4096], F32), ("hd", [128, HC * TILE], BF16),
                                ("wfcT0", [128, 1024], BF16), ("wprT0", [128, 4096], BF16)]:
            d_dbg[name] = nc.dram_tensor("dbg_" + name, shape, dt, kind="ExternalOutput").ap()

    d_wfcT = nc.dram_tensor("wfcT_scr", [FFH, D], BF16, kind="Internal").ap()
    d_wprT = nc.dram_tensor("wprT_scr", [D, FFH], BF16, kind="Internal").ap()
    d_woutb = nc.dram_tensor("wout_scr", [D, D], BF16, kind="Internal").ap()

    cnt = {}
    stack = ExitStack()
    e = stack.enter_context

    sems = {}
    for s in ("dc", "da", "db", "dwa", "dwb", "dx", "dy",
              "wo0", "wo1", "wf0", "wf1", "wp0", "wp1", "T", "V", "S", "BAR"):
        sems[s] = e(nc.semaphore(s))

    pb = [e(nc.psum_tensor(f"pb{i}", [128, 512], F32)) for i in range(8)]

    s_identf = e(nc.sbuf_tensor("sbident_f", [128, 128], F32))
    s_mask = e(nc.sbuf_tensor("sb_maskm", [128, 256], BF16))
    s_mask0 = e(nc.sbuf_tensor("sb_mask0", [128, 128], BF16))
    s_onesf = e(nc.sbuf_tensor("sb_onesf", [1, 128], F32))
    s_gcol = e(nc.sbuf_tensor("gcol", [128, 8], F32))
    s_g8col = e(nc.sbuf_tensor("g8col", [128, 8], F32))
    s_betab = e(nc.sbuf_tensor("betab", [128, 8], BF16))
    s_bw = e(nc.sbuf_tensor("bw", [128, 16], F32))
    s_bwv = e(nc.sbuf_tensor("bwv", [1, 1024], F32))
    s_bfc = e(nc.sbuf_tensor("bfccol", [128, 32], F32))
    s_bpr = e(nc.sbuf_tensor("bprcol", [128, 8], F32))
    s_cm05 = e(nc.sbuf_tensor("cm05", [128, 1], F32))
    s_cp05 = e(nc.sbuf_tensor("cp05", [128, 1], F32))
    s_scfc = e(nc.sbuf_tensor("scfc", [128, 1], F32))
    s_rsfc = e(nc.sbuf_tensor("rsfc", [128, 1], F32))
    s_scpr = e(nc.sbuf_tensor("scpr", [128, 1], F32))
    s_rspr = e(nc.sbuf_tensor("rspr", [128, 1], F32))
    s_am = e(nc.sbuf_tensor("amcols", [128, 32], F32))
    s_amc = e(nc.sbuf_tensor("amc", [128, 1], F32))
    s_amr = e(nc.sbuf_tensor("amr", [1, 128], F32))
    s_sc1 = e(nc.sbuf_tensor("sc1", [1, 1], F32))

    s_wall = e(nc.sbuf_tensor("wall", [128, DC * 3 * D], BF16))

    s_a2fc = e(nc.sbuf_tensor("a2fc", [16, D], BF16))
    s_btfc = e(nc.sbuf_tensor("btfc", [16, FFH], BF16))
    s_a2pr = e(nc.sbuf_tensor("a2pr", [16, FFH], BF16))
    s_btpr = e(nc.sbuf_tensor("btpr", [16, D], BF16))

    P = Prog(cnt)

    def dma(prog, eng, out_ap, in_ap, sem, slow=False):
        if slow:
            return prog.emit(eng, lambda E, o=out_ap, i=in_ap: E.dma_start(out=o, in_=i, allow_slow_non_contiguous=True), inc=sem, n=16)
        return prog.emit(eng, lambda E, o=out_ap, i=in_ap: E.dma_start(out=o, in_=i), inc=sem, n=16)

    def mm(prog, out_ap, lhsT, rhs, start, stop, tp=None, waits=()):
        def f(E, o=out_ap, l=lhsT, r=rhs):
            return E.matmul(o, l, r, start=start, stop=stop, tile_position=tp,
                            skip_group_check=True)
        return prog.emit("tensor", f, waits=waits, inc="T")

    def trans(prog, out_ap, in_ap, ident, start=True, stop=True, waits=()):
        def f(E, o=out_ap, i=in_ap, idn=ident):
            return E.matmul(o, i, idn, start=start, stop=stop, is_transpose=True,
                            skip_group_check=True)
        return prog.emit("tensor", f, waits=waits, inc="T")

    with ExitStack() as arena:
        a = arena.enter_context
        t_stage = a(nc.sbuf_tensor("stage", [128, 2 * FFH], F32))
        t_q = a(nc.sbuf_tensor("tq", [128, 1024], F32))
        t_k0 = a(nc.sbuf_tensor("tk0", [128, 1024], I32))
        t_k0f = a(nc.sbuf_tensor("tk0f", [128, 1024], F32))
        t_d = a(nc.sbuf_tensor("td", [128, 1024], F32))
        t_s1 = a(nc.sbuf_tensor("ts1", [128, 1024], F32))
        t_s2 = a(nc.sbuf_tensor("ts2", [128, 1024], F32))
        t_wqf = a(nc.sbuf_tensor("twqf", [128, FFH], F32))
        t_panel = a(nc.sbuf_tensor("tpanel", [128, FFH], BF16))
        t_small = a(nc.sbuf_tensor("tsmall", [128, 16], F32))
        t_arow = a(nc.sbuf_tensor("tarow", [16, FFH], F32))

        tok_dx = 0
        tok_dx = dma(P, "sync", s_identf[:], d_identf, "dc")
        tok_dx = dma(P, "sync", s_mask[:], d_mask, "dc")
        tok_dx = dma(P, "sync", s_mask0[:], d_mask0, "dc")
        tok_dx = dma(P, "sync", s_onesf[:], d_onesf, "dc")
        tok_dx = dma(P, "sync", s_gcol[:], d_gamma.rearrange("(c p) -> p c", p=128), "dc", slow=True)
        tok_dx = dma(P, "sync", t_small[:, 0:8], d_beta.rearrange("(c p) -> p c", p=128), "dc", slow=True)
        tok_dx = dma(P, "sync", s_bfc[:], d_bfc.rearrange("(c p) -> p c", p=128), "dc", slow=True)
        tok_dx = dma(P, "sync", s_bpr[:], d_bpr.rearrange("(c p) -> p c", p=128), "dc", slow=True)
        tok_consts = tok_dx
        P.emit("vector", lambda E: E.memset(s_cm05[:], -0.5))
        P.emit("vector", lambda E: E.memset(s_cp05[:], 0.5))
        P.emit("vector", lambda E: E.tensor_scalar_mul(s_g8col[:], s_gcol[:], 0.125),
               waits=[("dc", tok_consts)])
        tv = P.emit("vector", lambda E: E.tensor_copy(out=s_betab[:], in_=t_small[:, 0:8]),
                    waits=[("dc", tok_consts)], inc="V")

        cast_toks = []
        for dc in range(DC):
            w = [("S", cast_toks[dc - 2])] if dc >= 2 else []
            sem = "da" if dc % 2 == 0 else "db"
            tok_l = P.emit("sync",
                           lambda E, dc=dc: E.dma_start(out=t_stage[:, (dc % 2) * 3072:(dc % 2) * 3072 + 3072],
                                                        in_=d_wqkv[dc * 128:(dc + 1) * 128, :]),
                           waits=w, inc=sem, n=16)
            tok = P.emit("scalar",
                         lambda E, dc=dc: E.copy(out=s_wall[:, dc * 3072:(dc + 1) * 3072],
                                                 in_=t_stage[:, (dc % 2) * 3072:(dc % 2) * 3072 + 3072]),
                         waits=[(sem, tok_l)], inc="S")
            cast_toks.append(tok)

        P.fence()
        for oc in range(16):
            for dc in range(DC):
                mm(P, pb[0][:, oc:oc + 1],
                   s_wall[:, dc * 3072 + oc * 128: dc * 3072 + (oc + 1) * 128],
                   s_betab[:, dc:dc + 1],
                   start=(dc == 0), stop=(dc == DC - 1),
                   waits=[("S", cast_toks[-1]), ("V", tv)])
        tok_T = P.emit("tensor", lambda E: E.nop(), inc="T")
        for half in range(2):
            for dc in range(DC):
                mm(P, pb[1 + half][0:1, 0:512],
                   s_betab[:, dc:dc + 1],
                   s_wall[:, dc * 3072 + 2048 + half * 512: dc * 3072 + 2048 + (half + 1) * 512],
                   start=(dc == 0), stop=(dc == DC - 1))
        tok_T = P.emit("tensor", lambda E: E.nop(), inc="T")
        tok_bw = P.emit("vector", lambda E: E.tensor_copy(out=s_bw[:], in_=pb[0][:, 0:16]),
                        waits=[("T", tok_T)], inc="V")
        tok_bw = P.emit("vector", lambda E: E.tensor_scalar_mul(s_bw[:, 0:8], s_bw[:, 0:8], 0.125),
                        inc="V")
        for half in range(2):
            tok_bwv = P.emit("vector",
                             lambda E, half=half: E.tensor_copy(out=s_bwv[0:1, half * 512:(half + 1) * 512],
                                                                in_=pb[1 + half][0:1, 0:512]),
                             waits=[("T", tok_T)], inc="V")
        fold_toks = []
        for dc in range(DC):
            P.emit("scalar",
                   lambda E, dc=dc: E.activation(out=s_wall[:, dc * 3072:dc * 3072 + 1024],
                                                 in_=s_wall[:, dc * 3072:dc * 3072 + 1024],
                                                 func=AF.Copy, scale=s_g8col[:, dc:dc + 1]),
                   waits=[("V", tok_bwv)])
            tok = P.emit("scalar",
                         lambda E, dc=dc: E.activation(out=s_wall[:, dc * 3072 + 1024:dc * 3072 + 3072],
                                                       in_=s_wall[:, dc * 3072 + 1024:dc * 3072 + 3072],
                                                       func=AF.Copy, scale=s_gcol[:, dc:dc + 1]),
                         inc="S")
            fold_toks.append(tok)

        P.fence()
        st_toks = [0, 0]
        for ic in range(DC):
            sem = "da" if ic % 2 == 0 else "db"
            ssem = "dwa" if ic % 2 == 0 else "dwb"
            tok_l = dma(P, "sync", t_stage[:, (ic % 2) * 4096:(ic % 2) * 4096 + 1024],
                        d_wout[ic * 128:(ic + 1) * 128, :], sem)
            tok_c = P.emit("vector",
                           lambda E, ic=ic: E.tensor_copy(out=t_panel[:, (ic % 2) * 1024:(ic % 2) * 1024 + 1024],
                                                          in_=t_stage[:, (ic % 2) * 4096:(ic % 2) * 4096 + 1024]),
                           waits=[(sem, tok_l), (ssem, st_toks[ic % 2])], inc="V")
            st_toks[ic % 2] = P.emit(
                "gpsimd",
                lambda E, ic=ic: E.dma_start(out=d_woutb[ic * 128:(ic + 1) * 128, :],
                                             in_=t_panel[:, (ic % 2) * 1024:(ic % 2) * 1024 + 1024]),
                waits=[("V", tok_c)], inc=ssem, n=16)
            P.fence()

        tok_a = dma(P, "sync", t_arow[:, 0:1024], d_afc, "da")
        tva = P.emit("vector", lambda E: E.tensor_scalar(out=s_a2fc[:], in0=t_arow[:16, 0:1024],
                                                         scalar1=2.0, scalar2=None, op0=ALU.mult),
                     waits=[("da", tok_a)], inc="V")
        tok_a = P.emit("sync", lambda E: E.dma_start(out=t_arow[:], in_=d_apr),
                       waits=[("V", tva)], inc="da", n=16)
        tva = P.emit("vector", lambda E: E.tensor_scalar(out=s_a2pr[:], in0=t_arow[:16, :],
                                                         scalar1=2.0, scalar2=None, op0=ALU.mult),
                     waits=[("da", tok_a)], inc="V")
        P.fence()
        for c in range(HC):
            sem = "da" if c % 2 == 0 else "db"
            tok_l = dma(P, "sync", t_stage[:, (c % 2) * 4096:(c % 2) * 4096 + 16],
                        d_bfcB[c * 128:(c + 1) * 128, :], sem)
            tt = trans(P, pb[2][0:16, (c % 2) * 128:(c % 2) * 128 + 128],
                       t_stage[:, (c % 2) * 4096:(c % 2) * 4096 + 16], s_identf[:],
                       waits=[(sem, tok_l)])
            P.emit("scalar",
                   lambda E, c=c: E.copy(out=s_btfc[:16, c * 128:(c + 1) * 128],
                                         in_=pb[2][0:16, (c % 2) * 128:(c % 2) * 128 + 128]),
                   waits=[("T", tt)], inc="S")
            P.fence()
        for c in range(DC):
            sem = "da" if c % 2 == 0 else "db"
            tok_l = dma(P, "sync", t_stage[:, (c % 2) * 4096:(c % 2) * 4096 + 16],
                        d_bprB[c * 128:(c + 1) * 128, :], sem)
            tt = trans(P, pb[2][0:16, (c % 2) * 128:(c % 2) * 128 + 128],
                       t_stage[:, (c % 2) * 4096:(c % 2) * 4096 + 16], s_identf[:],
                       waits=[(sem, tok_l)])
            P.emit("scalar",
                   lambda E, c=c: E.copy(out=s_btpr[:16, c * 128:(c + 1) * 128],
                                         in_=pb[2][0:16, (c % 2) * 128:(c % 2) * 128 + 128]),
                   waits=[("T", tt)], inc="S")
            P.fence()

        def quant_weight(d_w, nchunk, free, s_scale, s_rscale, am_ncols, d_scrT, a2, bt,
                         panel_cols, tpos_cols):
            P.fence()
            for c in range(nchunk):
                sl = (c % 2) * 4096
                sem = "da" if c % 2 == 0 else "db"
                tok_l = dma(P, "sync", t_stage[:, sl:sl + free], d_w[c * 128:(c + 1) * 128, :], sem)
                P.emit("vector",
                       lambda E, c=c, sl=sl: E.reduce_max(out=s_am[:, c:c + 1], in_=t_stage[:, sl:sl + free],
                                                          axis=AX.X, apply_absolute_value=True),
                       waits=[(sem, tok_l)], inc="V")
                P.fence()
            tok = P.emit("vector", lambda E: E.reduce_max(out=s_amc[:], in_=s_am[:, 0:am_ncols], axis=AX.X),
                         inc="V")
            tt = trans(P, pb[3][0:1, 0:128], s_amc[:], s_identf[:], waits=[("V", tok)])
            tok = P.emit("vector", lambda E: E.tensor_copy(out=s_amr[:], in_=pb[3][0:1, 0:128]),
                         waits=[("T", tt)], inc="V")
            tok = P.emit("vector", lambda E: E.reduce_max(out=s_sc1[:], in_=s_amr[:], axis=AX.X), inc="V")
            tok = P.emit("vector", lambda E: E.tensor_scalar_mul(s_sc1[:], s_sc1[:], float(np.float32(1.0) / np.float32(127.0))),
                         inc="V")
            tt = mm(P, pb[3][:, 0:1], s_onesf[:], s_sc1[:], start=True, stop=True,
                    waits=[("V", tok)])
            tok = P.emit("vector", lambda E: E.tensor_copy(out=s_scale[:], in_=pb[3][:, 0:1]),
                         waits=[("T", tt)], inc="V")
            tok = P.emit("vector", lambda E: E.reciprocal(out=s_rscale[:], in_=s_scale[:]), inc="V")
            P.fence()
            nsl = free // 1024
            tok_store = 0
            ev_bank = {}
            trans_last = 0
            for c in range(nchunk):
                sl = (c % 2) * 4096
                sem = "da" if c % 2 == 0 else "db"
                tok_l = dma(P, "sync", t_stage[:, sl:sl + free], d_w[c * 128:(c + 1) * 128, :], sem)
                for s in range(nsl):
                    src = t_stage[:, sl + s * 1024: sl + (s + 1) * 1024]
                    tq = P.emit("scalar",
                                lambda E, src=src: E.activation(out=t_q[:], in_=src, func=AF.Copy,
                                                                scale=s_rscale[:]),
                                waits=[(sem, tok_l)], inc="S")
                    tk = P.emit("vector", lambda E: E.tensor_copy(out=t_k0[:], in_=t_q[:]),
                                waits=[("S", tq)], inc="V")
                    tk = P.emit("vector", lambda E: E.tensor_copy(out=t_k0f[:], in_=t_k0[:]), inc="V")
                    tk = P.emit("vector", lambda E: E.tensor_tensor(out=t_d[:], in0=t_q[:], in1=t_k0f[:],
                                                                    op=ALU.subtract), inc="V")
                    t1 = P.emit("scalar", lambda E: E.activation(out=t_s1[:], in_=t_d[:], func=AF.Sign,
                                                                 bias=s_cm05[:]),
                                waits=[("V", tk)], inc="S")
                    t1 = P.emit("scalar", lambda E: E.activation(out=t_s2[:], in_=t_d[:], func=AF.Sign,
                                                                 bias=s_cp05[:]), inc="S")
                    tk = P.emit("vector", lambda E: E.tensor_tensor(out=t_s1[:], in0=t_s1[:], in1=t_s2[:],
                                                                    op=ALU.add),
                                waits=[("S", t1)], inc="V")
                    tk = P.emit("vector", lambda E: E.scalar_tensor_tensor(out=t_d[:], in0=t_s1[:], scalar=0.5,
                                                                           in1=t_k0f[:], op0=ALU.mult, op1=ALU.add),
                                inc="V")
                    tw = P.emit("scalar",
                                lambda E, s=s: E.activation(out=t_wqf[:, s * 1024:(s + 1) * 1024], in_=t_d[:],
                                                            func=AF.Copy, scale=s_scale[:]),
                                waits=[("V", tk), ("T", trans_last)], inc="S")
                nblk = free // 128
                ev_tok = None
                for b in range(nblk):
                    bi = 4 + ((b // 4) % 2)
                    bank = pb[bi]
                    tt = trans(P, bank[:, (b % 4) * 128:(b % 4) * 128 + 128],
                               t_wqf[:, b * 128:(b + 1) * 128], s_identf[:],
                               start=True, stop=False,
                               waits=[("S", tw), ("S", ev_bank.get(bi, 0)),
                                      ("dwa", tok_store)])
                    tt = mm(P, bank[:, (b % 4) * 128:(b % 4) * 128 + 128],
                            a2[:16, b * 128:(b + 1) * 128],
                            bt[:16, c * 128:(c + 1) * 128],
                            start=False, stop=True)
                    if b % 4 == 3:
                        ev_tok = P.emit("scalar",
                                        lambda E, b=b, bank=bank: E.copy(out=t_panel[:, (b - 3) * 128:(b + 1) * 128],
                                                                         in_=bank[:]),
                                        waits=[("T", tt)], inc="S")
                        ev_bank[bi] = ev_tok
                trans_last = tt
                tok_store = P.emit("gpsimd",
                                   lambda E, c=c: E.dma_start(out=d_scrT[c * 128:(c + 1) * 128, :],
                                                              in_=t_panel[:, 0:free]),
                                   waits=[("S", ev_tok)], inc="dwa", n=16)
                P.fence()

        quant_weight(d_wfc, HC, 1024, s_scfc, s_rsfc, HC, d_wfcT, s_a2fc, s_btfc, 1024, None)
        quant_weight(d_wpr, DC, 4096, s_scpr, s_rspr, DC, d_wprT, s_a2pr, s_btpr, 4096, None)

        if dbg:
            P.fence()
            P.emit("sync", lambda E: E.nop(),
                   waits=[(k, cnt.get(k, 0)) for k in ("T", "V", "S")])
            dma(P, "sync", d_dbg["wall"], s_wall[:], "dc")
            dma(P, "sync", d_dbg["bw"], s_bw[:], "dc")
            dma(P, "sync", d_dbg["bwv"], s_bwv[:], "dc")
            P.emit("vector", lambda E: E.tensor_copy(out=s_am[:, 0:1], in_=s_scfc[:]))
            P.emit("vector", lambda E: E.tensor_copy(out=s_am[:, 1:2], in_=s_rsfc[:]))
            P.emit("vector", lambda E: E.tensor_copy(out=s_am[:, 2:3], in_=s_scpr[:]))
            tvd = P.emit("vector", lambda E: E.tensor_copy(out=s_am[:, 3:4], in_=s_rspr[:]), inc="V")
            dma(P, "sync", d_dbg["scl"], s_am[:, 0:8], "dc")
            dma(P, "sync", d_dbg["btfc"], s_btfc[:], "dc")
            dma(P, "sync", d_dbg["a2fc"], s_a2fc[:], "dc")
            dma(P, "sync", d_dbg["wfcT0"], d_wfcT[0:128, :], "dc")
            dma(P, "sync", d_dbg["wprT0"], d_wprT[0:128, :], "dc")
        for eng in ENGS:
            P.emit(eng, lambda E: E.nop(), inc="BAR")

        with nc.Block("setup") as b1:
            P.replay(b1, sems)

    s_x = e(nc.sbuf_tensor("xsb", [128, TILE // 128 * D], F32))
    s_xnb = e(nc.sbuf_tensor("xnb", [128, D], F32))
    s_xnfm = e(nc.sbuf_tensor("xnfm", [128, DC * TILE], BF16))
    s_q = e(nc.sbuf_tensor("qsb", [128, 8 * TILE], BF16))
    s_k = e(nc.sbuf_tensor("ksb", [128, 8 * TILE], BF16))
    s_v = e(nc.sbuf_tensor("vsb", [128, G * D], BF16))
    s_kprev = e(nc.sbuf_tensor("kprev", [128, 8 * 128], BF16))
    s_vprev = e(nc.sbuf_tensor("vprev", [128, D], BF16))
    s_ao = e(nc.sbuf_tensor("aosb", [128, 8 * TILE], BF16))
    s_h = e(nc.sbuf_tensor("hsb", [128, HC * TILE], BF16))
    s_sm = e(nc.sbuf_tensor("smbuf", [128, 4 * 256], F32))
    s_probs = e(nc.sbuf_tensor("probs", [128, 4 * 256], F32))
    s_pT = e(nc.sbuf_tensor("pT", [128, 4 * 256], BF16))
    s_stat = e(nc.sbuf_tensor("stat", [128, 64], F32))
    s_ofm = e(nc.sbuf_tensor("ofm", [128, 512], F32))
    s_wfcb = e(nc.sbuf_tensor("wfcb", [128, 2 * 1024], BF16))
    s_wprb = e(nc.sbuf_tensor("wprb", [128, 2 * 4096], BF16))
    s_woutb = e(nc.sbuf_tensor("woutb", [128, 2 * 1024], BF16))

    NEGMAX, ROWSUM, RECIP = 0, 4, 52
    SUM, SSQ, MU, MSQ, VARP, R1, V2P, ASC, NMA, T0 = 8, 12, 16, 20, 24, 28, 32, 36, 40, 44

    M = Prog(cnt)
    M.fence_snap = {}
    for eng in ENGS:
        M.emit(eng, lambda E: E.nop(), waits=[("BAR", cnt["BAR"])])

    tok_store = 0
    wout_cons = [0, 0]
    wfc_cons = [0, 0]
    wpr_cons = [0, 0]
    ofm_cons = [0, 0]
    ofm_trans_last = [0]
    xnb_cons = [0, 0]
    xnt_prev_ev = [0, 0]

    def emit_norm_transpose(M, a_col, nma_col):
        ev_last = None
        for g in range(G):
            sl = 0
            tok_x = M.emit("vector",
                           lambda E, g=g, sl=sl, a_col=a_col, nma_col=nma_col:
                           E.tensor_scalar(out=s_xnb[:, sl * D:(sl + 1) * D],
                                           in0=s_x[:, g * D:(g + 1) * D],
                                           scalar1=s_stat[:, a_col + g:a_col + g + 1],
                                           scalar2=s_stat[:, nma_col + g:nma_col + g + 1],
                                           op0=ALU.mult, op1=ALU.add),
                           waits=[("T", xnb_cons[sl])], inc="V")
            tt = None
            for half in range(2):
                bank = pb[half]
                for i in range(4):
                    dc = half * 4 + i
                    tt = trans(M, bank[:, i * 128:(i + 1) * 128],
                               s_xnb[:, sl * D + dc * 128: sl * D + (dc + 1) * 128],
                               s_identf[:],
                               waits=[("V", tok_x), ("S", xnt_prev_ev[half])])
                dst = bass.AP(s_xnfm, half * 4 * TILE + g * 128,
                              [[DC * TILE, 128], [TILE, 4], [1, 128]])
                tok = M.emit("scalar",
                             lambda E, bank=bank, dst=dst:
                             E.copy(out=dst, in_=bank[:].rearrange("p (a b) -> p a b", b=128)),
                             waits=[("T", tt)], inc="S")
                xnt_prev_ev[half] = tok
                ev_last = tok
            xnb_cons[sl] = tt
        return ev_last

    for t in range(NT):
        seq_start = (t % SEQ_TILES == 0)
        tok_x = dma(M, "sync", s_x[:].rearrange("p (g d) -> p g d", d=D),
                    d_x[t * TILE:(t + 1) * TILE, :].rearrange("(g p) d -> p g d", p=128),
                    "dx")

        def stats_and_norm(wait_tok, wait_sem, r_out_cols):
            tv = ts = None
            for g in range(G):
                xg = s_x[:, g * D:(g + 1) * D]
                tv = M.emit("vector",
                            lambda E, xg=xg, g=g: E.tensor_scalar(out=s_xnfm[:, 0:1024], in0=xg, scalar1=1.0,
                                                                  scalar2=None, op0=ALU.mult, op1=ALU.add,
                                                                  accum_out=s_stat[:, SUM + g:SUM + g + 1]),
                            waits=[(wait_sem, wait_tok)], inc="V")
                ts = M.emit("scalar",
                            lambda E, xg=xg, g=g: E.activation(out=s_xnfm[:, 1024:2048], in_=xg, func=AF.Square,
                                                               accum_out=s_stat[:, SSQ + g:SSQ + g + 1]),
                            waits=[(wait_sem, wait_tok)], inc="S")
            c = 1.0 / 1024.0
            tv = M.emit("vector", lambda E: E.tensor_scalar_mul(s_stat[:, MU:MU + 4], s_stat[:, SUM:SUM + 4], c),
                        waits=[("V", tv)], inc="V")
            tv = M.emit("vector", lambda E: E.tensor_scalar_mul(s_stat[:, MSQ:MSQ + 4], s_stat[:, SSQ:SSQ + 4], c),
                        waits=[("S", ts)], inc="V")
            tv = M.emit("vector", lambda E: E.tensor_tensor(out=s_stat[:, T0:T0 + 4], in0=s_stat[:, MU:MU + 4],
                                                            in1=s_stat[:, MU:MU + 4], op=ALU.mult), inc="V")
            tv = M.emit("vector", lambda E: E.tensor_tensor(out=s_stat[:, VARP:VARP + 4], in0=s_stat[:, MSQ:MSQ + 4],
                                                            in1=s_stat[:, T0:T0 + 4], op=ALU.subtract), inc="V")
            tv = M.emit("vector", lambda E: E.tensor_scalar_add(s_stat[:, VARP:VARP + 4], s_stat[:, VARP:VARP + 4],
                                                                1e-5), inc="V")
            ts = M.emit("scalar", lambda E: E.activation(out=s_stat[:, T0:T0 + 4], in_=s_stat[:, VARP:VARP + 4],
                                                         func=AF.Ln),
                        waits=[("V", tv)], inc="S")
            ts = M.emit("scalar", lambda E: E.activation(out=s_stat[:, R1:R1 + 4], in_=s_stat[:, T0:T0 + 4],
                                                         func=AF.Exp, scale=s_cm05[:]), inc="S")
            return tv, ts

        tv, ts = stats_and_norm(tok_x, "dx", R1)
        T1 = 48
        tv = M.emit("vector", lambda E: E.reciprocal(out=s_stat[:, T1:T1 + 4], in_=s_stat[:, VARP:VARP + 4]),
                    inc="V")
        tv = M.emit("vector", lambda E: E.tensor_scalar_add(s_stat[:, V2P:V2P + 4], s_stat[:, VARP:VARP + 4], -1e-5),
                    inc="V")
        tv = M.emit("vector", lambda E: E.tensor_tensor(out=s_stat[:, V2P:V2P + 4], in0=s_stat[:, V2P:V2P + 4],
                                                        in1=s_stat[:, T1:T1 + 4], op=ALU.mult), inc="V")
        tv = M.emit("vector", lambda E: E.tensor_scalar_add(s_stat[:, V2P:V2P + 4], s_stat[:, V2P:V2P + 4], 1e-5),
                    inc="V")
        ts = M.emit("scalar", lambda E: E.activation(out=s_stat[:, T1:T1 + 4], in_=s_stat[:, V2P:V2P + 4],
                                                     func=AF.Ln),
                    waits=[("V", tv), ("S", ts)], inc="S")
        ts = M.emit("scalar", lambda E: E.activation(out=s_stat[:, V2P:V2P + 4], in_=s_stat[:, T1:T1 + 4],
                                                     func=AF.Exp, scale=s_cm05[:]), inc="S")
        tv = M.emit("vector", lambda E: E.tensor_tensor(out=s_stat[:, ASC:ASC + 4], in0=s_stat[:, R1:R1 + 4],
                                                        in1=s_stat[:, V2P:V2P + 4], op=ALU.mult),
                    waits=[("S", ts)], inc="V")
        tv = M.emit("vector", lambda E: E.scalar_tensor_tensor(out=s_stat[:, NMA:NMA + 4], in0=s_stat[:, MU:MU + 4],
                                                               scalar=-1.0, in1=s_stat[:, ASC:ASC + 4],
                                                               op0=ALU.mult, op1=ALU.mult), inc="V")
        tok_xnfm = emit_norm_transpose(M, ASC, NMA)

        if dbg and t == 0:
            M.fence()
            M.emit("sync", lambda E: E.nop(), waits=[(k, cnt.get(k, 0)) for k in ("T", "V", "S")])
            dma(M, "sync", d_dbg["stat"], s_stat[:], "dc")
            dma(M, "sync", d_dbg["xnfm"], s_xnfm[:], "dc")
            for _e in ("vector", "scalar", "tensor", "gpsimd"):
                M.emit(_e, lambda E: E.nop(), waits=[("dc", cnt["dc"])])
        M.fence()
        qk_ev = []
        for oc in range(16):
            bank = pb[2 + (oc % 2)]
            tt = None
            for dc in range(DC):
                tt = mm(M, bank[:],
                        s_wall[:, dc * 3072 + oc * 128: dc * 3072 + (oc + 1) * 128],
                        s_xnfm[:, dc * TILE:(dc + 1) * TILE],
                        start=(dc == 0), stop=(dc == DC - 1),
                        waits=[("S", qk_ev[oc - 2] if oc >= 2 else 0)])
            dst = s_q if oc < 8 else s_k
            fc = oc % 8
            tok = M.emit("scalar",
                         lambda E, dst=dst, fc=fc, oc=oc, bank=bank:
                         E.activation(out=dst[:, fc * TILE:(fc + 1) * TILE], in_=bank[:],
                                      func=AF.Identity, bias=s_bw[:, oc:oc + 1]),
                         waits=[("T", tt)], inc="S")
            qk_ev.append(tok)
        v_ev = []
        for g in range(G):
            bh = [pb[4 + (g % 2) * 2], pb[5 + (g % 2) * 2]]
            tt = None
            for half in range(2):
                for dc in range(DC):
                    tt = mm(M, bh[half][:],
                            s_xnfm[:, dc * TILE + g * 128: dc * TILE + (g + 1) * 128],
                            s_wall[:, dc * 3072 + 2048 + half * 512: dc * 3072 + 2048 + (half + 1) * 512],
                            start=(dc == 0), stop=False,
                            waits=[("V", v_ev[g - 2] if g >= 2 else 0)])
                tt = mm(M, bh[half][:], s_onesf[:], s_bwv[0:1, half * 512:(half + 1) * 512],
                        start=False, stop=True)
            tok = None
            for half in range(2):
                tok = M.emit("vector",
                             lambda E, g=g, half=half, b=bh[half]:
                             E.tensor_copy(out=s_v[:, g * D + half * 512: g * D + (half + 1) * 512], in_=b[:]),
                             waits=[("T", tt)], inc="V")
            v_ev.append(tok)

        if dbg and t == 0:
            M.fence()
            M.emit("sync", lambda E: E.nop(), waits=[(k, cnt.get(k, 0)) for k in ("T", "V", "S")])
            dma(M, "sync", d_dbg["qd"], s_q[:], "dc")
            dma(M, "sync", d_dbg["kd"], s_k[:], "dc")
            dma(M, "sync", d_dbg["vd"], s_v[:], "dc")
            for _e in ("vector", "scalar", "tensor", "gpsimd"):
                M.emit(_e, lambda E: E.nop(), waits=[("dc", cnt["dc"])])
        M.fence()
        pv_ev_last = 0
        sc_cons = [0, 0, 0, 0]
        pT_cons = [0, 0]
        pv_cons = [0, 0]
        pv_eng = [0, 0]
        exp_cons = [0, 0, 0, 0]
        ptr_cons = [0, 0, 0, 0]
        for w in range(G):
            first_win = (seq_start and w == 0)
            nk = 128 if first_win else 256
            for h in range(H):
                i = w * H + h
                wh = i % 4
                p2 = i % 2
                po = 64 * (h % 2)
                fc = h // 2
                qs = s_q[po:po + 64, fc * TILE + w * 128: fc * TILE + (w + 1) * 128]
                bankS = pb[wh]
                if first_win:
                    tt = mm(M, bankS[:, 0:128], qs,
                            s_k[po:po + 64, fc * TILE: fc * TILE + 128],
                            start=True, stop=True, tp=(po, 0),
                            waits=[("S", qk_ev[8 + fc]), ("V", sc_cons[wh])])
                elif w == 0:
                    tt = mm(M, bankS[:, 0:128], qs,
                            s_kprev[po:po + 64, fc * 128:(fc + 1) * 128],
                            start=True, stop=True, tp=(po, 0),
                            waits=[("S", qk_ev[8 + fc]), ("V", sc_cons[wh])])
                    tt = mm(M, bankS[:, 128:256], qs,
                            s_k[po:po + 64, fc * TILE: fc * TILE + 128],
                            start=True, stop=True, tp=(po, 0))
                else:
                    tt = mm(M, bankS[:, 0:256], qs,
                            s_k[po:po + 64, fc * TILE + (w - 1) * 128: fc * TILE + (w + 1) * 128],
                            start=True, stop=True, tp=(po, 0),
                            waits=[("S", qk_ev[8 + fc]), ("V", sc_cons[wh])])
                msk = s_mask0 if first_win else s_mask
                tmask = M.emit("vector",
                               lambda E, wh=wh, nk=nk, msk=msk, bankS=bankS:
                               E.tensor_tensor(out=s_sm[:, wh * 256: wh * 256 + nk], in0=bankS[:, 0:nk],
                                               in1=msk[:, 0:nk], op=ALU.add),
                               waits=[("T", tt), ("S", exp_cons[wh])], inc="V")
                sc_cons[wh] = tmask
                tvv = M.emit("vector",
                             lambda E, wh=wh, nk=nk: E.reduce_max(out=s_stat[:, NEGMAX + wh:NEGMAX + wh + 1],
                                                                  in_=s_sm[:, wh * 256: wh * 256 + nk],
                                                                  axis=AX.X, negate=True),
                             inc="V")
                tss = M.emit("scalar",
                             lambda E, wh=wh, nk=nk: E.activation(out=s_probs[:, wh * 256: wh * 256 + nk],
                                                                  in_=s_sm[:, wh * 256: wh * 256 + nk],
                                                                  func=AF.Exp,
                                                                  bias=s_stat[:, NEGMAX + wh:NEGMAX + wh + 1],
                                                                  accum_out=s_stat[:, ROWSUM + wh:ROWSUM + wh + 1]),
                             waits=[("V", tvv), ("T", ptr_cons[wh])], inc="S")
                exp_cons[wh] = tss
                tvv = M.emit("vector",
                             lambda E, wh=wh: E.reciprocal(out=s_stat[:, RECIP + wh:RECIP + wh + 1],
                                                           in_=s_stat[:, ROWSUM + wh:ROWSUM + wh + 1]),
                             waits=[("S", tss)], inc="V")
                tnorm = M.emit("vector",
                               lambda E, wh=wh, nk=nk: E.tensor_scalar_mul(s_probs[:, wh * 256: wh * 256 + nk],
                                                                           s_probs[:, wh * 256: wh * 256 + nk],
                                                                           s_stat[:, RECIP + wh:RECIP + wh + 1]),
                               inc="V")
                bankT = pb[4 + p2]
                nchunk = nk // 128
                tt2 = None
                for kc in range(nchunk):
                    tt2 = trans(M, bankT[:, kc * 128:(kc + 1) * 128],
                                s_probs[:, wh * 256 + kc * 128: wh * 256 + (kc + 1) * 128],
                                s_identf[:],
                                waits=[("V", tnorm), ("S", pT_cons[p2])])
                ptr_cons[wh] = tt2
                tpev = M.emit("scalar",
                              lambda E, wh=wh, nk=nk, bankT=bankT:
                              E.copy(out=s_pT[:, wh * 256: wh * 256 + nk], in_=bankT[:, 0:nk]),
                              waits=[("T", tt2)], inc="S")
                pT_cons[p2] = tpev
                bankO = pb[6 + p2]
                pvw = [("S", tpev), (("V", "S")[pv_eng[p2]], pv_cons[p2]), ("V", v_ev[w])]
                if first_win:
                    tt3 = mm(M, bankO[po:po + 64, 0:128],
                             s_v[:, w * D + h * DH: w * D + (h + 1) * DH],
                             s_pT[:, wh * 256: wh * 256 + 128],
                             start=True, stop=True, tp=(0, po), waits=pvw)
                else:
                    vprev_ap = (s_vprev[:, h * DH:(h + 1) * DH] if w == 0
                                else s_v[:, (w - 1) * D + h * DH: (w - 1) * D + h * DH + DH])
                    tt3 = mm(M, bankO[po:po + 64, 0:128],
                             vprev_ap, s_pT[:, wh * 256: wh * 256 + 128],
                             start=True, stop=False, tp=(0, po), waits=pvw)
                    tt3 = mm(M, bankO[po:po + 64, 0:128],
                             s_v[:, w * D + h * DH: w * D + (h + 1) * DH],
                             s_pT[:, wh * 256 + 128: wh * 256 + 256],
                             start=False, stop=True, tp=(0, po))
                eng = "vector" if (i % 2 == 0) else "scalar"
                if eng == "vector":
                    tok = M.emit("vector",
                                 lambda E, po=po, fc=fc, w=w, bankO=bankO:
                                 E.tensor_copy(out=s_ao[po:po + 64, fc * TILE + w * 128: fc * TILE + (w + 1) * 128],
                                               in_=bankO[po:po + 64, 0:128]),
                                 waits=[("T", tt3)], inc="V")
                    pv_eng[p2] = 0
                else:
                    tok = M.emit("scalar",
                                 lambda E, po=po, fc=fc, w=w, bankO=bankO:
                                 E.copy(out=s_ao[po:po + 64, fc * TILE + w * 128: fc * TILE + (w + 1) * 128],
                                        in_=bankO[po:po + 64, 0:128]),
                                 waits=[("T", tt3)], inc="S")
                    pv_eng[p2] = 1
                pv_cons[p2] = tok
                pv_ev_last = tok
        tok = M.emit("vector", lambda E: E.engine_nop() if False else E.memset(s_stat[:, 60:61], 0.0),
                     waits=[("S", pv_ev_last if pv_eng[(G * H - 1) % 2] else 0)], inc="V")
        pv_ev_last = tok

        if (t + 1) % SEQ_TILES != 0:
            kp = M.emit("vector",
                        lambda E: E.tensor_copy(
                            out=s_kprev[:].rearrange("p (f c) -> p f c", c=128),
                            in_=s_k[:].rearrange("p (f t) -> p f t", t=TILE)[:, :, TILE - 128:TILE]),
                        waits=[("S", qk_ev[-1])], inc="V")
            vp = M.emit("vector",
                        lambda E: E.tensor_copy(out=s_vprev[:], in_=s_v[:, (G - 1) * D: G * D]),
                        waits=[("V", v_ev[-1])], inc="V")

        if dbg and t == 0:
            M.fence()
            M.emit("sync", lambda E: E.nop(), waits=[(k, cnt.get(k, 0)) for k in ("T", "V", "S")])
            dma(M, "sync", d_dbg["ao"], s_ao[:], "dc")
            for _e in ("vector", "scalar", "tensor", "gpsimd"):
                M.emit(_e, lambda E: E.nop(), waits=[("dc", cnt["dc"])])
        M.fence()
        res_toks = []
        for ic in range(DC):
            sl = ic % 2
            tok_w = M.emit("gpsimd",
                           lambda E, ic=ic, sl=sl: E.dma_start(out=s_woutb[:, sl * 1024:(sl + 1) * 1024],
                                                               in_=d_woutb[ic * 128:(ic + 1) * 128, :]),
                           waits=[("T", wout_cons[sl])], inc="wo%d" % sl, n=16)
            tt = None
            for g in range(G):
                for half in range(2):
                    tt = mm(M, pb[2 * g + half][:],
                            s_ao[:, ic * TILE + g * 128: ic * TILE + (g + 1) * 128],
                            s_woutb[:, sl * 1024 + half * 512: sl * 1024 + (half + 1) * 512],
                            start=(ic == 0), stop=(ic == DC - 1),
                            waits=[("wo%d" % sl, tok_w), ("V", pv_ev_last)])
            wout_cons[sl] = tt
        tok_T = M.emit("tensor", lambda E: E.nop(), inc="T")
        tvlast = None
        for g in range(G):
            for half in range(2):
                tvlast = M.emit("vector",
                                lambda E, g=g, half=half:
                                E.scalar_tensor_tensor(out=s_x[:, g * D + half * 512: g * D + (half + 1) * 512],
                                                       in0=pb[2 * g + half][:], scalar=1.0,
                                                       in1=s_x[:, g * D + half * 512: g * D + (half + 1) * 512],
                                                       op0=ALU.mult, op1=ALU.add),
                                waits=[("T", tok_T)], inc="V")

        if dbg and t == 0:
            M.fence()
            M.emit("sync", lambda E: E.nop(), waits=[(k, cnt.get(k, 0)) for k in ("T", "V", "S")])
            dma(M, "sync", d_dbg["x1"], s_x[:], "dc")
            for _e in ("vector", "scalar", "tensor", "gpsimd"):
                M.emit(_e, lambda E: E.nop(), waits=[("dc", cnt["dc"])])
        M.fence()
        tv, ts = stats_and_norm(tvlast, "V", R1)
        xnb_toks = []
        for g in range(G):
            tok = M.emit("vector",
                         lambda E, g=g: E.scalar_tensor_tensor(out=s_stat[:, NMA + g:NMA + g + 1],
                                                               in0=s_stat[:, MU + g:MU + g + 1], scalar=-1.0,
                                                               in1=s_stat[:, R1 + g:R1 + g + 1],
                                                               op0=ALU.mult, op1=ALU.mult),
                         waits=[("S", ts)], inc="V")
        tok_xnfm2 = emit_norm_transpose(M, R1, NMA)
        ev_tr = [tok_xnfm2]

        M.fence()
        h_toks = []
        for c in range(HC):
            sl = c % 2
            tok_w = M.emit("gpsimd",
                           lambda E, c=c, sl=sl: E.dma_start(out=s_wfcb[:, sl * 1024:(sl + 1) * 1024],
                                                             in_=d_wfcT[c * 128:(c + 1) * 128, :]),
                           waits=[("T", wfc_cons[sl])], inc="wf%d" % sl, n=16)
            bank = pb[2 + sl]
            tt = None
            for dc in range(DC):
                tt = mm(M, bank[:],
                        s_wfcb[:, sl * 1024 + dc * 128: sl * 1024 + (dc + 1) * 128],
                        s_xnfm[:, dc * TILE:(dc + 1) * TILE],
                        start=(dc == 0), stop=(dc == DC - 1),
                        waits=[("wf%d" % sl, tok_w), ("S", ev_tr[-1]),
                               ("S", h_toks[c - 2] if c >= 2 else 0)])
            wfc_cons[sl] = tt
            tok = M.emit("scalar",
                         lambda E, c=c, bank=bank: E.activation(out=s_h[:, c * TILE:(c + 1) * TILE],
                                                                in_=bank[:], func=AF.Gelu,
                                                                bias=s_bfc[:, c:c + 1]),
                         waits=[("T", tt)], inc="S")
            h_toks.append(tok)

        if dbg and t == 0:
            M.fence()
            M.emit("sync", lambda E: E.nop(), waits=[(k, cnt.get(k, 0)) for k in ("T", "V", "S")])
            dma(M, "sync", d_dbg["hd"], s_h[:], "dc")
            for _e in ("vector", "scalar", "tensor", "gpsimd"):
                M.emit(_e, lambda E: E.nop(), waits=[("dc", cnt["dc"])])
        M.fence()
        last_res = None
        for do in range(DC):
            sl = do % 2
            tok_w = M.emit("gpsimd",
                           lambda E, do=do, sl=sl: E.dma_start(out=s_wprb[:, sl * 4096:(sl + 1) * 4096],
                                                               in_=d_wprT[do * 128:(do + 1) * 128, :]),
                           waits=[("T", wpr_cons[sl])], inc="wp%d" % sl, n=16)
            bank = pb[0 + sl]
            tt = None
            for c in range(HC):
                tt = mm(M, bank[:],
                        s_wprb[:, sl * 4096 + c * 128: sl * 4096 + (c + 1) * 128],
                        s_h[:, c * TILE:(c + 1) * TILE],
                        start=(c == 0), stop=(c == HC - 1),
                        waits=[("wp%d" % sl, tok_w), ("S", h_toks[-1]),
                               ("V", ofm_cons[sl])])
            wpr_cons[sl] = tt
            tok_ev = M.emit("scalar",
                            lambda E, bank=bank, do=do:
                            E.activation(out=s_ofm[:, 0:512], in_=bank[:],
                                         func=AF.Identity, bias=s_bpr[:, do:do + 1]),
                            waits=[("T", tt), ("T", ofm_trans_last[0])], inc="S")
            bank2 = pb[4 + sl]
            tt2 = None
            for g in range(G):
                tt2 = trans(M, bank2[:, g * 128:(g + 1) * 128],
                            s_ofm[:, g * 128:(g + 1) * 128], s_identf[:],
                            waits=[("S", tok_ev), ("V", ofm_cons[sl])])
            ofm_trans_last[0] = tt2
            tok_res = M.emit("vector",
                             lambda E, do=do, bank2=bank2:
                             E.scalar_tensor_tensor(
                                 out=s_x[:].rearrange("p (g d) -> p g d", d=D)[:, :, do * 128:(do + 1) * 128],
                                 in0=bank2[:].rearrange("p (g c) -> p g c", c=128), scalar=1.0,
                                 in1=s_x[:].rearrange("p (g d) -> p g d", d=D)[:, :, do * 128:(do + 1) * 128],
                                 op0=ALU.mult, op1=ALU.add),
                             waits=[("T", tt2)], inc="V")
            ofm_cons[sl] = tok_res
            last_res = tok_res

        tok_store = M.emit("sync",
                           lambda E, t=t: E.dma_start(
                               out=d_y[t * TILE:(t + 1) * TILE, :].rearrange("(g p) d -> p g d", p=128),
                               in_=s_x[:].rearrange("p (g d) -> p g d", d=D)),
                           waits=[("V", last_res)], inc="dy", n=16)
        M.fence()
        M.emit("sync", lambda E: E.nop(), waits=[("dy", tok_store)])

    M.emit("sync", lambda E: E.nop(), waits=[("dy", cnt["dy"])])

    with nc.Block("main") as b2:
        M.replay(b2, sems)

    stack.close()
    return nc


_NC_CACHE = None


def kernel(L0_attn_x, L0_attn_ln_gamma, L0_attn_ln_beta, L0_attn_w_qkv, L0_attn_w_out,
           L0_ffn_W_fc, L0_ffn_b_fc, L0_ffn_A_fc, L0_ffn_B_fc,
           L0_ffn_W_proj, L0_ffn_b_proj, L0_ffn_A_proj, L0_ffn_B_proj):
    global _NC_CACHE
    x = np.ascontiguousarray(np.asarray(L0_attn_x, dtype=np.float32))
    B, N, Dm = x.shape
    ncores = 8
    per = B // ncores

    if _NC_CACHE is None:
        _NC_CACHE = build_program(dbg=DBG)
    nc = _NC_CACHE

    ident_f = np.eye(128, dtype=np.float32)
    ii = np.arange(128)[:, None]
    jj = np.arange(256)[None, :]
    prev = jj < 128
    jjm = np.where(prev, jj, jj - 128)
    bad = np.where(prev, jjm < ii, jjm > ii)
    mask_m = np.where(bad, np.float32(NEG), np.float32(0.0)).astype(BFNP)
    mask_0 = np.where(np.arange(128)[None, :] > ii, np.float32(NEG), np.float32(0.0)).astype(BFNP)
    ones_f = np.ones((1, 128), dtype=np.float32)

    common = dict(
        gamma=np.asarray(L0_attn_ln_gamma, np.float32),
        beta=np.asarray(L0_attn_ln_beta, np.float32),
        wqkv=np.asarray(L0_attn_w_qkv, np.float32),
        wout=np.asarray(L0_attn_w_out, np.float32),
        wfc=np.asarray(L0_ffn_W_fc, np.float32),
        bfc=np.asarray(L0_ffn_b_fc, np.float32),
        afc=np.asarray(L0_ffn_A_fc, np.float32),
        bfcB=np.asarray(L0_ffn_B_fc, np.float32),
        wpr=np.asarray(L0_ffn_W_proj, np.float32),
        bpr=np.asarray(L0_ffn_b_proj, np.float32),
        apr=np.asarray(L0_ffn_A_proj, np.float32),
        bprB=np.asarray(L0_ffn_B_proj, np.float32),
        identf=ident_f, maskm=mask_m, mask0=mask_0, onesf=ones_f,
    )
    in_maps = []
    for c in range(ncores):
        m = dict(common)
        m["x"] = np.ascontiguousarray(x[c * per:(c + 1) * per].reshape(TOK, Dm))
        in_maps.append(m)

    res = run_bass_kernel_spmd(nc, in_maps, core_ids=list(range(ncores)))
    out = np.empty((B, N, Dm), np.float32)
    for c in range(ncores):
        out[c * per:(c + 1) * per] = res.results[c]["y"].reshape(per, N, Dm)
    if DBG:
        global DBG_RES
        DBG_RES = res.results
    return out


# version 08
# speedup vs baseline: 1.3034x; 1.3034x over previous
import numpy as np
import ml_dtypes
from contextlib import ExitStack

import concourse.bass as bass
import concourse.mybir as mybir
from concourse.bass_utils import run_bass_kernel_spmd

F32 = mybir.dt.float32
BF16 = mybir.dt.bfloat16
I32 = mybir.dt.int32
AF = mybir.ActivationFunctionType
ALU = mybir.AluOpType
AX = mybir.AxisListType
BFNP = ml_dtypes.bfloat16

D = 1024
TOK = 4096
TILE = 512
NT = TOK // TILE
G = TILE // 128
DC = D // 128
H = 16
DH = 64
FFH = 4096
HC = FFH // 128
NEG = -1.0e30
SEQ_TILES = 2048 // TILE

ENGS = ("sync", "tensor", "vector", "scalar", "gpsimd")


class Prog:

    def __init__(self, cnt):
        self.ops = {e: [] for e in ENGS}
        self.cnt = cnt
        self.last_wait = {e: {} for e in ENGS}
        self.fence_snap = {}

    def emit(self, eng, fn, waits=(), inc=None, n=1):
        own = _OWN.get(eng)
        merged = dict(self.fence_snap)
        if own in ("V", "S"):
            if inc is None:
                inc = own
            merged[own] = self.cnt.get(own, 0)
        for s, v in waits:
            if v > merged.get(s, 0):
                merged[s] = v
        w = []
        for s, v in merged.items():
            if s == "T" and eng == "tensor":
                continue
            if v > self.last_wait[eng].get(s, 0):
                self.last_wait[eng][s] = v
                w.append((s, v))
        self.ops[eng].append((fn, w, inc, n))
        if inc is not None:
            self.cnt[inc] = self.cnt.get(inc, 0) + n
            return self.cnt[inc]
        return None

    def fence(self):
        self.fence_snap = {s: self.cnt.get(s, 0) for s in ("T", "V", "S")}

    def replay(self, block, sems):
        for ename in ENGS:
            ops = self.ops[ename]

            def body(eng, ops=ops):
                for fn, w, inc, n in ops:
                    for s, v in w:
                        eng.wait_ge(sems[s], v)
                    instr = fn(eng)
                    if inc is not None:
                        instr.then_inc(sems[inc], n)
            getattr(block, ename)(body)


_OWN = {"tensor": "T", "vector": "V", "scalar": "S"}


DBG = False


def build_program(dbg=False):
    nc = bass.Bass("TRN2", target_bir_lowering=False, detect_race_conditions=False)

    d_x = nc.dram_tensor("x", [TOK, D], F32, kind="ExternalInput").ap()
    d_gamma = nc.dram_tensor("gamma", [D], F32, kind="ExternalInput").ap()
    d_beta = nc.dram_tensor("beta", [D], F32, kind="ExternalInput").ap()
    d_wqkv = nc.dram_tensor("wqkv", [D, 3 * D], F32, kind="ExternalInput").ap()
    d_wout = nc.dram_tensor("wout", [D, D], F32, kind="ExternalInput").ap()
    d_wfc = nc.dram_tensor("wfc", [FFH, D], F32, kind="ExternalInput").ap()
    d_bfc = nc.dram_tensor("bfc", [FFH], F32, kind="ExternalInput").ap()
    d_afc = nc.dram_tensor("afc", [16, D], F32, kind="ExternalInput").ap()
    d_bfcB = nc.dram_tensor("bfcB", [FFH, 16], F32, kind="ExternalInput").ap()
    d_wpr = nc.dram_tensor("wpr", [D, FFH], F32, kind="ExternalInput").ap()
    d_bpr = nc.dram_tensor("bpr", [D], F32, kind="ExternalInput").ap()
    d_apr = nc.dram_tensor("apr", [16, FFH], F32, kind="ExternalInput").ap()
    d_bprB = nc.dram_tensor("bprB", [D, 16], F32, kind="ExternalInput").ap()
    d_identf = nc.dram_tensor("identf", [128, 128], F32, kind="ExternalInput").ap()
    d_mask = nc.dram_tensor("maskm", [128, 256], BF16, kind="ExternalInput").ap()
    d_mask0 = nc.dram_tensor("mask0", [128, 128], BF16, kind="ExternalInput").ap()
    d_onesf = nc.dram_tensor("onesf", [1, 128], F32, kind="ExternalInput").ap()
    d_onesb = nc.dram_tensor("onesb", [1, 128], BF16, kind="ExternalInput").ap()

    d_y = nc.dram_tensor("y", [TOK, D], F32, kind="ExternalOutput").ap()
    d_dbg = {}
    if dbg:
        for name, shape, dt in [("wall", [128, 24576], BF16), ("bw", [128, 16], F32),
                                ("bwv", [1, 1024], F32), ("scl", [128, 8], F32),
                                ("btfc", [16, FFH], BF16), ("a2fc", [16, D], BF16),
                                ("stat", [128, 64], F32), ("xnfm", [128, 4096], BF16),
                                ("qd", [128, 4096], BF16), ("kd", [128, 4096], BF16),
                                ("vd", [128, 4096], BF16), ("ao", [128, 4096], BF16),
                                ("x1", [128, 4096], F32), ("hd", [128, HC * TILE], BF16),
                                ("wfcT0", [128, 1024], BF16), ("wprT0", [128, 4096], BF16)]:
            d_dbg[name] = nc.dram_tensor("dbg_" + name, shape, dt, kind="ExternalOutput").ap()

    d_wfcT = nc.dram_tensor("wfcT_scr", [FFH, D], BF16, kind="Internal").ap()
    d_wprT = nc.dram_tensor("wprT_scr", [D, FFH], BF16, kind="Internal").ap()
    d_woutb = nc.dram_tensor("wout_scr", [D, D], BF16, kind="Internal").ap()

    cnt = {}
    stack = ExitStack()
    e = stack.enter_context

    sems = {}
    for s in ("dc", "da", "db", "dwa", "dwb", "dx", "dy",
              "wo0", "wo1", "wf0", "wf1", "wp0", "wp1", "T", "V", "S", "BAR"):
        sems[s] = e(nc.semaphore(s))

    pb = [e(nc.psum_tensor(f"pb{i}", [128, 512], F32)) for i in range(8)]

    s_identf = e(nc.sbuf_tensor("sbident_f", [128, 128], F32))
    s_mask = e(nc.sbuf_tensor("sb_maskm", [128, 256], BF16))
    s_mask0 = e(nc.sbuf_tensor("sb_mask0", [128, 128], BF16))
    s_onesf = e(nc.sbuf_tensor("sb_onesf", [1, 128], F32))
    s_gcol = e(nc.sbuf_tensor("gcol", [128, 8], F32))
    s_g8col = e(nc.sbuf_tensor("g8col", [128, 8], F32))
    s_betab = e(nc.sbuf_tensor("betab", [128, 8], BF16))
    s_bw = e(nc.sbuf_tensor("bw", [128, 16], F32))
    s_bwv = e(nc.sbuf_tensor("bwv", [1, 1024], BF16))
    s_onesb = e(nc.sbuf_tensor("sb_onesb", [1, 128], BF16))
    s_bfc = e(nc.sbuf_tensor("bfccol", [128, 32], F32))
    s_bpr = e(nc.sbuf_tensor("bprcol", [128, 8], F32))
    s_cm05 = e(nc.sbuf_tensor("cm05", [128, 1], F32))
    s_cp05 = e(nc.sbuf_tensor("cp05", [128, 1], F32))
    s_scfc = e(nc.sbuf_tensor("scfc", [128, 1], F32))
    s_rsfc = e(nc.sbuf_tensor("rsfc", [128, 1], F32))
    s_scpr = e(nc.sbuf_tensor("scpr", [128, 1], F32))
    s_rspr = e(nc.sbuf_tensor("rspr", [128, 1], F32))
    s_am = e(nc.sbuf_tensor("amcols", [128, 32], F32))
    s_amc = e(nc.sbuf_tensor("amc", [128, 1], F32))
    s_amr = e(nc.sbuf_tensor("amr", [1, 128], F32))
    s_sc1 = e(nc.sbuf_tensor("sc1", [1, 1], F32))

    s_wall = e(nc.sbuf_tensor("wall", [128, DC * 3 * D], BF16))

    s_a2fc = e(nc.sbuf_tensor("a2fc", [16, D], BF16))
    s_btfc = e(nc.sbuf_tensor("btfc", [16, FFH], BF16))
    s_a2pr = e(nc.sbuf_tensor("a2pr", [16, FFH], BF16))
    s_btpr = e(nc.sbuf_tensor("btpr", [16, D], BF16))

    P = Prog(cnt)

    def dma(prog, eng, out_ap, in_ap, sem, slow=False):
        if slow:
            return prog.emit(eng, lambda E, o=out_ap, i=in_ap: E.dma_start(out=o, in_=i, allow_slow_non_contiguous=True), inc=sem, n=16)
        return prog.emit(eng, lambda E, o=out_ap, i=in_ap: E.dma_start(out=o, in_=i), inc=sem, n=16)

    def mm(prog, out_ap, lhsT, rhs, start, stop, tp=None, waits=()):
        def f(E, o=out_ap, l=lhsT, r=rhs):
            return E.matmul(o, l, r, start=start, stop=stop, tile_position=tp,
                            skip_group_check=True)
        return prog.emit("tensor", f, waits=waits, inc="T")

    def trans(prog, out_ap, in_ap, ident, start=True, stop=True, waits=()):
        def f(E, o=out_ap, i=in_ap, idn=ident):
            return E.matmul(o, i, idn, start=start, stop=stop, is_transpose=True,
                            skip_group_check=True)
        return prog.emit("tensor", f, waits=waits, inc="T")

    with ExitStack() as arena:
        a = arena.enter_context
        t_stage = a(nc.sbuf_tensor("stage", [128, 2 * FFH], F32))
        t_q = a(nc.sbuf_tensor("tq", [128, 1024], F32))
        t_k0 = a(nc.sbuf_tensor("tk0", [128, 1024], I32))
        t_k0f = a(nc.sbuf_tensor("tk0f", [128, 1024], F32))
        t_d = a(nc.sbuf_tensor("td", [128, 1024], F32))
        t_s1 = a(nc.sbuf_tensor("ts1", [128, 1024], F32))
        t_s2 = a(nc.sbuf_tensor("ts2", [128, 1024], F32))
        t_wqf = a(nc.sbuf_tensor("twqf", [128, FFH], F32))
        t_panel = a(nc.sbuf_tensor("tpanel", [128, FFH], BF16))
        t_small = a(nc.sbuf_tensor("tsmall", [128, 16], F32))
        t_arow = a(nc.sbuf_tensor("tarow", [16, FFH], F32))

        tok_dx = 0
        tok_dx = dma(P, "sync", s_identf[:], d_identf, "dc")
        tok_dx = dma(P, "sync", s_mask[:], d_mask, "dc")
        tok_dx = dma(P, "sync", s_mask0[:], d_mask0, "dc")
        tok_dx = dma(P, "sync", s_onesf[:], d_onesf, "dc")
        tok_dx = dma(P, "sync", s_onesb[:], d_onesb, "dc")
        tok_dx = dma(P, "sync", s_gcol[:], d_gamma.rearrange("(c p) -> p c", p=128), "dc", slow=True)
        tok_dx = dma(P, "sync", t_small[:, 0:8], d_beta.rearrange("(c p) -> p c", p=128), "dc", slow=True)
        tok_dx = dma(P, "sync", s_bfc[:], d_bfc.rearrange("(c p) -> p c", p=128), "dc", slow=True)
        tok_dx = dma(P, "sync", s_bpr[:], d_bpr.rearrange("(c p) -> p c", p=128), "dc", slow=True)
        tok_consts = tok_dx
        P.emit("vector", lambda E: E.memset(s_cm05[:], -0.5))
        P.emit("vector", lambda E: E.memset(s_cp05[:], 0.5))
        P.emit("vector", lambda E: E.tensor_scalar_mul(s_g8col[:], s_gcol[:], 0.125),
               waits=[("dc", tok_consts)])
        tv = P.emit("vector", lambda E: E.tensor_copy(out=s_betab[:], in_=t_small[:, 0:8]),
                    waits=[("dc", tok_consts)], inc="V")

        cast_toks = []
        for dc in range(DC):
            w = [("S", cast_toks[dc - 2])] if dc >= 2 else []
            sem = "da" if dc % 2 == 0 else "db"
            tok_l = P.emit("sync",
                           lambda E, dc=dc: E.dma_start(out=t_stage[:, (dc % 2) * 3072:(dc % 2) * 3072 + 3072],
                                                        in_=d_wqkv[dc * 128:(dc + 1) * 128, :]),
                           waits=w, inc=sem, n=16)
            tok = P.emit("scalar",
                         lambda E, dc=dc: E.copy(out=s_wall[:, dc * 3072:(dc + 1) * 3072],
                                                 in_=t_stage[:, (dc % 2) * 3072:(dc % 2) * 3072 + 3072]),
                         waits=[(sem, tok_l)], inc="S")
            cast_toks.append(tok)

        P.fence()
        for oc in range(16):
            for dc in range(DC):
                mm(P, pb[0][:, oc:oc + 1],
                   s_wall[:, dc * 3072 + oc * 128: dc * 3072 + (oc + 1) * 128],
                   s_betab[:, dc:dc + 1],
                   start=(dc == 0), stop=(dc == DC - 1),
                   waits=[("S", cast_toks[-1]), ("V", tv)])
        tok_T = P.emit("tensor", lambda E: E.nop(), inc="T")
        for half in range(2):
            for dc in range(DC):
                mm(P, pb[1 + half][0:1, 0:512],
                   s_betab[:, dc:dc + 1],
                   s_wall[:, dc * 3072 + 2048 + half * 512: dc * 3072 + 2048 + (half + 1) * 512],
                   start=(dc == 0), stop=(dc == DC - 1))
        tok_T = P.emit("tensor", lambda E: E.nop(), inc="T")
        tok_bw = P.emit("vector", lambda E: E.tensor_copy(out=s_bw[:], in_=pb[0][:, 0:16]),
                        waits=[("T", tok_T)], inc="V")
        tok_bw = P.emit("vector", lambda E: E.tensor_scalar_mul(s_bw[:, 0:8], s_bw[:, 0:8], 0.125),
                        inc="V")
        for half in range(2):
            tok_bwv = P.emit("vector",
                             lambda E, half=half: E.tensor_copy(out=s_bwv[0:1, half * 512:(half + 1) * 512],
                                                                in_=pb[1 + half][0:1, 0:512]),
                             waits=[("T", tok_T)], inc="V")
        fold_toks = []
        for dc in range(DC):
            P.emit("scalar",
                   lambda E, dc=dc: E.activation(out=s_wall[:, dc * 3072:dc * 3072 + 1024],
                                                 in_=s_wall[:, dc * 3072:dc * 3072 + 1024],
                                                 func=AF.Copy, scale=s_g8col[:, dc:dc + 1]),
                   waits=[("V", tok_bwv)])
            tok = P.emit("scalar",
                         lambda E, dc=dc: E.activation(out=s_wall[:, dc * 3072 + 1024:dc * 3072 + 3072],
                                                       in_=s_wall[:, dc * 3072 + 1024:dc * 3072 + 3072],
                                                       func=AF.Copy, scale=s_gcol[:, dc:dc + 1]),
                         inc="S")
            fold_toks.append(tok)

        P.fence()
        st_toks = [0, 0]
        for ic in range(DC):
            sem = "da" if ic % 2 == 0 else "db"
            ssem = "dwa" if ic % 2 == 0 else "dwb"
            tok_l = dma(P, "sync", t_stage[:, (ic % 2) * 4096:(ic % 2) * 4096 + 1024],
                        d_wout[ic * 128:(ic + 1) * 128, :], sem)
            tok_c = P.emit("vector",
                           lambda E, ic=ic: E.tensor_copy(out=t_panel[:, (ic % 2) * 1024:(ic % 2) * 1024 + 1024],
                                                          in_=t_stage[:, (ic % 2) * 4096:(ic % 2) * 4096 + 1024]),
                           waits=[(sem, tok_l), (ssem, st_toks[ic % 2])], inc="V")
            st_toks[ic % 2] = P.emit(
                "gpsimd",
                lambda E, ic=ic: E.dma_start(out=d_woutb[ic * 128:(ic + 1) * 128, :],
                                             in_=t_panel[:, (ic % 2) * 1024:(ic % 2) * 1024 + 1024]),
                waits=[("V", tok_c)], inc=ssem, n=16)
            P.fence()

        tok_a = dma(P, "sync", t_arow[:, 0:1024], d_afc, "da")
        tva = P.emit("vector", lambda E: E.tensor_scalar(out=s_a2fc[:], in0=t_arow[:16, 0:1024],
                                                         scalar1=2.0, scalar2=None, op0=ALU.mult),
                     waits=[("da", tok_a)], inc="V")
        tok_a = P.emit("sync", lambda E: E.dma_start(out=t_arow[:], in_=d_apr),
                       waits=[("V", tva)], inc="da", n=16)
        tva = P.emit("vector", lambda E: E.tensor_scalar(out=s_a2pr[:], in0=t_arow[:16, :],
                                                         scalar1=2.0, scalar2=None, op0=ALU.mult),
                     waits=[("da", tok_a)], inc="V")
        P.fence()
        for c in range(HC):
            sem = "da" if c % 2 == 0 else "db"
            tok_l = dma(P, "sync", t_stage[:, (c % 2) * 4096:(c % 2) * 4096 + 16],
                        d_bfcB[c * 128:(c + 1) * 128, :], sem)
            tt = trans(P, pb[2][0:16, (c % 2) * 128:(c % 2) * 128 + 128],
                       t_stage[:, (c % 2) * 4096:(c % 2) * 4096 + 16], s_identf[:],
                       waits=[(sem, tok_l)])
            P.emit("scalar",
                   lambda E, c=c: E.copy(out=s_btfc[:16, c * 128:(c + 1) * 128],
                                         in_=pb[2][0:16, (c % 2) * 128:(c % 2) * 128 + 128]),
                   waits=[("T", tt)], inc="S")
            P.fence()
        for c in range(DC):
            sem = "da" if c % 2 == 0 else "db"
            tok_l = dma(P, "sync", t_stage[:, (c % 2) * 4096:(c % 2) * 4096 + 16],
                        d_bprB[c * 128:(c + 1) * 128, :], sem)
            tt = trans(P, pb[2][0:16, (c % 2) * 128:(c % 2) * 128 + 128],
                       t_stage[:, (c % 2) * 4096:(c % 2) * 4096 + 16], s_identf[:],
                       waits=[(sem, tok_l)])
            P.emit("scalar",
                   lambda E, c=c: E.copy(out=s_btpr[:16, c * 128:(c + 1) * 128],
                                         in_=pb[2][0:16, (c % 2) * 128:(c % 2) * 128 + 128]),
                   waits=[("T", tt)], inc="S")
            P.fence()

        def quant_weight(d_w, nchunk, free, s_scale, s_rscale, am_ncols, d_scrT, a2, bt,
                         panel_cols, tpos_cols):
            P.fence()
            for c in range(nchunk):
                sl = (c % 2) * 4096
                sem = "da" if c % 2 == 0 else "db"
                tok_l = dma(P, "sync", t_stage[:, sl:sl + free], d_w[c * 128:(c + 1) * 128, :], sem)
                P.emit("vector",
                       lambda E, c=c, sl=sl: E.reduce_max(out=s_am[:, c:c + 1], in_=t_stage[:, sl:sl + free],
                                                          axis=AX.X, apply_absolute_value=True),
                       waits=[(sem, tok_l)], inc="V")
                P.fence()
            tok = P.emit("vector", lambda E: E.reduce_max(out=s_amc[:], in_=s_am[:, 0:am_ncols], axis=AX.X),
                         inc="V")
            tt = trans(P, pb[3][0:1, 0:128], s_amc[:], s_identf[:], waits=[("V", tok)])
            tok = P.emit("vector", lambda E: E.tensor_copy(out=s_amr[:], in_=pb[3][0:1, 0:128]),
                         waits=[("T", tt)], inc="V")
            tok = P.emit("vector", lambda E: E.reduce_max(out=s_sc1[:], in_=s_amr[:], axis=AX.X), inc="V")
            tok = P.emit("vector", lambda E: E.tensor_scalar_mul(s_sc1[:], s_sc1[:], float(np.float32(1.0) / np.float32(127.0))),
                         inc="V")
            tt = mm(P, pb[3][:, 0:1], s_onesf[:], s_sc1[:], start=True, stop=True,
                    waits=[("V", tok)])
            tok = P.emit("vector", lambda E: E.tensor_copy(out=s_scale[:], in_=pb[3][:, 0:1]),
                         waits=[("T", tt)], inc="V")
            tok = P.emit("vector", lambda E: E.reciprocal(out=s_rscale[:], in_=s_scale[:]), inc="V")
            P.fence()
            nsl = free // 1024
            tok_store = 0
            ev_bank = {}
            trans_last = 0
            for c in range(nchunk):
                sl = (c % 2) * 4096
                sem = "da" if c % 2 == 0 else "db"
                tok_l = dma(P, "sync", t_stage[:, sl:sl + free], d_w[c * 128:(c + 1) * 128, :], sem)
                for s in range(nsl):
                    src = t_stage[:, sl + s * 1024: sl + (s + 1) * 1024]
                    tq = P.emit("scalar",
                                lambda E, src=src: E.activation(out=t_q[:], in_=src, func=AF.Copy,
                                                                scale=s_rscale[:]),
                                waits=[(sem, tok_l)], inc="S")
                    tk = P.emit("vector", lambda E: E.tensor_copy(out=t_k0[:], in_=t_q[:]),
                                waits=[("S", tq)], inc="V")
                    tk = P.emit("vector", lambda E: E.tensor_copy(out=t_k0f[:], in_=t_k0[:]), inc="V")
                    tk = P.emit("vector", lambda E: E.tensor_tensor(out=t_d[:], in0=t_q[:], in1=t_k0f[:],
                                                                    op=ALU.subtract), inc="V")
                    t1 = P.emit("scalar", lambda E: E.activation(out=t_s1[:], in_=t_d[:], func=AF.Sign,
                                                                 bias=s_cm05[:]),
                                waits=[("V", tk)], inc="S")
                    t1 = P.emit("scalar", lambda E: E.activation(out=t_s2[:], in_=t_d[:], func=AF.Sign,
                                                                 bias=s_cp05[:]), inc="S")
                    tk = P.emit("vector", lambda E: E.tensor_tensor(out=t_s1[:], in0=t_s1[:], in1=t_s2[:],
                                                                    op=ALU.add),
                                waits=[("S", t1)], inc="V")
                    tk = P.emit("vector", lambda E: E.scalar_tensor_tensor(out=t_d[:], in0=t_s1[:], scalar=0.5,
                                                                           in1=t_k0f[:], op0=ALU.mult, op1=ALU.add),
                                inc="V")
                    tw = P.emit("scalar",
                                lambda E, s=s: E.activation(out=t_wqf[:, s * 1024:(s + 1) * 1024], in_=t_d[:],
                                                            func=AF.Copy, scale=s_scale[:]),
                                waits=[("V", tk), ("T", trans_last)], inc="S")
                nblk = free // 128
                ev_tok = None
                for b in range(nblk):
                    bi = 4 + ((b // 4) % 2)
                    bank = pb[bi]
                    tt = trans(P, bank[:, (b % 4) * 128:(b % 4) * 128 + 128],
                               t_wqf[:, b * 128:(b + 1) * 128], s_identf[:],
                               start=True, stop=False,
                               waits=[("S", tw), ("S", ev_bank.get(bi, 0)),
                                      ("dwa", tok_store)])
                    tt = mm(P, bank[:, (b % 4) * 128:(b % 4) * 128 + 128],
                            a2[:16, b * 128:(b + 1) * 128],
                            bt[:16, c * 128:(c + 1) * 128],
                            start=False, stop=True)
                    if b % 4 == 3:
                        ev_tok = P.emit("scalar",
                                        lambda E, b=b, bank=bank: E.copy(out=t_panel[:, (b - 3) * 128:(b + 1) * 128],
                                                                         in_=bank[:]),
                                        waits=[("T", tt)], inc="S")
                        ev_bank[bi] = ev_tok
                trans_last = tt
                tok_store = P.emit("gpsimd",
                                   lambda E, c=c: E.dma_start(out=d_scrT[c * 128:(c + 1) * 128, :],
                                                              in_=t_panel[:, 0:free]),
                                   waits=[("S", ev_tok)], inc="dwa", n=16)
                P.fence()

        quant_weight(d_wfc, HC, 1024, s_scfc, s_rsfc, HC, d_wfcT, s_a2fc, s_btfc, 1024, None)
        quant_weight(d_wpr, DC, 4096, s_scpr, s_rspr, DC, d_wprT, s_a2pr, s_btpr, 4096, None)

        if dbg:
            P.fence()
            P.emit("sync", lambda E: E.nop(),
                   waits=[(k, cnt.get(k, 0)) for k in ("T", "V", "S")])
            dma(P, "sync", d_dbg["wall"], s_wall[:], "dc")
            dma(P, "sync", d_dbg["bw"], s_bw[:], "dc")
            dma(P, "sync", d_dbg["bwv"], s_bwv[:], "dc")
            P.emit("vector", lambda E: E.tensor_copy(out=s_am[:, 0:1], in_=s_scfc[:]))
            P.emit("vector", lambda E: E.tensor_copy(out=s_am[:, 1:2], in_=s_rsfc[:]))
            P.emit("vector", lambda E: E.tensor_copy(out=s_am[:, 2:3], in_=s_scpr[:]))
            tvd = P.emit("vector", lambda E: E.tensor_copy(out=s_am[:, 3:4], in_=s_rspr[:]), inc="V")
            dma(P, "sync", d_dbg["scl"], s_am[:, 0:8], "dc")
            dma(P, "sync", d_dbg["btfc"], s_btfc[:], "dc")
            dma(P, "sync", d_dbg["a2fc"], s_a2fc[:], "dc")
            dma(P, "sync", d_dbg["wfcT0"], d_wfcT[0:128, :], "dc")
            dma(P, "sync", d_dbg["wprT0"], d_wprT[0:128, :], "dc")
        for eng in ENGS:
            P.emit(eng, lambda E: E.nop(), inc="BAR")

        with nc.Block("setup") as b1:
            P.replay(b1, sems)

    s_x = e(nc.sbuf_tensor("xsb", [128, TILE // 128 * D], F32))
    s_xnb = e(nc.sbuf_tensor("xnb", [128, D], F32))
    s_xnfm = e(nc.sbuf_tensor("xnfm", [128, DC * TILE], BF16))
    s_q = e(nc.sbuf_tensor("qsb", [128, 8 * TILE], BF16))
    s_k = e(nc.sbuf_tensor("ksb", [128, 8 * TILE], BF16))
    s_v = e(nc.sbuf_tensor("vsb", [128, G * D], BF16))
    s_kprev = e(nc.sbuf_tensor("kprev", [128, 8 * 128], BF16))
    s_vprev = e(nc.sbuf_tensor("vprev", [128, D], BF16))
    s_ao = e(nc.sbuf_tensor("aosb", [128, 8 * TILE], BF16))
    s_h = e(nc.sbuf_tensor("hsb", [128, HC * TILE], BF16))
    s_sm = e(nc.sbuf_tensor("smbuf", [128, 4 * 256], F32))
    s_probs = e(nc.sbuf_tensor("probs", [128, 4 * 256], F32))
    s_pT = e(nc.sbuf_tensor("pT", [128, 4 * 256], BF16))
    s_stat = e(nc.sbuf_tensor("stat", [128, 64], F32))
    s_ofm = e(nc.sbuf_tensor("ofm", [128, 512], F32))
    s_wfcb = e(nc.sbuf_tensor("wfcb", [128, 2 * 1024], BF16))
    s_wprb = e(nc.sbuf_tensor("wprb", [128, 2 * 4096], BF16))
    s_woutb = e(nc.sbuf_tensor("woutb", [128, 2 * 1024], BF16))

    NEGMAX, ROWSUM, RECIP = 0, 4, 52
    SUM, SSQ, MU, MSQ, VARP, R1, V2P, ASC, NMA, T0 = 8, 12, 16, 20, 24, 28, 32, 36, 40, 44

    M = Prog(cnt)
    M.fence_snap = {}
    for eng in ENGS:
        M.emit(eng, lambda E: E.nop(), waits=[("BAR", cnt["BAR"])])

    tok_store = 0
    wout_cons = [0, 0]
    wfc_cons = [0, 0]
    wpr_cons = [0, 0]
    ofm_cons = [0, 0]
    ofm_trans_last = [0]
    xnb_cons = [0, 0]
    xnt_prev_ev = [0, 0]

    def emit_norm_transpose(M, a_col, nma_col):
        ev_last = None
        for g in range(G):
            sl = 0
            tok_x = M.emit("vector",
                           lambda E, g=g, sl=sl, a_col=a_col, nma_col=nma_col:
                           E.tensor_scalar(out=s_xnb[:, sl * D:(sl + 1) * D],
                                           in0=s_x[:, g * D:(g + 1) * D],
                                           scalar1=s_stat[:, a_col + g:a_col + g + 1],
                                           scalar2=s_stat[:, nma_col + g:nma_col + g + 1],
                                           op0=ALU.mult, op1=ALU.add),
                           waits=[("T", xnb_cons[sl])], inc="V")
            tt = None
            for half in range(2):
                bank = pb[half]
                for i in range(4):
                    dc = half * 4 + i
                    tt = trans(M, bank[:, i * 128:(i + 1) * 128],
                               s_xnb[:, sl * D + dc * 128: sl * D + (dc + 1) * 128],
                               s_identf[:],
                               waits=[("V", tok_x), ("S", xnt_prev_ev[half])])
                dst = bass.AP(s_xnfm, half * 4 * TILE + g * 128,
                              [[DC * TILE, 128], [TILE, 4], [1, 128]])
                tok = M.emit("scalar",
                             lambda E, bank=bank, dst=dst:
                             E.copy(out=dst, in_=bank[:].rearrange("p (a b) -> p a b", b=128)),
                             waits=[("T", tt)], inc="S")
                xnt_prev_ev[half] = tok
                ev_last = tok
            xnb_cons[sl] = tt
        return ev_last

    for t in range(NT):
        seq_start = (t % SEQ_TILES == 0)
        tok_x = dma(M, "sync", s_x[:].rearrange("p (g d) -> p g d", d=D),
                    d_x[t * TILE:(t + 1) * TILE, :].rearrange("(g p) d -> p g d", p=128),
                    "dx")

        def stats_and_norm(wait_tok, wait_sem, r_out_cols):
            tv = ts = None
            for g in range(G):
                xg = s_x[:, g * D:(g + 1) * D]
                tv = M.emit("vector",
                            lambda E, xg=xg, g=g: E.tensor_scalar(out=s_xnfm[:, 0:1024], in0=xg, scalar1=1.0,
                                                                  scalar2=None, op0=ALU.mult, op1=ALU.add,
                                                                  accum_out=s_stat[:, SUM + g:SUM + g + 1]),
                            waits=[(wait_sem, wait_tok)], inc="V")
                ts = M.emit("scalar",
                            lambda E, xg=xg, g=g: E.activation(out=s_xnfm[:, 1024:2048], in_=xg, func=AF.Square,
                                                               accum_out=s_stat[:, SSQ + g:SSQ + g + 1]),
                            waits=[(wait_sem, wait_tok)], inc="S")
            c = 1.0 / 1024.0
            tv = M.emit("vector", lambda E: E.tensor_scalar_mul(s_stat[:, MU:MU + 4], s_stat[:, SUM:SUM + 4], c),
                        waits=[("V", tv)], inc="V")
            tv = M.emit("vector", lambda E: E.tensor_scalar_mul(s_stat[:, MSQ:MSQ + 4], s_stat[:, SSQ:SSQ + 4], c),
                        waits=[("S", ts)], inc="V")
            tv = M.emit("vector", lambda E: E.tensor_tensor(out=s_stat[:, T0:T0 + 4], in0=s_stat[:, MU:MU + 4],
                                                            in1=s_stat[:, MU:MU + 4], op=ALU.mult), inc="V")
            tv = M.emit("vector", lambda E: E.tensor_tensor(out=s_stat[:, VARP:VARP + 4], in0=s_stat[:, MSQ:MSQ + 4],
                                                            in1=s_stat[:, T0:T0 + 4], op=ALU.subtract), inc="V")
            tv = M.emit("vector", lambda E: E.tensor_scalar_add(s_stat[:, VARP:VARP + 4], s_stat[:, VARP:VARP + 4],
                                                                1e-5), inc="V")
            ts = M.emit("scalar", lambda E: E.activation(out=s_stat[:, T0:T0 + 4], in_=s_stat[:, VARP:VARP + 4],
                                                         func=AF.Ln),
                        waits=[("V", tv)], inc="S")
            ts = M.emit("scalar", lambda E: E.activation(out=s_stat[:, R1:R1 + 4], in_=s_stat[:, T0:T0 + 4],
                                                         func=AF.Exp, scale=s_cm05[:]), inc="S")
            return tv, ts

        tv, ts = stats_and_norm(tok_x, "dx", R1)
        T1 = 48
        tv = M.emit("vector", lambda E: E.reciprocal(out=s_stat[:, T1:T1 + 4], in_=s_stat[:, VARP:VARP + 4]),
                    inc="V")
        tv = M.emit("vector", lambda E: E.tensor_scalar_add(s_stat[:, V2P:V2P + 4], s_stat[:, VARP:VARP + 4], -1e-5),
                    inc="V")
        tv = M.emit("vector", lambda E: E.tensor_tensor(out=s_stat[:, V2P:V2P + 4], in0=s_stat[:, V2P:V2P + 4],
                                                        in1=s_stat[:, T1:T1 + 4], op=ALU.mult), inc="V")
        tv = M.emit("vector", lambda E: E.tensor_scalar_add(s_stat[:, V2P:V2P + 4], s_stat[:, V2P:V2P + 4], 1e-5),
                    inc="V")
        ts = M.emit("scalar", lambda E: E.activation(out=s_stat[:, T1:T1 + 4], in_=s_stat[:, V2P:V2P + 4],
                                                     func=AF.Ln),
                    waits=[("V", tv), ("S", ts)], inc="S")
        ts = M.emit("scalar", lambda E: E.activation(out=s_stat[:, V2P:V2P + 4], in_=s_stat[:, T1:T1 + 4],
                                                     func=AF.Exp, scale=s_cm05[:]), inc="S")
        tv = M.emit("vector", lambda E: E.tensor_tensor(out=s_stat[:, ASC:ASC + 4], in0=s_stat[:, R1:R1 + 4],
                                                        in1=s_stat[:, V2P:V2P + 4], op=ALU.mult),
                    waits=[("S", ts)], inc="V")
        tv = M.emit("vector", lambda E: E.scalar_tensor_tensor(out=s_stat[:, NMA:NMA + 4], in0=s_stat[:, MU:MU + 4],
                                                               scalar=-1.0, in1=s_stat[:, ASC:ASC + 4],
                                                               op0=ALU.mult, op1=ALU.mult), inc="V")
        tok_xnfm = emit_norm_transpose(M, ASC, NMA)

        if dbg and t == 0:
            M.fence()
            M.emit("sync", lambda E: E.nop(), waits=[(k, cnt.get(k, 0)) for k in ("T", "V", "S")])
            dma(M, "sync", d_dbg["stat"], s_stat[:], "dc")
            dma(M, "sync", d_dbg["xnfm"], s_xnfm[:], "dc")
            for _e in ("vector", "scalar", "tensor", "gpsimd"):
                M.emit(_e, lambda E: E.nop(), waits=[("dc", cnt["dc"])])
        M.fence()
        qk_ev = []
        for oc in range(16):
            bank = pb[2 + (oc % 2)]
            tt = None
            for dc in range(DC):
                tt = mm(M, bank[:],
                        s_wall[:, dc * 3072 + oc * 128: dc * 3072 + (oc + 1) * 128],
                        s_xnfm[:, dc * TILE:(dc + 1) * TILE],
                        start=(dc == 0), stop=(dc == DC - 1),
                        waits=[("S", qk_ev[oc - 2] if oc >= 2 else 0)])
            dst = s_q if oc < 8 else s_k
            fc = oc % 8
            tok = M.emit("scalar",
                         lambda E, dst=dst, fc=fc, oc=oc, bank=bank:
                         E.activation(out=dst[:, fc * TILE:(fc + 1) * TILE], in_=bank[:],
                                      func=AF.Identity, bias=s_bw[:, oc:oc + 1]),
                         waits=[("T", tt)], inc="S")
            qk_ev.append(tok)
        v_ev = []
        for g in range(G):
            bh = [pb[4 + (g % 2) * 2], pb[5 + (g % 2) * 2]]
            tt = None
            for half in range(2):
                for dc in range(DC):
                    tt = mm(M, bh[half][:],
                            s_xnfm[:, dc * TILE + g * 128: dc * TILE + (g + 1) * 128],
                            s_wall[:, dc * 3072 + 2048 + half * 512: dc * 3072 + 2048 + (half + 1) * 512],
                            start=(dc == 0), stop=False,
                            waits=[("V", v_ev[g - 2] if g >= 2 else 0)])
                tt = mm(M, bh[half][:], s_onesb[:], s_bwv[0:1, half * 512:(half + 1) * 512],
                        start=False, stop=True)
            tok = None
            for half in range(2):
                tok = M.emit("vector",
                             lambda E, g=g, half=half, b=bh[half]:
                             E.tensor_copy(out=s_v[:, g * D + half * 512: g * D + (half + 1) * 512], in_=b[:]),
                             waits=[("T", tt)], inc="V")
            v_ev.append(tok)

        if dbg and t == 0:
            M.fence()
            M.emit("sync", lambda E: E.nop(), waits=[(k, cnt.get(k, 0)) for k in ("T", "V", "S")])
            dma(M, "sync", d_dbg["qd"], s_q[:], "dc")
            dma(M, "sync", d_dbg["kd"], s_k[:], "dc")
            dma(M, "sync", d_dbg["vd"], s_v[:], "dc")
            for _e in ("vector", "scalar", "tensor", "gpsimd"):
                M.emit(_e, lambda E: E.nop(), waits=[("dc", cnt["dc"])])
        M.fence()
        NI = G * H
        ctx = []
        for w in range(G):
            fw = (seq_start and w == 0)
            for h in range(H):
                ctx.append((w, h, fw))
        tokA = [0] * NI; tokB = [0] * NI; tokC = [0] * NI; tokD = [0] * NI
        tokE = [0] * NI; tokF = [0] * NI; tokG = [0] * NI; tokH = [0] * NI
        engH = [0] * NI

        def stA(i):
            w, h, fw = ctx[i]
            po = 64 * (h % 2); fc = h // 2
            qs = s_q[po:po + 64, fc * TILE + w * 128: fc * TILE + (w + 1) * 128]
            bankS = pb[i % 4]
            wts = [("S", qk_ev[8 + fc]), ("V", tokB[i - 4] if i >= 4 else 0)]
            if fw:
                tt = mm(M, bankS[:, 0:128], qs, s_k[po:po + 64, fc * TILE: fc * TILE + 128],
                        start=True, stop=True, tp=(po, 0), waits=wts)
            elif w == 0:
                tt = mm(M, bankS[:, 0:128], qs, s_kprev[po:po + 64, fc * 128:(fc + 1) * 128],
                        start=True, stop=True, tp=(po, 0), waits=wts)
                tt = mm(M, bankS[:, 128:256], qs, s_k[po:po + 64, fc * TILE: fc * TILE + 128],
                        start=True, stop=True, tp=(po, 0))
            else:
                tt = mm(M, bankS[:, 0:256], qs,
                        s_k[po:po + 64, fc * TILE + (w - 1) * 128: fc * TILE + (w + 1) * 128],
                        start=True, stop=True, tp=(po, 0), waits=wts)
            tokA[i] = tt

        def stB(i):
            w, h, fw = ctx[i]
            nk = 128 if fw else 256
            wh = i % 4
            msk = s_mask0 if fw else s_mask
            bankS = pb[wh]
            M.emit("vector",
                   lambda E, wh=wh, nk=nk, msk=msk, bankS=bankS:
                   E.tensor_tensor(out=s_sm[:, wh * 256: wh * 256 + nk], in0=bankS[:, 0:nk],
                                   in1=msk[:, 0:nk], op=ALU.add),
                   waits=[("T", tokA[i]), ("S", tokC[i - 4] if i >= 4 else 0)], inc="V")
            tokB[i] = M.emit("vector",
                             lambda E, wh=wh, nk=nk: E.reduce_max(out=s_stat[:, NEGMAX + wh:NEGMAX + wh + 1],
                                                                  in_=s_sm[:, wh * 256: wh * 256 + nk],
                                                                  axis=AX.X, negate=True),
                             inc="V")

        def stC(i):
            w, h, fw = ctx[i]
            nk = 128 if fw else 256
            wh = i % 4
            tokC[i] = M.emit("scalar",
                             lambda E, wh=wh, nk=nk: E.activation(out=s_probs[:, wh * 256: wh * 256 + nk],
                                                                  in_=s_sm[:, wh * 256: wh * 256 + nk],
                                                                  func=AF.Exp,
                                                                  bias=s_stat[:, NEGMAX + wh:NEGMAX + wh + 1],
                                                                  accum_out=s_stat[:, ROWSUM + wh:ROWSUM + wh + 1]),
                             waits=[("V", tokB[i]), ("T", tokE[i - 4] if i >= 4 else 0)], inc="S")

        def stD(i):
            w, h, fw = ctx[i]
            nk = 128 if fw else 256
            wh = i % 4
            M.emit("vector",
                   lambda E, wh=wh: E.reciprocal(out=s_stat[:, RECIP + wh:RECIP + wh + 1],
                                                 in_=s_stat[:, ROWSUM + wh:ROWSUM + wh + 1]),
                   waits=[("S", tokC[i])], inc="V")
            tokD[i] = M.emit("vector",
                             lambda E, wh=wh, nk=nk: E.tensor_scalar_mul(s_probs[:, wh * 256: wh * 256 + nk],
                                                                         s_probs[:, wh * 256: wh * 256 + nk],
                                                                         s_stat[:, RECIP + wh:RECIP + wh + 1]),
                             inc="V")

        def stE(i):
            w, h, fw = ctx[i]
            nk = 128 if fw else 256
            wh = i % 4
            bankT = pb[4 + i % 2]
            tt2 = None
            for kc in range(nk // 128):
                tt2 = trans(M, bankT[:, kc * 128:(kc + 1) * 128],
                            s_probs[:, wh * 256 + kc * 128: wh * 256 + (kc + 1) * 128],
                            s_identf[:],
                            waits=[("V", tokD[i]), ("S", tokF[i - 2] if i >= 2 else 0)])
            tokE[i] = tt2

        def stF(i):
            w, h, fw = ctx[i]
            nk = 128 if fw else 256
            wh = i % 4
            bankT = pb[4 + i % 2]
            tokF[i] = M.emit("scalar",
                             lambda E, wh=wh, nk=nk, bankT=bankT:
                             E.copy(out=s_pT[:, wh * 256: wh * 256 + nk], in_=bankT[:, 0:nk]),
                             waits=[("T", tokE[i])], inc="S")

        def stG(i):
            w, h, fw = ctx[i]
            wh = i % 4
            po = 64 * (h % 2)
            bankO = pb[6 + i % 2]
            pvw = [("S", tokF[i]), ("V", v_ev[w])]
            if i >= 2:
                pvw.append((("V", "S")[engH[i - 2]], tokH[i - 2]))
            if fw:
                tt3 = mm(M, bankO[po:po + 64, 0:128],
                         s_v[:, w * D + h * DH: w * D + (h + 1) * DH],
                         s_pT[:, wh * 256: wh * 256 + 128],
                         start=True, stop=True, tp=(0, po), waits=pvw)
            else:
                vprev_ap = (s_vprev[:, h * DH:(h + 1) * DH] if w == 0
                            else s_v[:, (w - 1) * D + h * DH: (w - 1) * D + h * DH + DH])
                tt3 = mm(M, bankO[po:po + 64, 0:128],
                         vprev_ap, s_pT[:, wh * 256: wh * 256 + 128],
                         start=True, stop=False, tp=(0, po), waits=pvw)
                tt3 = mm(M, bankO[po:po + 64, 0:128],
                         s_v[:, w * D + h * DH: w * D + (h + 1) * DH],
                         s_pT[:, wh * 256 + 128: wh * 256 + 256],
                         start=False, stop=True, tp=(0, po))
            tokG[i] = tt3

        def stH(i):
            w, h, fw = ctx[i]
            po = 64 * (h % 2); fc = h // 2
            bankO = pb[6 + i % 2]
            if i % 2 == 0:
                tokH[i] = M.emit("vector",
                                 lambda E, po=po, fc=fc, w=w, bankO=bankO:
                                 E.tensor_copy(out=s_ao[po:po + 64, fc * TILE + w * 128: fc * TILE + (w + 1) * 128],
                                               in_=bankO[po:po + 64, 0:128]),
                                 waits=[("T", tokG[i])], inc="V")
                engH[i] = 0
            else:
                tokH[i] = M.emit("scalar",
                                 lambda E, po=po, fc=fc, w=w, bankO=bankO:
                                 E.copy(out=s_ao[po:po + 64, fc * TILE + w * 128: fc * TILE + (w + 1) * 128],
                                        in_=bankO[po:po + 64, 0:128]),
                                 waits=[("T", tokG[i])], inc="S")
                engH[i] = 1

        stages = (stA, stB, stC, stD, stE, stF, stG, stH)
        for j in range(NI + len(stages) - 1):
            for k, st in enumerate(stages):
                i = j - k
                if 0 <= i < NI:
                    st(i)
        last_i = NI - 1
        pv_ev_last = M.emit("vector", lambda E: E.memset(s_stat[:, 60:61], 0.0),
                            waits=[(("V", "S")[engH[last_i]], tokH[last_i]),
                                   (("V", "S")[engH[last_i - 1]], tokH[last_i - 1])], inc="V")

        if (t + 1) % SEQ_TILES != 0:
            kp = M.emit("vector",
                        lambda E: E.tensor_copy(
                            out=s_kprev[:].rearrange("p (f c) -> p f c", c=128),
                            in_=s_k[:].rearrange("p (f t) -> p f t", t=TILE)[:, :, TILE - 128:TILE]),
                        waits=[("S", qk_ev[-1])], inc="V")
            vp = M.emit("vector",
                        lambda E: E.tensor_copy(out=s_vprev[:], in_=s_v[:, (G - 1) * D: G * D]),
                        waits=[("V", v_ev[-1])], inc="V")

        if dbg and t == 0:
            M.fence()
            M.emit("sync", lambda E: E.nop(), waits=[(k, cnt.get(k, 0)) for k in ("T", "V", "S")])
            dma(M, "sync", d_dbg["ao"], s_ao[:], "dc")
            for _e in ("vector", "scalar", "tensor", "gpsimd"):
                M.emit(_e, lambda E: E.nop(), waits=[("dc", cnt["dc"])])
        M.fence()
        res_toks = []
        for ic in range(DC):
            sl = ic % 2
            tok_w = M.emit("gpsimd",
                           lambda E, ic=ic, sl=sl: E.dma_start(out=s_woutb[:, sl * 1024:(sl + 1) * 1024],
                                                               in_=d_woutb[ic * 128:(ic + 1) * 128, :]),
                           waits=[("T", wout_cons[sl])], inc="wo%d" % sl, n=16)
            tt = None
            for g in range(G):
                for half in range(2):
                    tt = mm(M, pb[2 * g + half][:],
                            s_ao[:, ic * TILE + g * 128: ic * TILE + (g + 1) * 128],
                            s_woutb[:, sl * 1024 + half * 512: sl * 1024 + (half + 1) * 512],
                            start=(ic == 0), stop=(ic == DC - 1),
                            waits=[("wo%d" % sl, tok_w), ("V", pv_ev_last)])
            wout_cons[sl] = tt
        tok_T = M.emit("tensor", lambda E: E.nop(), inc="T")
        tvlast = None
        for g in range(G):
            for half in range(2):
                tvlast = M.emit("vector",
                                lambda E, g=g, half=half:
                                E.scalar_tensor_tensor(out=s_x[:, g * D + half * 512: g * D + (half + 1) * 512],
                                                       in0=pb[2 * g + half][:], scalar=1.0,
                                                       in1=s_x[:, g * D + half * 512: g * D + (half + 1) * 512],
                                                       op0=ALU.mult, op1=ALU.add),
                                waits=[("T", tok_T)], inc="V")

        if dbg and t == 0:
            M.fence()
            M.emit("sync", lambda E: E.nop(), waits=[(k, cnt.get(k, 0)) for k in ("T", "V", "S")])
            dma(M, "sync", d_dbg["x1"], s_x[:], "dc")
            for _e in ("vector", "scalar", "tensor", "gpsimd"):
                M.emit(_e, lambda E: E.nop(), waits=[("dc", cnt["dc"])])
        M.fence()
        tv, ts = stats_and_norm(tvlast, "V", R1)
        xnb_toks = []
        for g in range(G):
            tok = M.emit("vector",
                         lambda E, g=g: E.scalar_tensor_tensor(out=s_stat[:, NMA + g:NMA + g + 1],
                                                               in0=s_stat[:, MU + g:MU + g + 1], scalar=-1.0,
                                                               in1=s_stat[:, R1 + g:R1 + g + 1],
                                                               op0=ALU.mult, op1=ALU.mult),
                         waits=[("S", ts)], inc="V")
        tok_xnfm2 = emit_norm_transpose(M, R1, NMA)
        ev_tr = [tok_xnfm2]

        M.fence()
        h_toks = []
        for c in range(HC):
            sl = c % 2
            tok_w = M.emit("gpsimd",
                           lambda E, c=c, sl=sl: E.dma_start(out=s_wfcb[:, sl * 1024:(sl + 1) * 1024],
                                                             in_=d_wfcT[c * 128:(c + 1) * 128, :]),
                           waits=[("T", wfc_cons[sl])], inc="wf%d" % sl, n=16)
            bank = pb[2 + sl]
            tt = None
            for dc in range(DC):
                tt = mm(M, bank[:],
                        s_wfcb[:, sl * 1024 + dc * 128: sl * 1024 + (dc + 1) * 128],
                        s_xnfm[:, dc * TILE:(dc + 1) * TILE],
                        start=(dc == 0), stop=(dc == DC - 1),
                        waits=[("wf%d" % sl, tok_w), ("S", ev_tr[-1]),
                               ("S", h_toks[c - 2] if c >= 2 else 0)])
            wfc_cons[sl] = tt
            tok = M.emit("scalar",
                         lambda E, c=c, bank=bank: E.activation(out=s_h[:, c * TILE:(c + 1) * TILE],
                                                                in_=bank[:], func=AF.Gelu,
                                                                bias=s_bfc[:, c:c + 1]),
                         waits=[("T", tt)], inc="S")
            h_toks.append(tok)

        if dbg and t == 0:
            M.fence()
            M.emit("sync", lambda E: E.nop(), waits=[(k, cnt.get(k, 0)) for k in ("T", "V", "S")])
            dma(M, "sync", d_dbg["hd"], s_h[:], "dc")
            for _e in ("vector", "scalar", "tensor", "gpsimd"):
                M.emit(_e, lambda E: E.nop(), waits=[("dc", cnt["dc"])])
        M.fence()
        last_res = None
        for do in range(DC):
            sl = do % 2
            tok_w = M.emit("gpsimd",
                           lambda E, do=do, sl=sl: E.dma_start(out=s_wprb[:, sl * 4096:(sl + 1) * 4096],
                                                               in_=d_wprT[do * 128:(do + 1) * 128, :]),
                           waits=[("T", wpr_cons[sl])], inc="wp%d" % sl, n=16)
            bank = pb[0 + sl]
            tt = None
            for c in range(HC):
                tt = mm(M, bank[:],
                        s_wprb[:, sl * 4096 + c * 128: sl * 4096 + (c + 1) * 128],
                        s_h[:, c * TILE:(c + 1) * TILE],
                        start=(c == 0), stop=(c == HC - 1),
                        waits=[("wp%d" % sl, tok_w), ("S", h_toks[-1]),
                               ("V", ofm_cons[sl])])
            wpr_cons[sl] = tt
            tok_ev = M.emit("scalar",
                            lambda E, bank=bank, do=do:
                            E.activation(out=s_ofm[:, 0:512], in_=bank[:],
                                         func=AF.Identity, bias=s_bpr[:, do:do + 1]),
                            waits=[("T", tt), ("T", ofm_trans_last[0])], inc="S")
            bank2 = pb[4 + sl]
            tt2 = None
            for g in range(G):
                tt2 = trans(M, bank2[:, g * 128:(g + 1) * 128],
                            s_ofm[:, g * 128:(g + 1) * 128], s_identf[:],
                            waits=[("S", tok_ev), ("V", ofm_cons[sl])])
            ofm_trans_last[0] = tt2
            tok_res = M.emit("vector",
                             lambda E, do=do, bank2=bank2:
                             E.scalar_tensor_tensor(
                                 out=s_x[:].rearrange("p (g d) -> p g d", d=D)[:, :, do * 128:(do + 1) * 128],
                                 in0=bank2[:].rearrange("p (g c) -> p g c", c=128), scalar=1.0,
                                 in1=s_x[:].rearrange("p (g d) -> p g d", d=D)[:, :, do * 128:(do + 1) * 128],
                                 op0=ALU.mult, op1=ALU.add),
                             waits=[("T", tt2)], inc="V")
            ofm_cons[sl] = tok_res
            last_res = tok_res

        tok_store = M.emit("sync",
                           lambda E, t=t: E.dma_start(
                               out=d_y[t * TILE:(t + 1) * TILE, :].rearrange("(g p) d -> p g d", p=128),
                               in_=s_x[:].rearrange("p (g d) -> p g d", d=D)),
                           waits=[("V", last_res)], inc="dy", n=16)
        M.fence()
        M.emit("sync", lambda E: E.nop(), waits=[("dy", tok_store)])

    M.emit("sync", lambda E: E.nop(), waits=[("dy", cnt["dy"])])

    with nc.Block("main") as b2:
        M.replay(b2, sems)

    stack.close()
    return nc


_NC_CACHE = None


def kernel(L0_attn_x, L0_attn_ln_gamma, L0_attn_ln_beta, L0_attn_w_qkv, L0_attn_w_out,
           L0_ffn_W_fc, L0_ffn_b_fc, L0_ffn_A_fc, L0_ffn_B_fc,
           L0_ffn_W_proj, L0_ffn_b_proj, L0_ffn_A_proj, L0_ffn_B_proj):
    global _NC_CACHE
    x = np.ascontiguousarray(np.asarray(L0_attn_x, dtype=np.float32))
    B, N, Dm = x.shape
    ncores = 8
    per = B // ncores

    if _NC_CACHE is None:
        _NC_CACHE = build_program(dbg=DBG)
    nc = _NC_CACHE

    ident_f = np.eye(128, dtype=np.float32)
    ii = np.arange(128)[:, None]
    jj = np.arange(256)[None, :]
    prev = jj < 128
    jjm = np.where(prev, jj, jj - 128)
    bad = np.where(prev, jjm < ii, jjm > ii)
    mask_m = np.where(bad, np.float32(NEG), np.float32(0.0)).astype(BFNP)
    mask_0 = np.where(np.arange(128)[None, :] > ii, np.float32(NEG), np.float32(0.0)).astype(BFNP)
    ones_f = np.ones((1, 128), dtype=np.float32)

    common = dict(
        gamma=np.asarray(L0_attn_ln_gamma, np.float32),
        beta=np.asarray(L0_attn_ln_beta, np.float32),
        wqkv=np.asarray(L0_attn_w_qkv, np.float32),
        wout=np.asarray(L0_attn_w_out, np.float32),
        wfc=np.asarray(L0_ffn_W_fc, np.float32),
        bfc=np.asarray(L0_ffn_b_fc, np.float32),
        afc=np.asarray(L0_ffn_A_fc, np.float32),
        bfcB=np.asarray(L0_ffn_B_fc, np.float32),
        wpr=np.asarray(L0_ffn_W_proj, np.float32),
        bpr=np.asarray(L0_ffn_b_proj, np.float32),
        apr=np.asarray(L0_ffn_A_proj, np.float32),
        bprB=np.asarray(L0_ffn_B_proj, np.float32),
        identf=ident_f, maskm=mask_m, mask0=mask_0, onesf=ones_f,
        onesb=np.ones((1, 128), dtype=BFNP),
    )
    in_maps = []
    for c in range(ncores):
        m = dict(common)
        m["x"] = np.ascontiguousarray(x[c * per:(c + 1) * per].reshape(TOK, Dm))
        in_maps.append(m)

    res = run_bass_kernel_spmd(nc, in_maps, core_ids=list(range(ncores)))
    out = np.empty((B, N, Dm), np.float32)
    for c in range(ncores):
        out[c * per:(c + 1) * per] = res.results[c]["y"].reshape(per, N, Dm)
    if DBG:
        global DBG_RES
        DBG_RES = res.results
    return out


# version 10
# speedup vs baseline: 1.3916x; 1.0677x over previous
import numpy as np
import ml_dtypes
from contextlib import ExitStack

import concourse.bass as bass
import concourse.mybir as mybir
from concourse.bass_utils import run_bass_kernel_spmd

F32 = mybir.dt.float32
BF16 = mybir.dt.bfloat16
I32 = mybir.dt.int32
AF = mybir.ActivationFunctionType
ALU = mybir.AluOpType
AX = mybir.AxisListType
BFNP = ml_dtypes.bfloat16

D = 1024
TOK = 4096
TILE = 512
NT = TOK // TILE
G = TILE // 128
DC = D // 128
H = 16
DH = 64
FFH = 4096
HC = FFH // 128
NEG = -1.0e30
SEQ_TILES = 2048 // TILE

ENGS = ("sync", "tensor", "vector", "scalar", "gpsimd")


class Prog:

    def __init__(self, cnt):
        self.ops = {e: [] for e in ENGS}
        self.cnt = cnt
        self.last_wait = {e: {} for e in ENGS}
        self.fence_snap = {}

    def emit(self, eng, fn, waits=(), inc=None, n=1):
        own = _OWN.get(eng)
        merged = dict(self.fence_snap)
        if own in ("V", "S"):
            if inc is None:
                inc = own
            merged[own] = self.cnt.get(own, 0)
        for s, v in waits:
            if v > merged.get(s, 0):
                merged[s] = v
        w = []
        for s, v in merged.items():
            if s == "T" and eng == "tensor":
                continue
            if v > self.last_wait[eng].get(s, 0):
                self.last_wait[eng][s] = v
                w.append((s, v))
        self.ops[eng].append((fn, w, inc, n))
        if inc is not None:
            self.cnt[inc] = self.cnt.get(inc, 0) + n
            return self.cnt[inc]
        return None

    def fence(self):
        self.fence_snap = {s: self.cnt.get(s, 0) for s in ("T", "V", "S")}

    def replay(self, block, sems):
        for ename in ENGS:
            ops = self.ops[ename]

            def body(eng, ops=ops):
                for fn, w, inc, n in ops:
                    for s, v in w:
                        eng.wait_ge(sems[s], v)
                    instr = fn(eng)
                    if inc is not None:
                        instr.then_inc(sems[inc], n)
            getattr(block, ename)(body)


_OWN = {"tensor": "T", "vector": "V", "scalar": "S"}


DBG = False


def build_program(dbg=False):
    nc = bass.Bass("TRN2", target_bir_lowering=False, detect_race_conditions=False)

    d_x = nc.dram_tensor("x", [TOK, D], F32, kind="ExternalInput").ap()
    d_gamma = nc.dram_tensor("gamma", [D], F32, kind="ExternalInput").ap()
    d_beta = nc.dram_tensor("beta", [D], F32, kind="ExternalInput").ap()
    d_wqkv = nc.dram_tensor("wqkv", [D, 3 * D], F32, kind="ExternalInput").ap()
    d_wout = nc.dram_tensor("wout", [D, D], F32, kind="ExternalInput").ap()
    d_wfc = nc.dram_tensor("wfc", [FFH, D], F32, kind="ExternalInput").ap()
    d_bfc = nc.dram_tensor("bfc", [FFH], F32, kind="ExternalInput").ap()
    d_afc = nc.dram_tensor("afc", [16, D], F32, kind="ExternalInput").ap()
    d_bfcB = nc.dram_tensor("bfcB", [FFH, 16], F32, kind="ExternalInput").ap()
    d_wpr = nc.dram_tensor("wpr", [D, FFH], F32, kind="ExternalInput").ap()
    d_bpr = nc.dram_tensor("bpr", [D], F32, kind="ExternalInput").ap()
    d_apr = nc.dram_tensor("apr", [16, FFH], F32, kind="ExternalInput").ap()
    d_bprB = nc.dram_tensor("bprB", [D, 16], F32, kind="ExternalInput").ap()
    d_identf = nc.dram_tensor("identf", [128, 128], F32, kind="ExternalInput").ap()
    d_mask = nc.dram_tensor("maskm", [128, 256], BF16, kind="ExternalInput").ap()
    d_mask0 = nc.dram_tensor("mask0", [128, 128], BF16, kind="ExternalInput").ap()
    d_onesf = nc.dram_tensor("onesf", [1, 128], F32, kind="ExternalInput").ap()
    d_onesb = nc.dram_tensor("onesb", [1, 128], BF16, kind="ExternalInput").ap()

    d_y = nc.dram_tensor("y", [TOK, D], F32, kind="ExternalOutput").ap()
    d_dbg = {}
    if dbg:
        for name, shape, dt in [("wall", [128, 24576], BF16), ("bw", [128, 16], F32),
                                ("bwv", [1, 1024], F32), ("scl", [128, 8], F32),
                                ("btfc", [16, FFH], BF16), ("a2fc", [16, D], BF16),
                                ("stat", [128, 64], F32), ("xnfm", [128, 4096], BF16),
                                ("qd", [128, 4096], BF16), ("kd", [128, 4096], BF16),
                                ("vd", [128, 4096], BF16), ("ao", [128, 4096], BF16),
                                ("x1", [128, 4096], F32), ("hd", [128, HC * TILE], BF16),
                                ("wfcT0", [128, 1024], BF16), ("wprT0", [128, 4096], BF16)]:
            d_dbg[name] = nc.dram_tensor("dbg_" + name, shape, dt, kind="ExternalOutput").ap()

    d_wfcT = nc.dram_tensor("wfcT_scr", [FFH, D], BF16, kind="Internal").ap()
    d_wprT = nc.dram_tensor("wprT_scr", [D, FFH], BF16, kind="Internal").ap()
    d_woutb = nc.dram_tensor("wout_scr", [D, D], BF16, kind="Internal").ap()

    cnt = {}
    stack = ExitStack()
    e = stack.enter_context

    sems = {}
    for s in ("dc", "da", "db", "dwa", "dwb", "dx", "dy",
              "wo0", "wo1", "wf0", "wf1", "wp0", "wp1", "T", "V", "S", "BAR"):
        sems[s] = e(nc.semaphore(s))

    pb = [e(nc.psum_tensor(f"pb{i}", [128, 512], F32)) for i in range(8)]

    s_identf = e(nc.sbuf_tensor("sbident_f", [128, 128], F32))
    s_mask = e(nc.sbuf_tensor("sb_maskm", [128, 256], BF16))
    s_mask0 = e(nc.sbuf_tensor("sb_mask0", [128, 128], BF16))
    s_onesf = e(nc.sbuf_tensor("sb_onesf", [1, 128], F32))
    s_gcol = e(nc.sbuf_tensor("gcol", [128, 8], F32))
    s_g8col = e(nc.sbuf_tensor("g8col", [128, 8], F32))
    s_betab = e(nc.sbuf_tensor("betab", [128, 8], BF16))
    s_bw = e(nc.sbuf_tensor("bw", [128, 16], F32))
    s_bwv = e(nc.sbuf_tensor("bwv", [1, 1024], BF16))
    s_onesb = e(nc.sbuf_tensor("sb_onesb", [1, 128], BF16))
    s_bfc = e(nc.sbuf_tensor("bfccol", [128, 32], F32))
    s_bpr = e(nc.sbuf_tensor("bprcol", [128, 8], F32))
    s_cm05 = e(nc.sbuf_tensor("cm05", [128, 1], F32))
    s_cp05 = e(nc.sbuf_tensor("cp05", [128, 1], F32))
    s_scfc = e(nc.sbuf_tensor("scfc", [128, 1], F32))
    s_rsfc = e(nc.sbuf_tensor("rsfc", [128, 1], F32))
    s_scpr = e(nc.sbuf_tensor("scpr", [128, 1], F32))
    s_rspr = e(nc.sbuf_tensor("rspr", [128, 1], F32))
    s_am = e(nc.sbuf_tensor("amcols", [128, 32], F32))
    s_amc = e(nc.sbuf_tensor("amc", [128, 1], F32))
    s_amr = e(nc.sbuf_tensor("amr", [1, 128], F32))
    s_sc1 = e(nc.sbuf_tensor("sc1", [1, 1], F32))

    s_wall = e(nc.sbuf_tensor("wall", [128, DC * 3 * D], BF16))

    s_a2fc = e(nc.sbuf_tensor("a2fc", [16, D], BF16))
    s_btfc = e(nc.sbuf_tensor("btfc", [16, FFH], BF16))
    s_a2pr = e(nc.sbuf_tensor("a2pr", [16, FFH], BF16))
    s_btpr = e(nc.sbuf_tensor("btpr", [16, D], BF16))

    P = Prog(cnt)

    def dma(prog, eng, out_ap, in_ap, sem, slow=False):
        if slow:
            return prog.emit(eng, lambda E, o=out_ap, i=in_ap: E.dma_start(out=o, in_=i, allow_slow_non_contiguous=True), inc=sem, n=16)
        return prog.emit(eng, lambda E, o=out_ap, i=in_ap: E.dma_start(out=o, in_=i), inc=sem, n=16)

    def mm(prog, out_ap, lhsT, rhs, start, stop, tp=None, waits=()):
        def f(E, o=out_ap, l=lhsT, r=rhs):
            return E.matmul(o, l, r, start=start, stop=stop, tile_position=tp,
                            skip_group_check=True)
        return prog.emit("tensor", f, waits=waits, inc="T")

    def trans(prog, out_ap, in_ap, ident, start=True, stop=True, waits=()):
        def f(E, o=out_ap, i=in_ap, idn=ident):
            return E.matmul(o, i, idn, start=start, stop=stop, is_transpose=True,
                            skip_group_check=True)
        return prog.emit("tensor", f, waits=waits, inc="T")

    with ExitStack() as arena:
        a = arena.enter_context
        t_stage = a(nc.sbuf_tensor("stage", [128, 2 * FFH], F32))
        t_q = a(nc.sbuf_tensor("tq", [128, 1024], F32))
        t_k0 = a(nc.sbuf_tensor("tk0", [128, 1024], I32))
        t_k0f = a(nc.sbuf_tensor("tk0f", [128, 1024], F32))
        t_d = a(nc.sbuf_tensor("td", [128, 1024], F32))
        t_s1 = a(nc.sbuf_tensor("ts1", [128, 1024], F32))
        t_s2 = a(nc.sbuf_tensor("ts2", [128, 1024], F32))
        t_wqf = a(nc.sbuf_tensor("twqf", [128, FFH], F32))
        t_panel = a(nc.sbuf_tensor("tpanel", [128, FFH], BF16))
        t_small = a(nc.sbuf_tensor("tsmall", [128, 16], F32))
        t_arow = a(nc.sbuf_tensor("tarow", [16, FFH], F32))

        tok_dx = 0
        tok_dx = dma(P, "sync", s_identf[:], d_identf, "dc")
        tok_dx = dma(P, "sync", s_mask[:], d_mask, "dc")
        tok_dx = dma(P, "sync", s_mask0[:], d_mask0, "dc")
        tok_dx = dma(P, "sync", s_onesf[:], d_onesf, "dc")
        tok_dx = dma(P, "sync", s_onesb[:], d_onesb, "dc")
        tok_dx = dma(P, "sync", s_gcol[:], d_gamma.rearrange("(c p) -> p c", p=128), "dc", slow=True)
        tok_dx = dma(P, "sync", t_small[:, 0:8], d_beta.rearrange("(c p) -> p c", p=128), "dc", slow=True)
        tok_dx = dma(P, "sync", s_bfc[:], d_bfc.rearrange("(c p) -> p c", p=128), "dc", slow=True)
        tok_dx = dma(P, "sync", s_bpr[:], d_bpr.rearrange("(c p) -> p c", p=128), "dc", slow=True)
        tok_consts = tok_dx
        P.emit("vector", lambda E: E.memset(s_cm05[:], -0.5))
        P.emit("vector", lambda E: E.memset(s_cp05[:], 0.5))
        P.emit("vector", lambda E: E.tensor_scalar_mul(s_g8col[:], s_gcol[:], 0.125),
               waits=[("dc", tok_consts)])
        tv = P.emit("vector", lambda E: E.tensor_copy(out=s_betab[:], in_=t_small[:, 0:8]),
                    waits=[("dc", tok_consts)], inc="V")

        cast_toks = []
        for dc in range(DC):
            w = [("S", cast_toks[dc - 2])] if dc >= 2 else []
            sem = "da" if dc % 2 == 0 else "db"
            tok_l = P.emit("sync",
                           lambda E, dc=dc: E.dma_start(out=t_stage[:, (dc % 2) * 3072:(dc % 2) * 3072 + 3072],
                                                        in_=d_wqkv[dc * 128:(dc + 1) * 128, :]),
                           waits=w, inc=sem, n=16)
            tok = P.emit("scalar",
                         lambda E, dc=dc: E.copy(out=s_wall[:, dc * 3072:(dc + 1) * 3072],
                                                 in_=t_stage[:, (dc % 2) * 3072:(dc % 2) * 3072 + 3072]),
                         waits=[(sem, tok_l)], inc="S")
            cast_toks.append(tok)

        P.fence()
        for oc in range(16):
            for dc in range(DC):
                mm(P, pb[0][:, oc:oc + 1],
                   s_wall[:, dc * 3072 + oc * 128: dc * 3072 + (oc + 1) * 128],
                   s_betab[:, dc:dc + 1],
                   start=(dc == 0), stop=(dc == DC - 1),
                   waits=[("S", cast_toks[-1]), ("V", tv)])
        tok_T = P.emit("tensor", lambda E: E.nop(), inc="T")
        for half in range(2):
            for dc in range(DC):
                mm(P, pb[1 + half][0:1, 0:512],
                   s_betab[:, dc:dc + 1],
                   s_wall[:, dc * 3072 + 2048 + half * 512: dc * 3072 + 2048 + (half + 1) * 512],
                   start=(dc == 0), stop=(dc == DC - 1))
        tok_T = P.emit("tensor", lambda E: E.nop(), inc="T")
        tok_bw = P.emit("vector", lambda E: E.tensor_copy(out=s_bw[:], in_=pb[0][:, 0:16]),
                        waits=[("T", tok_T)], inc="V")
        tok_bw = P.emit("vector", lambda E: E.tensor_scalar_mul(s_bw[:, 0:8], s_bw[:, 0:8], 0.125),
                        inc="V")
        for half in range(2):
            tok_bwv = P.emit("vector",
                             lambda E, half=half: E.tensor_copy(out=s_bwv[0:1, half * 512:(half + 1) * 512],
                                                                in_=pb[1 + half][0:1, 0:512]),
                             waits=[("T", tok_T)], inc="V")
        fold_toks = []
        for dc in range(DC):
            P.emit("scalar",
                   lambda E, dc=dc: E.activation(out=s_wall[:, dc * 3072:dc * 3072 + 1024],
                                                 in_=s_wall[:, dc * 3072:dc * 3072 + 1024],
                                                 func=AF.Copy, scale=s_g8col[:, dc:dc + 1]),
                   waits=[("V", tok_bwv)])
            tok = P.emit("scalar",
                         lambda E, dc=dc: E.activation(out=s_wall[:, dc * 3072 + 1024:dc * 3072 + 3072],
                                                       in_=s_wall[:, dc * 3072 + 1024:dc * 3072 + 3072],
                                                       func=AF.Copy, scale=s_gcol[:, dc:dc + 1]),
                         inc="S")
            fold_toks.append(tok)

        P.fence()
        st_toks = [0, 0]
        cast_tk = [0, 0]
        for ic in range(DC):
            sem = "da" if ic % 2 == 0 else "db"
            ssem = "dwa" if ic % 2 == 0 else "dwb"
            tok_l = P.emit("sync",
                           lambda E, ic=ic: E.dma_start(out=t_stage[:, (ic % 2) * 4096:(ic % 2) * 4096 + 1024],
                                                        in_=d_wout[ic * 128:(ic + 1) * 128, :]),
                           waits=[("V", cast_tk[ic % 2])], inc=sem, n=16)
            tok_c = P.emit("vector",
                           lambda E, ic=ic: E.tensor_copy(out=t_panel[:, (ic % 2) * 1024:(ic % 2) * 1024 + 1024],
                                                          in_=t_stage[:, (ic % 2) * 4096:(ic % 2) * 4096 + 1024]),
                           waits=[(sem, tok_l), (ssem, st_toks[ic % 2])], inc="V")
            cast_tk[ic % 2] = tok_c
            st_toks[ic % 2] = P.emit(
                "gpsimd",
                lambda E, ic=ic: E.dma_start(out=d_woutb[ic * 128:(ic + 1) * 128, :],
                                             in_=t_panel[:, (ic % 2) * 1024:(ic % 2) * 1024 + 1024]),
                waits=[("V", tok_c)], inc=ssem, n=16)
        P.fence()

        tok_a = dma(P, "sync", t_arow[:, 0:1024], d_afc, "da")
        tva = P.emit("vector", lambda E: E.tensor_scalar(out=s_a2fc[:], in0=t_arow[:16, 0:1024],
                                                         scalar1=2.0, scalar2=None, op0=ALU.mult),
                     waits=[("da", tok_a)], inc="V")
        tok_a = P.emit("sync", lambda E: E.dma_start(out=t_arow[:], in_=d_apr),
                       waits=[("V", tva)], inc="da", n=16)
        tva = P.emit("vector", lambda E: E.tensor_scalar(out=s_a2pr[:], in0=t_arow[:16, :],
                                                         scalar1=2.0, scalar2=None, op0=ALU.mult),
                     waits=[("da", tok_a)], inc="V")
        P.fence()
        btr = [0, 0]
        bev = [0, 0]
        for c in range(HC):
            sem = "da" if c % 2 == 0 else "db"
            tok_l = P.emit("sync",
                           lambda E, c=c: E.dma_start(out=t_stage[:, (c % 2) * 4096:(c % 2) * 4096 + 16],
                                                      in_=d_bfcB[c * 128:(c + 1) * 128, :]),
                           waits=[("T", btr[c % 2])], inc=sem, n=16)
            tt = trans(P, pb[2 + c % 2][0:16, 0:128],
                       t_stage[:, (c % 2) * 4096:(c % 2) * 4096 + 16], s_identf[:],
                       waits=[(sem, tok_l), ("S", bev[c % 2])])
            btr[c % 2] = tt
            bev[c % 2] = P.emit("scalar",
                                lambda E, c=c: E.copy(out=s_btfc[:16, c * 128:(c + 1) * 128],
                                                      in_=pb[2 + c % 2][0:16, 0:128]),
                                waits=[("T", tt)], inc="S")
        P.fence()
        btr = [0, 0]
        bev = [0, 0]
        for c in range(DC):
            sem = "da" if c % 2 == 0 else "db"
            tok_l = P.emit("sync",
                           lambda E, c=c: E.dma_start(out=t_stage[:, (c % 2) * 4096:(c % 2) * 4096 + 16],
                                                      in_=d_bprB[c * 128:(c + 1) * 128, :]),
                           waits=[("T", btr[c % 2])], inc=sem, n=16)
            tt = trans(P, pb[2 + c % 2][0:16, 0:128],
                       t_stage[:, (c % 2) * 4096:(c % 2) * 4096 + 16], s_identf[:],
                       waits=[(sem, tok_l), ("S", bev[c % 2])])
            btr[c % 2] = tt
            bev[c % 2] = P.emit("scalar",
                                lambda E, c=c: E.copy(out=s_btpr[:16, c * 128:(c + 1) * 128],
                                                      in_=pb[2 + c % 2][0:16, 0:128]),
                                waits=[("T", tt)], inc="S")
        P.fence()

        def quant_weight(d_w, nchunk, free, s_scale, s_rscale, am_ncols, d_scrT, a2, bt,
                         panel_cols, tpos_cols):
            P.fence()
            red_tk = [0, 0]
            for c in range(nchunk):
                sl = (c % 2) * 4096
                sem = "da" if c % 2 == 0 else "db"
                tok_l = P.emit("sync",
                               lambda E, c=c, sl=sl: E.dma_start(out=t_stage[:, sl:sl + free],
                                                                 in_=d_w[c * 128:(c + 1) * 128, :]),
                               waits=[("V", red_tk[c % 2])], inc=sem, n=16)
                red_tk[c % 2] = P.emit("vector",
                                       lambda E, c=c, sl=sl: E.reduce_max(out=s_am[:, c:c + 1],
                                                                          in_=t_stage[:, sl:sl + free],
                                                                          axis=AX.X, apply_absolute_value=True),
                                       waits=[(sem, tok_l)], inc="V")
            P.fence()
            tok = P.emit("vector", lambda E: E.reduce_max(out=s_amc[:], in_=s_am[:, 0:am_ncols], axis=AX.X),
                         inc="V")
            tt = trans(P, pb[3][0:1, 0:128], s_amc[:], s_identf[:], waits=[("V", tok)])
            tok = P.emit("vector", lambda E: E.tensor_copy(out=s_amr[:], in_=pb[3][0:1, 0:128]),
                         waits=[("T", tt)], inc="V")
            tok = P.emit("vector", lambda E: E.reduce_max(out=s_sc1[:], in_=s_amr[:], axis=AX.X), inc="V")
            tok = P.emit("vector", lambda E: E.tensor_scalar_mul(s_sc1[:], s_sc1[:], float(np.float32(1.0) / np.float32(127.0))),
                         inc="V")
            tt = mm(P, pb[3][:, 0:1], s_onesf[:], s_sc1[:], start=True, stop=True,
                    waits=[("V", tok)])
            tok = P.emit("vector", lambda E: E.tensor_copy(out=s_scale[:], in_=pb[3][:, 0:1]),
                         waits=[("T", tt)], inc="V")
            tok = P.emit("vector", lambda E: E.reciprocal(out=s_rscale[:], in_=s_scale[:]), inc="V")
            P.fence()
            nsl = free // 1024
            tok_store = 0
            ev_bank = {}
            trans_last = 0
            tq_tk = [0, 0]
            for c in range(nchunk):
                sl = (c % 2) * 4096
                sem = "da" if c % 2 == 0 else "db"
                tok_l = P.emit("sync",
                               lambda E, c=c, sl=sl: E.dma_start(out=t_stage[:, sl:sl + free],
                                                                 in_=d_w[c * 128:(c + 1) * 128, :]),
                               waits=[("S", tq_tk[c % 2])], inc=sem, n=16)
                for s in range(nsl):
                    src = t_stage[:, sl + s * 1024: sl + (s + 1) * 1024]
                    tq = P.emit("scalar",
                                lambda E, src=src: E.activation(out=t_q[:], in_=src, func=AF.Copy,
                                                                scale=s_rscale[:]),
                                waits=[(sem, tok_l)], inc="S")
                    tk = P.emit("vector", lambda E: E.tensor_copy(out=t_k0[:], in_=t_q[:]),
                                waits=[("S", tq)], inc="V")
                    tk = P.emit("vector", lambda E: E.tensor_copy(out=t_k0f[:], in_=t_k0[:]), inc="V")
                    tk = P.emit("vector", lambda E: E.tensor_tensor(out=t_d[:], in0=t_q[:], in1=t_k0f[:],
                                                                    op=ALU.subtract), inc="V")
                    t1 = P.emit("scalar", lambda E: E.activation(out=t_s1[:], in_=t_d[:], func=AF.Sign,
                                                                 bias=s_cm05[:]),
                                waits=[("V", tk)], inc="S")
                    t1 = P.emit("scalar", lambda E: E.activation(out=t_s2[:], in_=t_d[:], func=AF.Sign,
                                                                 bias=s_cp05[:]), inc="S")
                    tk = P.emit("vector", lambda E: E.tensor_tensor(out=t_s1[:], in0=t_s1[:], in1=t_s2[:],
                                                                    op=ALU.add),
                                waits=[("S", t1)], inc="V")
                    tk = P.emit("vector", lambda E: E.scalar_tensor_tensor(out=t_d[:], in0=t_s1[:], scalar=0.5,
                                                                           in1=t_k0f[:], op0=ALU.mult, op1=ALU.add),
                                inc="V")
                    tw = P.emit("scalar",
                                lambda E, s=s: E.activation(out=t_wqf[:, s * 1024:(s + 1) * 1024], in_=t_d[:],
                                                            func=AF.Copy, scale=s_scale[:]),
                                waits=[("V", tk), ("T", trans_last)], inc="S")
                nblk = free // 128
                ev_tok = None
                for b in range(nblk):
                    bi = 4 + ((b // 4) % 2)
                    bank = pb[bi]
                    tt = trans(P, bank[:, (b % 4) * 128:(b % 4) * 128 + 128],
                               t_wqf[:, b * 128:(b + 1) * 128], s_identf[:],
                               start=True, stop=False,
                               waits=[("S", tw), ("S", ev_bank.get(bi, 0)),
                                      ("dwa", tok_store)])
                    tt = mm(P, bank[:, (b % 4) * 128:(b % 4) * 128 + 128],
                            a2[:16, b * 128:(b + 1) * 128],
                            bt[:16, c * 128:(c + 1) * 128],
                            start=False, stop=True)
                    if b % 4 == 3:
                        ev_tok = P.emit("scalar",
                                        lambda E, b=b, bank=bank: E.copy(out=t_panel[:, (b - 3) * 128:(b + 1) * 128],
                                                                         in_=bank[:]),
                                        waits=[("T", tt)], inc="S")
                        ev_bank[bi] = ev_tok
                trans_last = tt
                tq_tk[c % 2] = tq
                tok_store = P.emit("gpsimd",
                                   lambda E, c=c: E.dma_start(out=d_scrT[c * 128:(c + 1) * 128, :],
                                                              in_=t_panel[:, 0:free]),
                                   waits=[("S", ev_tok)], inc="dwa", n=16)
            P.fence()

        quant_weight(d_wfc, HC, 1024, s_scfc, s_rsfc, HC, d_wfcT, s_a2fc, s_btfc, 1024, None)
        quant_weight(d_wpr, DC, 4096, s_scpr, s_rspr, DC, d_wprT, s_a2pr, s_btpr, 4096, None)

        if dbg:
            P.fence()
            P.emit("sync", lambda E: E.nop(),
                   waits=[(k, cnt.get(k, 0)) for k in ("T", "V", "S")])
            dma(P, "sync", d_dbg["wall"], s_wall[:], "dc")
            dma(P, "sync", d_dbg["bw"], s_bw[:], "dc")
            dma(P, "sync", d_dbg["bwv"], s_bwv[:], "dc")
            P.emit("vector", lambda E: E.tensor_copy(out=s_am[:, 0:1], in_=s_scfc[:]))
            P.emit("vector", lambda E: E.tensor_copy(out=s_am[:, 1:2], in_=s_rsfc[:]))
            P.emit("vector", lambda E: E.tensor_copy(out=s_am[:, 2:3], in_=s_scpr[:]))
            tvd = P.emit("vector", lambda E: E.tensor_copy(out=s_am[:, 3:4], in_=s_rspr[:]), inc="V")
            dma(P, "sync", d_dbg["scl"], s_am[:, 0:8], "dc")
            dma(P, "sync", d_dbg["btfc"], s_btfc[:], "dc")
            dma(P, "sync", d_dbg["a2fc"], s_a2fc[:], "dc")
            dma(P, "sync", d_dbg["wfcT0"], d_wfcT[0:128, :], "dc")
            dma(P, "sync", d_dbg["wprT0"], d_wprT[0:128, :], "dc")
        for eng in ENGS:
            P.emit(eng, lambda E: E.nop(), inc="BAR")

        with nc.Block("setup") as b1:
            P.replay(b1, sems)

    s_x = e(nc.sbuf_tensor("xsb", [128, TILE // 128 * D], F32))
    s_xnb = e(nc.sbuf_tensor("xnb", [128, D], F32))
    s_xnfm = e(nc.sbuf_tensor("xnfm", [128, DC * TILE], BF16))
    s_q = e(nc.sbuf_tensor("qsb", [128, 8 * TILE], BF16))
    s_k = e(nc.sbuf_tensor("ksb", [128, 8 * TILE], BF16))
    s_v = e(nc.sbuf_tensor("vsb", [128, G * D], BF16))
    s_kprev = e(nc.sbuf_tensor("kprev", [128, 8 * 128], BF16))
    s_vprev = e(nc.sbuf_tensor("vprev", [128, D], BF16))
    s_ao = e(nc.sbuf_tensor("aosb", [128, 8 * TILE], BF16))
    s_h = e(nc.sbuf_tensor("hsb", [128, HC * TILE], BF16))
    s_sm = e(nc.sbuf_tensor("smbuf", [128, 4 * 256], F32))
    s_probs = e(nc.sbuf_tensor("probs", [128, 4 * 256], F32))
    s_pT = e(nc.sbuf_tensor("pT", [128, 4 * 256], BF16))
    s_stat = e(nc.sbuf_tensor("stat", [128, 64], F32))
    s_ofm = e(nc.sbuf_tensor("ofm", [128, 512], F32))
    s_wfcb = e(nc.sbuf_tensor("wfcb", [128, 2 * 1024], BF16))
    s_wprb = e(nc.sbuf_tensor("wprb", [128, 2 * 4096], BF16))
    s_woutb = e(nc.sbuf_tensor("woutb", [128, 2 * 1024], BF16))

    NEGMAX, ROWSUM, RECIP = 0, 4, 52
    SUM, SSQ, MU, MSQ, VARP, R1, V2P, ASC, NMA, T0 = 8, 12, 16, 20, 24, 28, 32, 36, 40, 44

    M = Prog(cnt)
    M.fence_snap = {}
    for eng in ENGS:
        M.emit(eng, lambda E: E.nop(), waits=[("BAR", cnt["BAR"])])

    tok_store = 0
    wout_cons = [0, 0]
    wfc_cons = [0, 0]
    wpr_cons = [0, 0]
    ofm_cons = [0, 0]
    ofm_trans_last = [0]
    xnb_cons = [0, 0]
    xnt_prev_ev = [0, 0]

    def emit_norm_transpose(M, a_col, nma_col):
        ev_last = None
        for g in range(G):
            sl = 0
            tok_x = M.emit("vector",
                           lambda E, g=g, sl=sl, a_col=a_col, nma_col=nma_col:
                           E.tensor_scalar(out=s_xnb[:, sl * D:(sl + 1) * D],
                                           in0=s_x[:, g * D:(g + 1) * D],
                                           scalar1=s_stat[:, a_col + g:a_col + g + 1],
                                           scalar2=s_stat[:, nma_col + g:nma_col + g + 1],
                                           op0=ALU.mult, op1=ALU.add),
                           waits=[("T", xnb_cons[sl])], inc="V")
            tt = None
            for half in range(2):
                bank = pb[half]
                for i in range(4):
                    dc = half * 4 + i
                    tt = trans(M, bank[:, i * 128:(i + 1) * 128],
                               s_xnb[:, sl * D + dc * 128: sl * D + (dc + 1) * 128],
                               s_identf[:],
                               waits=[("V", tok_x), ("S", xnt_prev_ev[half])])
                dst = bass.AP(s_xnfm, half * 4 * TILE + g * 128,
                              [[DC * TILE, 128], [TILE, 4], [1, 128]])
                tok = M.emit("scalar",
                             lambda E, bank=bank, dst=dst:
                             E.copy(out=dst, in_=bank[:].rearrange("p (a b) -> p a b", b=128)),
                             waits=[("T", tt)], inc="S")
                xnt_prev_ev[half] = tok
                ev_last = tok
            xnb_cons[sl] = tt
        return ev_last

    for t in range(NT):
        seq_start = (t % SEQ_TILES == 0)
        tok_x = dma(M, "sync", s_x[:].rearrange("p (g d) -> p g d", d=D),
                    d_x[t * TILE:(t + 1) * TILE, :].rearrange("(g p) d -> p g d", p=128),
                    "dx")

        def stats_and_norm(wait_tok, wait_sem, r_out_cols):
            tv = ts = None
            for g in range(G):
                xg = s_x[:, g * D:(g + 1) * D]
                tv = M.emit("vector",
                            lambda E, xg=xg, g=g: E.tensor_scalar(out=s_xnfm[:, 0:1024], in0=xg, scalar1=1.0,
                                                                  scalar2=None, op0=ALU.mult, op1=ALU.add,
                                                                  accum_out=s_stat[:, SUM + g:SUM + g + 1]),
                            waits=[(wait_sem, wait_tok)], inc="V")
                ts = M.emit("scalar",
                            lambda E, xg=xg, g=g: E.activation(out=s_xnfm[:, 1024:2048], in_=xg, func=AF.Square,
                                                               accum_out=s_stat[:, SSQ + g:SSQ + g + 1]),
                            waits=[(wait_sem, wait_tok)], inc="S")
            c = 1.0 / 1024.0
            tv = M.emit("vector", lambda E: E.tensor_scalar_mul(s_stat[:, MU:MU + 4], s_stat[:, SUM:SUM + 4], c),
                        waits=[("V", tv)], inc="V")
            tv = M.emit("vector", lambda E: E.tensor_scalar_mul(s_stat[:, MSQ:MSQ + 4], s_stat[:, SSQ:SSQ + 4], c),
                        waits=[("S", ts)], inc="V")
            tv = M.emit("vector", lambda E: E.tensor_tensor(out=s_stat[:, T0:T0 + 4], in0=s_stat[:, MU:MU + 4],
                                                            in1=s_stat[:, MU:MU + 4], op=ALU.mult), inc="V")
            tv = M.emit("vector", lambda E: E.tensor_tensor(out=s_stat[:, VARP:VARP + 4], in0=s_stat[:, MSQ:MSQ + 4],
                                                            in1=s_stat[:, T0:T0 + 4], op=ALU.subtract), inc="V")
            tv = M.emit("vector", lambda E: E.tensor_scalar_add(s_stat[:, VARP:VARP + 4], s_stat[:, VARP:VARP + 4],
                                                                1e-5), inc="V")
            ts = M.emit("scalar", lambda E: E.activation(out=s_stat[:, T0:T0 + 4], in_=s_stat[:, VARP:VARP + 4],
                                                         func=AF.Ln),
                        waits=[("V", tv)], inc="S")
            ts = M.emit("scalar", lambda E: E.activation(out=s_stat[:, R1:R1 + 4], in_=s_stat[:, T0:T0 + 4],
                                                         func=AF.Exp, scale=s_cm05[:]), inc="S")
            return tv, ts

        tv, ts = stats_and_norm(tok_x, "dx", R1)
        T1 = 48
        tv = M.emit("vector", lambda E: E.reciprocal(out=s_stat[:, T1:T1 + 4], in_=s_stat[:, VARP:VARP + 4]),
                    inc="V")
        tv = M.emit("vector", lambda E: E.tensor_scalar_add(s_stat[:, V2P:V2P + 4], s_stat[:, VARP:VARP + 4], -1e-5),
                    inc="V")
        tv = M.emit("vector", lambda E: E.tensor_tensor(out=s_stat[:, V2P:V2P + 4], in0=s_stat[:, V2P:V2P + 4],
                                                        in1=s_stat[:, T1:T1 + 4], op=ALU.mult), inc="V")
        tv = M.emit("vector", lambda E: E.tensor_scalar_add(s_stat[:, V2P:V2P + 4], s_stat[:, V2P:V2P + 4], 1e-5),
                    inc="V")
        ts = M.emit("scalar", lambda E: E.activation(out=s_stat[:, T1:T1 + 4], in_=s_stat[:, V2P:V2P + 4],
                                                     func=AF.Ln),
                    waits=[("V", tv), ("S", ts)], inc="S")
        ts = M.emit("scalar", lambda E: E.activation(out=s_stat[:, V2P:V2P + 4], in_=s_stat[:, T1:T1 + 4],
                                                     func=AF.Exp, scale=s_cm05[:]), inc="S")
        tv = M.emit("vector", lambda E: E.tensor_tensor(out=s_stat[:, ASC:ASC + 4], in0=s_stat[:, R1:R1 + 4],
                                                        in1=s_stat[:, V2P:V2P + 4], op=ALU.mult),
                    waits=[("S", ts)], inc="V")
        tv = M.emit("vector", lambda E: E.scalar_tensor_tensor(out=s_stat[:, NMA:NMA + 4], in0=s_stat[:, MU:MU + 4],
                                                               scalar=-1.0, in1=s_stat[:, ASC:ASC + 4],
                                                               op0=ALU.mult, op1=ALU.mult), inc="V")
        tok_xnfm = emit_norm_transpose(M, ASC, NMA)

        if dbg and t == 0:
            M.fence()
            M.emit("sync", lambda E: E.nop(), waits=[(k, cnt.get(k, 0)) for k in ("T", "V", "S")])
            dma(M, "sync", d_dbg["stat"], s_stat[:], "dc")
            dma(M, "sync", d_dbg["xnfm"], s_xnfm[:], "dc")
            for _e in ("vector", "scalar", "tensor", "gpsimd"):
                M.emit(_e, lambda E: E.nop(), waits=[("dc", cnt["dc"])])
        M.fence()
        qk_ev = []
        for oc in range(16):
            bank = pb[2 + (oc % 2)]
            tt = None
            for dc in range(DC):
                tt = mm(M, bank[:],
                        s_wall[:, dc * 3072 + oc * 128: dc * 3072 + (oc + 1) * 128],
                        s_xnfm[:, dc * TILE:(dc + 1) * TILE],
                        start=(dc == 0), stop=(dc == DC - 1),
                        waits=[("S", qk_ev[oc - 2] if oc >= 2 else 0)])
            dst = s_q if oc < 8 else s_k
            fc = oc % 8
            tok = M.emit("scalar",
                         lambda E, dst=dst, fc=fc, oc=oc, bank=bank:
                         E.activation(out=dst[:, fc * TILE:(fc + 1) * TILE], in_=bank[:],
                                      func=AF.Identity, bias=s_bw[:, oc:oc + 1]),
                         waits=[("T", tt)], inc="S")
            qk_ev.append(tok)
        v_ev = []
        for g in range(G):
            bh = [pb[4 + (g % 2) * 2], pb[5 + (g % 2) * 2]]
            tt = None
            for half in range(2):
                for dc in range(DC):
                    tt = mm(M, bh[half][:],
                            s_xnfm[:, dc * TILE + g * 128: dc * TILE + (g + 1) * 128],
                            s_wall[:, dc * 3072 + 2048 + half * 512: dc * 3072 + 2048 + (half + 1) * 512],
                            start=(dc == 0), stop=False,
                            waits=[("V", v_ev[g - 2] if g >= 2 else 0)])
                tt = mm(M, bh[half][:], s_onesb[:], s_bwv[0:1, half * 512:(half + 1) * 512],
                        start=False, stop=True)
            tok = None
            for half in range(2):
                tok = M.emit("vector",
                             lambda E, g=g, half=half, b=bh[half]:
                             E.tensor_copy(out=s_v[:, g * D + half * 512: g * D + (half + 1) * 512], in_=b[:]),
                             waits=[("T", tt)], inc="V")
            v_ev.append(tok)

        if dbg and t == 0:
            M.fence()
            M.emit("sync", lambda E: E.nop(), waits=[(k, cnt.get(k, 0)) for k in ("T", "V", "S")])
            dma(M, "sync", d_dbg["qd"], s_q[:], "dc")
            dma(M, "sync", d_dbg["kd"], s_k[:], "dc")
            dma(M, "sync", d_dbg["vd"], s_v[:], "dc")
            for _e in ("vector", "scalar", "tensor", "gpsimd"):
                M.emit(_e, lambda E: E.nop(), waits=[("dc", cnt["dc"])])
        M.fence()
        NI = G * H
        ctx = []
        for w in range(G):
            fw = (seq_start and w == 0)
            for h in range(H):
                ctx.append((w, h, fw))
        tokA = [0] * NI; tokB = [0] * NI; tokC = [0] * NI; tokD = [0] * NI
        tokE = [0] * NI; tokF = [0] * NI; tokG = [0] * NI; tokH = [0] * NI
        engH = [0] * NI

        def stA(i):
            w, h, fw = ctx[i]
            po = 64 * (h % 2); fc = h // 2
            qs = s_q[po:po + 64, fc * TILE + w * 128: fc * TILE + (w + 1) * 128]
            bankS = pb[i % 4]
            wts = [("S", qk_ev[8 + fc]), ("V", tokB[i - 4] if i >= 4 else 0)]
            if fw:
                tt = mm(M, bankS[:, 0:128], qs, s_k[po:po + 64, fc * TILE: fc * TILE + 128],
                        start=True, stop=True, tp=(po, 0), waits=wts)
            elif w == 0:
                tt = mm(M, bankS[:, 0:128], qs, s_kprev[po:po + 64, fc * 128:(fc + 1) * 128],
                        start=True, stop=True, tp=(po, 0), waits=wts)
                tt = mm(M, bankS[:, 128:256], qs, s_k[po:po + 64, fc * TILE: fc * TILE + 128],
                        start=True, stop=True, tp=(po, 0))
            else:
                tt = mm(M, bankS[:, 0:256], qs,
                        s_k[po:po + 64, fc * TILE + (w - 1) * 128: fc * TILE + (w + 1) * 128],
                        start=True, stop=True, tp=(po, 0), waits=wts)
            tokA[i] = tt

        def stB(i):
            w, h, fw = ctx[i]
            nk = 128 if fw else 256
            wh = i % 4
            msk = s_mask0 if fw else s_mask
            bankS = pb[wh]
            M.emit("vector",
                   lambda E, wh=wh, nk=nk, msk=msk, bankS=bankS:
                   E.tensor_tensor(out=s_sm[:, wh * 256: wh * 256 + nk], in0=bankS[:, 0:nk],
                                   in1=msk[:, 0:nk], op=ALU.add),
                   waits=[("T", tokA[i]), ("S", tokC[i - 4] if i >= 4 else 0)], inc="V")
            tokB[i] = M.emit("vector",
                             lambda E, wh=wh, nk=nk: E.reduce_max(out=s_stat[:, NEGMAX + wh:NEGMAX + wh + 1],
                                                                  in_=s_sm[:, wh * 256: wh * 256 + nk],
                                                                  axis=AX.X, negate=True),
                             inc="V")

        def stC(i):
            w, h, fw = ctx[i]
            nk = 128 if fw else 256
            wh = i % 4
            tokC[i] = M.emit("scalar",
                             lambda E, wh=wh, nk=nk: E.activation(out=s_probs[:, wh * 256: wh * 256 + nk],
                                                                  in_=s_sm[:, wh * 256: wh * 256 + nk],
                                                                  func=AF.Exp,
                                                                  bias=s_stat[:, NEGMAX + wh:NEGMAX + wh + 1],
                                                                  accum_out=s_stat[:, ROWSUM + wh:ROWSUM + wh + 1]),
                             waits=[("V", tokB[i]), ("T", tokE[i - 4] if i >= 4 else 0)], inc="S")

        def stD(i):
            w, h, fw = ctx[i]
            nk = 128 if fw else 256
            wh = i % 4
            M.emit("vector",
                   lambda E, wh=wh: E.reciprocal(out=s_stat[:, RECIP + wh:RECIP + wh + 1],
                                                 in_=s_stat[:, ROWSUM + wh:ROWSUM + wh + 1]),
                   waits=[("S", tokC[i])], inc="V")
            tokD[i] = M.emit("vector",
                             lambda E, wh=wh, nk=nk: E.tensor_scalar_mul(s_probs[:, wh * 256: wh * 256 + nk],
                                                                         s_probs[:, wh * 256: wh * 256 + nk],
                                                                         s_stat[:, RECIP + wh:RECIP + wh + 1]),
                             inc="V")

        def stE(i):
            w, h, fw = ctx[i]
            nk = 128 if fw else 256
            wh = i % 4
            bankT = pb[4 + i % 2]
            tt2 = None
            for kc in range(nk // 128):
                tt2 = trans(M, bankT[:, kc * 128:(kc + 1) * 128],
                            s_probs[:, wh * 256 + kc * 128: wh * 256 + (kc + 1) * 128],
                            s_identf[:],
                            waits=[("V", tokD[i]), ("S", tokF[i - 2] if i >= 2 else 0)])
            tokE[i] = tt2

        def stF(i):
            w, h, fw = ctx[i]
            nk = 128 if fw else 256
            wh = i % 4
            bankT = pb[4 + i % 2]
            tokF[i] = M.emit("scalar",
                             lambda E, wh=wh, nk=nk, bankT=bankT:
                             E.copy(out=s_pT[:, wh * 256: wh * 256 + nk], in_=bankT[:, 0:nk]),
                             waits=[("T", tokE[i])], inc="S")

        def stG(i):
            w, h, fw = ctx[i]
            wh = i % 4
            po = 64 * (h % 2)
            bankO = pb[6 + i % 2]
            pvw = [("S", tokF[i]), ("V", v_ev[w])]
            if i >= 2:
                pvw.append((("V", "S")[engH[i - 2]], tokH[i - 2]))
            if fw:
                tt3 = mm(M, bankO[po:po + 64, 0:128],
                         s_v[:, w * D + h * DH: w * D + (h + 1) * DH],
                         s_pT[:, wh * 256: wh * 256 + 128],
                         start=True, stop=True, tp=(0, po), waits=pvw)
            else:
                vprev_ap = (s_vprev[:, h * DH:(h + 1) * DH] if w == 0
                            else s_v[:, (w - 1) * D + h * DH: (w - 1) * D + h * DH + DH])
                tt3 = mm(M, bankO[po:po + 64, 0:128],
                         vprev_ap, s_pT[:, wh * 256: wh * 256 + 128],
                         start=True, stop=False, tp=(0, po), waits=pvw)
                tt3 = mm(M, bankO[po:po + 64, 0:128],
                         s_v[:, w * D + h * DH: w * D + (h + 1) * DH],
                         s_pT[:, wh * 256 + 128: wh * 256 + 256],
                         start=False, stop=True, tp=(0, po))
            tokG[i] = tt3

        def stH(i):
            w, h, fw = ctx[i]
            po = 64 * (h % 2); fc = h // 2
            bankO = pb[6 + i % 2]
            if i % 2 == 0:
                tokH[i] = M.emit("vector",
                                 lambda E, po=po, fc=fc, w=w, bankO=bankO:
                                 E.tensor_copy(out=s_ao[po:po + 64, fc * TILE + w * 128: fc * TILE + (w + 1) * 128],
                                               in_=bankO[po:po + 64, 0:128]),
                                 waits=[("T", tokG[i])], inc="V")
                engH[i] = 0
            else:
                tokH[i] = M.emit("scalar",
                                 lambda E, po=po, fc=fc, w=w, bankO=bankO:
                                 E.copy(out=s_ao[po:po + 64, fc * TILE + w * 128: fc * TILE + (w + 1) * 128],
                                        in_=bankO[po:po + 64, 0:128]),
                                 waits=[("T", tokG[i])], inc="S")
                engH[i] = 1

        stages = (stA, stB, stC, stD, stE, stF, stG, stH)
        for j in range(NI + len(stages) - 1):
            for k, st in enumerate(stages):
                i = j - k
                if 0 <= i < NI:
                    st(i)
        last_i = NI - 1
        pv_ev_last = M.emit("vector", lambda E: E.memset(s_stat[:, 60:61], 0.0),
                            waits=[(("V", "S")[engH[last_i]], tokH[last_i]),
                                   (("V", "S")[engH[last_i - 1]], tokH[last_i - 1])], inc="V")

        if (t + 1) % SEQ_TILES != 0:
            kp = M.emit("vector",
                        lambda E: E.tensor_copy(
                            out=s_kprev[:].rearrange("p (f c) -> p f c", c=128),
                            in_=s_k[:].rearrange("p (f t) -> p f t", t=TILE)[:, :, TILE - 128:TILE]),
                        waits=[("S", qk_ev[-1])], inc="V")
            vp = M.emit("vector",
                        lambda E: E.tensor_copy(out=s_vprev[:], in_=s_v[:, (G - 1) * D: G * D]),
                        waits=[("V", v_ev[-1])], inc="V")

        if dbg and t == 0:
            M.fence()
            M.emit("sync", lambda E: E.nop(), waits=[(k, cnt.get(k, 0)) for k in ("T", "V", "S")])
            dma(M, "sync", d_dbg["ao"], s_ao[:], "dc")
            for _e in ("vector", "scalar", "tensor", "gpsimd"):
                M.emit(_e, lambda E: E.nop(), waits=[("dc", cnt["dc"])])
        M.fence()
        res_toks = []
        for ic in range(DC):
            sl = ic % 2
            tok_w = M.emit("gpsimd",
                           lambda E, ic=ic, sl=sl: E.dma_start(out=s_woutb[:, sl * 1024:(sl + 1) * 1024],
                                                               in_=d_woutb[ic * 128:(ic + 1) * 128, :]),
                           waits=[("T", wout_cons[sl])], inc="wo%d" % sl, n=16)
            tt = None
            for g in range(G):
                for half in range(2):
                    tt = mm(M, pb[2 * g + half][:],
                            s_ao[:, ic * TILE + g * 128: ic * TILE + (g + 1) * 128],
                            s_woutb[:, sl * 1024 + half * 512: sl * 1024 + (half + 1) * 512],
                            start=(ic == 0), stop=(ic == DC - 1),
                            waits=[("wo%d" % sl, tok_w), ("V", pv_ev_last)])
            wout_cons[sl] = tt
        tok_T = M.emit("tensor", lambda E: E.nop(), inc="T")
        tvlast = None
        for g in range(G):
            for half in range(2):
                tvlast = M.emit("vector",
                                lambda E, g=g, half=half:
                                E.scalar_tensor_tensor(out=s_x[:, g * D + half * 512: g * D + (half + 1) * 512],
                                                       in0=pb[2 * g + half][:], scalar=1.0,
                                                       in1=s_x[:, g * D + half * 512: g * D + (half + 1) * 512],
                                                       op0=ALU.mult, op1=ALU.add),
                                waits=[("T", tok_T)], inc="V")

        if dbg and t == 0:
            M.fence()
            M.emit("sync", lambda E: E.nop(), waits=[(k, cnt.get(k, 0)) for k in ("T", "V", "S")])
            dma(M, "sync", d_dbg["x1"], s_x[:], "dc")
            for _e in ("vector", "scalar", "tensor", "gpsimd"):
                M.emit(_e, lambda E: E.nop(), waits=[("dc", cnt["dc"])])
        M.fence()
        tv, ts = stats_and_norm(tvlast, "V", R1)
        xnb_toks = []
        for g in range(G):
            tok = M.emit("vector",
                         lambda E, g=g: E.scalar_tensor_tensor(out=s_stat[:, NMA + g:NMA + g + 1],
                                                               in0=s_stat[:, MU + g:MU + g + 1], scalar=-1.0,
                                                               in1=s_stat[:, R1 + g:R1 + g + 1],
                                                               op0=ALU.mult, op1=ALU.mult),
                         waits=[("S", ts)], inc="V")
        tok_xnfm2 = emit_norm_transpose(M, R1, NMA)
        ev_tr = [tok_xnfm2]

        M.fence()
        h_toks = []
        for c in range(HC):
            sl = c % 2
            tok_w = M.emit("gpsimd",
                           lambda E, c=c, sl=sl: E.dma_start(out=s_wfcb[:, sl * 1024:(sl + 1) * 1024],
                                                             in_=d_wfcT[c * 128:(c + 1) * 128, :]),
                           waits=[("T", wfc_cons[sl])], inc="wf%d" % sl, n=16)
            bank = pb[2 + sl]
            tt = None
            for dc in range(DC):
                tt = mm(M, bank[:],
                        s_wfcb[:, sl * 1024 + dc * 128: sl * 1024 + (dc + 1) * 128],
                        s_xnfm[:, dc * TILE:(dc + 1) * TILE],
                        start=(dc == 0), stop=(dc == DC - 1),
                        waits=[("wf%d" % sl, tok_w), ("S", ev_tr[-1]),
                               ("S", h_toks[c - 2] if c >= 2 else 0)])
            wfc_cons[sl] = tt
            tok = M.emit("scalar",
                         lambda E, c=c, bank=bank: E.activation(out=s_h[:, c * TILE:(c + 1) * TILE],
                                                                in_=bank[:], func=AF.Gelu,
                                                                bias=s_bfc[:, c:c + 1]),
                         waits=[("T", tt)], inc="S")
            h_toks.append(tok)

        if dbg and t == 0:
            M.fence()
            M.emit("sync", lambda E: E.nop(), waits=[(k, cnt.get(k, 0)) for k in ("T", "V", "S")])
            dma(M, "sync", d_dbg["hd"], s_h[:], "dc")
            for _e in ("vector", "scalar", "tensor", "gpsimd"):
                M.emit(_e, lambda E: E.nop(), waits=[("dc", cnt["dc"])])
        M.fence()
        last_res = None
        for do in range(DC):
            sl = do % 2
            tok_w = M.emit("gpsimd",
                           lambda E, do=do, sl=sl: E.dma_start(out=s_wprb[:, sl * 4096:(sl + 1) * 4096],
                                                               in_=d_wprT[do * 128:(do + 1) * 128, :]),
                           waits=[("T", wpr_cons[sl])], inc="wp%d" % sl, n=16)
            bank = pb[0 + sl]
            tt = None
            for c in range(HC):
                tt = mm(M, bank[:],
                        s_wprb[:, sl * 4096 + c * 128: sl * 4096 + (c + 1) * 128],
                        s_h[:, c * TILE:(c + 1) * TILE],
                        start=(c == 0), stop=(c == HC - 1),
                        waits=[("wp%d" % sl, tok_w), ("S", h_toks[-1]),
                               ("V", ofm_cons[sl])])
            wpr_cons[sl] = tt
            tok_ev = M.emit("scalar",
                            lambda E, bank=bank, do=do:
                            E.activation(out=s_ofm[:, 0:512], in_=bank[:],
                                         func=AF.Identity, bias=s_bpr[:, do:do + 1]),
                            waits=[("T", tt), ("T", ofm_trans_last[0])], inc="S")
            bank2 = pb[4 + sl]
            tt2 = None
            for g in range(G):
                tt2 = trans(M, bank2[:, g * 128:(g + 1) * 128],
                            s_ofm[:, g * 128:(g + 1) * 128], s_identf[:],
                            waits=[("S", tok_ev), ("V", ofm_cons[sl])])
            ofm_trans_last[0] = tt2
            tok_res = M.emit("vector",
                             lambda E, do=do, bank2=bank2:
                             E.scalar_tensor_tensor(
                                 out=s_x[:].rearrange("p (g d) -> p g d", d=D)[:, :, do * 128:(do + 1) * 128],
                                 in0=bank2[:].rearrange("p (g c) -> p g c", c=128), scalar=1.0,
                                 in1=s_x[:].rearrange("p (g d) -> p g d", d=D)[:, :, do * 128:(do + 1) * 128],
                                 op0=ALU.mult, op1=ALU.add),
                             waits=[("T", tt2)], inc="V")
            ofm_cons[sl] = tok_res
            last_res = tok_res

        tok_store = M.emit("sync",
                           lambda E, t=t: E.dma_start(
                               out=d_y[t * TILE:(t + 1) * TILE, :].rearrange("(g p) d -> p g d", p=128),
                               in_=s_x[:].rearrange("p (g d) -> p g d", d=D)),
                           waits=[("V", last_res)], inc="dy", n=16)
        M.fence()
        M.emit("sync", lambda E: E.nop(), waits=[("dy", tok_store)])

    M.emit("sync", lambda E: E.nop(), waits=[("dy", cnt["dy"])])

    with nc.Block("main") as b2:
        M.replay(b2, sems)

    stack.close()
    return nc


_NC_CACHE = None


def kernel(L0_attn_x, L0_attn_ln_gamma, L0_attn_ln_beta, L0_attn_w_qkv, L0_attn_w_out,
           L0_ffn_W_fc, L0_ffn_b_fc, L0_ffn_A_fc, L0_ffn_B_fc,
           L0_ffn_W_proj, L0_ffn_b_proj, L0_ffn_A_proj, L0_ffn_B_proj):
    global _NC_CACHE
    x = np.ascontiguousarray(np.asarray(L0_attn_x, dtype=np.float32))
    B, N, Dm = x.shape
    ncores = 8
    per = B // ncores

    if _NC_CACHE is None:
        _NC_CACHE = build_program(dbg=DBG)
    nc = _NC_CACHE

    ident_f = np.eye(128, dtype=np.float32)
    ii = np.arange(128)[:, None]
    jj = np.arange(256)[None, :]
    prev = jj < 128
    jjm = np.where(prev, jj, jj - 128)
    bad = np.where(prev, jjm < ii, jjm > ii)
    mask_m = np.where(bad, np.float32(NEG), np.float32(0.0)).astype(BFNP)
    mask_0 = np.where(np.arange(128)[None, :] > ii, np.float32(NEG), np.float32(0.0)).astype(BFNP)
    ones_f = np.ones((1, 128), dtype=np.float32)

    common = dict(
        gamma=np.asarray(L0_attn_ln_gamma, np.float32),
        beta=np.asarray(L0_attn_ln_beta, np.float32),
        wqkv=np.asarray(L0_attn_w_qkv, np.float32),
        wout=np.asarray(L0_attn_w_out, np.float32),
        wfc=np.asarray(L0_ffn_W_fc, np.float32),
        bfc=np.asarray(L0_ffn_b_fc, np.float32),
        afc=np.asarray(L0_ffn_A_fc, np.float32),
        bfcB=np.asarray(L0_ffn_B_fc, np.float32),
        wpr=np.asarray(L0_ffn_W_proj, np.float32),
        bpr=np.asarray(L0_ffn_b_proj, np.float32),
        apr=np.asarray(L0_ffn_A_proj, np.float32),
        bprB=np.asarray(L0_ffn_B_proj, np.float32),
        identf=ident_f, maskm=mask_m, mask0=mask_0, onesf=ones_f,
        onesb=np.ones((1, 128), dtype=BFNP),
    )
    in_maps = []
    for c in range(ncores):
        m = dict(common)
        m["x"] = np.ascontiguousarray(x[c * per:(c + 1) * per].reshape(TOK, Dm))
        in_maps.append(m)

    res = run_bass_kernel_spmd(nc, in_maps, core_ids=list(range(ncores)))
    out = np.empty((B, N, Dm), np.float32)
    for c in range(ncores):
        out[c * per:(c + 1) * per] = res.results[c]["y"].reshape(per, N, Dm)
    if DBG:
        global DBG_RES
        DBG_RES = res.results
    return out


# version 11
# speedup vs baseline: 1.4062x; 1.0105x over previous
import numpy as np
import ml_dtypes
from contextlib import ExitStack

import concourse.bass as bass
import concourse.mybir as mybir
from concourse.bass_utils import run_bass_kernel_spmd

F32 = mybir.dt.float32
BF16 = mybir.dt.bfloat16
I32 = mybir.dt.int32
AF = mybir.ActivationFunctionType
ALU = mybir.AluOpType
AX = mybir.AxisListType
BFNP = ml_dtypes.bfloat16

D = 1024
TOK = 4096
TILE = 512
NT = TOK // TILE
G = TILE // 128
DC = D // 128
H = 16
DH = 64
FFH = 4096
HC = FFH // 128
NEG = -1.0e30
SEQ_TILES = 2048 // TILE

ENGS = ("sync", "tensor", "vector", "scalar", "gpsimd")


class Prog:

    def __init__(self, cnt):
        self.ops = {e: [] for e in ENGS}
        self.cnt = cnt
        self.last_wait = {e: {} for e in ENGS}
        self.fence_snap = {}

    def emit(self, eng, fn, waits=(), inc=None, n=1):
        own = _OWN.get(eng)
        merged = dict(self.fence_snap)
        if own in ("V", "S"):
            if inc is None:
                inc = own
            merged[own] = self.cnt.get(own, 0)
        for s, v in waits:
            if v > merged.get(s, 0):
                merged[s] = v
        w = []
        for s, v in merged.items():
            if s == "T" and eng == "tensor":
                continue
            if v > self.last_wait[eng].get(s, 0):
                self.last_wait[eng][s] = v
                w.append((s, v))
        self.ops[eng].append((fn, w, inc, n))
        if inc is not None:
            self.cnt[inc] = self.cnt.get(inc, 0) + n
            return self.cnt[inc]
        return None

    def fence(self):
        self.fence_snap = {s: self.cnt.get(s, 0) for s in ("T", "V", "S")}

    def replay(self, block, sems):
        for ename in ENGS:
            ops = self.ops[ename]

            def body(eng, ops=ops):
                for fn, w, inc, n in ops:
                    for s, v in w:
                        eng.wait_ge(sems[s], v)
                    instr = fn(eng)
                    if inc is not None:
                        instr.then_inc(sems[inc], n)
            getattr(block, ename)(body)


_OWN = {"tensor": "T", "vector": "V", "scalar": "S"}


DBG = False


def build_program(dbg=False):
    nc = bass.Bass("TRN2", target_bir_lowering=False, detect_race_conditions=False)

    d_x = nc.dram_tensor("x", [TOK, D], F32, kind="ExternalInput").ap()
    d_gamma = nc.dram_tensor("gamma", [D], F32, kind="ExternalInput").ap()
    d_beta = nc.dram_tensor("beta", [D], F32, kind="ExternalInput").ap()
    d_wqkv = nc.dram_tensor("wqkv", [D, 3 * D], F32, kind="ExternalInput").ap()
    d_wout = nc.dram_tensor("wout", [D, D], F32, kind="ExternalInput").ap()
    d_wfc = nc.dram_tensor("wfc", [FFH, D], F32, kind="ExternalInput").ap()
    d_bfc = nc.dram_tensor("bfc", [FFH], F32, kind="ExternalInput").ap()
    d_afc = nc.dram_tensor("afc", [16, D], F32, kind="ExternalInput").ap()
    d_bfcB = nc.dram_tensor("bfcB", [FFH, 16], F32, kind="ExternalInput").ap()
    d_wpr = nc.dram_tensor("wpr", [D, FFH], F32, kind="ExternalInput").ap()
    d_bpr = nc.dram_tensor("bpr", [D], F32, kind="ExternalInput").ap()
    d_apr = nc.dram_tensor("apr", [16, FFH], F32, kind="ExternalInput").ap()
    d_bprB = nc.dram_tensor("bprB", [D, 16], F32, kind="ExternalInput").ap()
    d_identf = nc.dram_tensor("identf", [128, 128], F32, kind="ExternalInput").ap()
    d_mask = nc.dram_tensor("maskm", [128, 256], BF16, kind="ExternalInput").ap()
    d_mask0 = nc.dram_tensor("mask0", [128, 128], BF16, kind="ExternalInput").ap()
    d_onesf = nc.dram_tensor("onesf", [1, 128], F32, kind="ExternalInput").ap()
    d_onesb = nc.dram_tensor("onesb", [1, 128], BF16, kind="ExternalInput").ap()

    d_y = nc.dram_tensor("y", [TOK, D], F32, kind="ExternalOutput").ap()
    d_dbg = {}
    if dbg:
        for name, shape, dt in [("wall", [128, 24576], BF16), ("bw", [128, 16], F32),
                                ("bwv", [1, 1024], F32), ("scl", [128, 8], F32),
                                ("btfc", [16, FFH], BF16), ("a2fc", [16, D], BF16),
                                ("stat", [128, 64], F32), ("xnfm", [128, 4096], BF16),
                                ("qd", [128, 4096], BF16), ("kd", [128, 4096], BF16),
                                ("vd", [128, 4096], BF16), ("ao", [128, 4096], BF16),
                                ("x1", [128, 4096], F32), ("hd", [128, HC * TILE], BF16),
                                ("wfcT0", [128, 1024], BF16), ("wprT0", [128, 4096], BF16)]:
            d_dbg[name] = nc.dram_tensor("dbg_" + name, shape, dt, kind="ExternalOutput").ap()

    d_wfcT = nc.dram_tensor("wfcT_scr", [FFH, D], BF16, kind="Internal").ap()
    d_wprT = nc.dram_tensor("wprT_scr", [D, FFH], BF16, kind="Internal").ap()
    d_woutb = nc.dram_tensor("wout_scr", [D, D], BF16, kind="Internal").ap()

    cnt = {}
    stack = ExitStack()
    e = stack.enter_context

    sems = {}
    for s in ("dc", "da", "db", "dwa", "dwb", "dx", "dy",
              "wo0", "wo1", "wf0", "wf1", "wp0", "wp1", "T", "V", "S", "BAR"):
        sems[s] = e(nc.semaphore(s))

    pb = [e(nc.psum_tensor(f"pb{i}", [128, 512], F32)) for i in range(8)]

    s_identf = e(nc.sbuf_tensor("sbident_f", [128, 128], F32))
    s_mask = e(nc.sbuf_tensor("sb_maskm", [128, 256], BF16))
    s_mask0 = e(nc.sbuf_tensor("sb_mask0", [128, 128], BF16))
    s_onesf = e(nc.sbuf_tensor("sb_onesf", [1, 128], F32))
    s_gcol = e(nc.sbuf_tensor("gcol", [128, 8], F32))
    s_g8col = e(nc.sbuf_tensor("g8col", [128, 8], F32))
    s_betab = e(nc.sbuf_tensor("betab", [128, 8], BF16))
    s_bw = e(nc.sbuf_tensor("bw", [128, 16], F32))
    s_bwv = e(nc.sbuf_tensor("bwv", [1, 1024], BF16))
    s_onesb = e(nc.sbuf_tensor("sb_onesb", [1, 128], BF16))
    s_bfc = e(nc.sbuf_tensor("bfccol", [128, 32], F32))
    s_bpr = e(nc.sbuf_tensor("bprcol", [128, 8], F32))
    s_cm05 = e(nc.sbuf_tensor("cm05", [128, 1], F32))
    s_cp05 = e(nc.sbuf_tensor("cp05", [128, 1], F32))
    s_scfc = e(nc.sbuf_tensor("scfc", [128, 1], F32))
    s_rsfc = e(nc.sbuf_tensor("rsfc", [128, 1], F32))
    s_scpr = e(nc.sbuf_tensor("scpr", [128, 1], F32))
    s_rspr = e(nc.sbuf_tensor("rspr", [128, 1], F32))
    s_am = e(nc.sbuf_tensor("amcols", [128, 32], F32))
    s_amc = e(nc.sbuf_tensor("amc", [128, 1], F32))
    s_amr = e(nc.sbuf_tensor("amr", [1, 128], F32))
    s_sc1 = e(nc.sbuf_tensor("sc1", [1, 1], F32))

    s_wall = e(nc.sbuf_tensor("wall", [128, DC * 3 * D], BF16))

    s_a2fc = e(nc.sbuf_tensor("a2fc", [16, D], BF16))
    s_btfc = e(nc.sbuf_tensor("btfc", [16, FFH], BF16))
    s_a2pr = e(nc.sbuf_tensor("a2pr", [16, FFH], BF16))
    s_btpr = e(nc.sbuf_tensor("btpr", [16, D], BF16))

    P = Prog(cnt)

    def dma(prog, eng, out_ap, in_ap, sem, slow=False):
        if slow:
            return prog.emit(eng, lambda E, o=out_ap, i=in_ap: E.dma_start(out=o, in_=i, allow_slow_non_contiguous=True), inc=sem, n=16)
        return prog.emit(eng, lambda E, o=out_ap, i=in_ap: E.dma_start(out=o, in_=i), inc=sem, n=16)

    def mm(prog, out_ap, lhsT, rhs, start, stop, tp=None, waits=()):
        def f(E, o=out_ap, l=lhsT, r=rhs):
            return E.matmul(o, l, r, start=start, stop=stop, tile_position=tp,
                            skip_group_check=True)
        return prog.emit("tensor", f, waits=waits, inc="T")

    def trans(prog, out_ap, in_ap, ident, start=True, stop=True, waits=()):
        def f(E, o=out_ap, i=in_ap, idn=ident):
            return E.matmul(o, i, idn, start=start, stop=stop, is_transpose=True,
                            skip_group_check=True)
        return prog.emit("tensor", f, waits=waits, inc="T")

    with ExitStack() as arena:
        a = arena.enter_context
        t_stage = a(nc.sbuf_tensor("stage", [128, 2 * FFH], F32))
        t_q = a(nc.sbuf_tensor("tq", [128, 2048], F32))
        t_k0 = a(nc.sbuf_tensor("tk0", [128, 2048], I32))
        t_k0f = a(nc.sbuf_tensor("tk0f", [128, 2048], F32))
        t_d = a(nc.sbuf_tensor("td", [128, 2048], F32))
        t_s1 = a(nc.sbuf_tensor("ts1", [128, 2048], F32))
        t_s2 = a(nc.sbuf_tensor("ts2", [128, 2048], F32))
        t_wqf = a(nc.sbuf_tensor("twqf", [128, FFH], F32))
        t_panel = a(nc.sbuf_tensor("tpanel", [128, FFH], BF16))
        t_small = a(nc.sbuf_tensor("tsmall", [128, 16], F32))
        t_arow = a(nc.sbuf_tensor("tarow", [16, FFH], F32))

        tok_dx = 0
        tok_dx = dma(P, "sync", s_identf[:], d_identf, "dc")
        tok_dx = dma(P, "sync", s_mask[:], d_mask, "dc")
        tok_dx = dma(P, "sync", s_mask0[:], d_mask0, "dc")
        tok_dx = dma(P, "sync", s_onesf[:], d_onesf, "dc")
        tok_dx = dma(P, "sync", s_onesb[:], d_onesb, "dc")
        tok_dx = dma(P, "sync", s_gcol[:], d_gamma.rearrange("(c p) -> p c", p=128), "dc", slow=True)
        tok_dx = dma(P, "sync", t_small[:, 0:8], d_beta.rearrange("(c p) -> p c", p=128), "dc", slow=True)
        tok_dx = dma(P, "sync", s_bfc[:], d_bfc.rearrange("(c p) -> p c", p=128), "dc", slow=True)
        tok_dx = dma(P, "sync", s_bpr[:], d_bpr.rearrange("(c p) -> p c", p=128), "dc", slow=True)
        tok_consts = tok_dx
        P.emit("vector", lambda E: E.memset(s_cm05[:], -0.5))
        P.emit("vector", lambda E: E.memset(s_cp05[:], 0.5))
        P.emit("vector", lambda E: E.tensor_scalar_mul(s_g8col[:], s_gcol[:], 0.125),
               waits=[("dc", tok_consts)])
        tv = P.emit("vector", lambda E: E.tensor_copy(out=s_betab[:], in_=t_small[:, 0:8]),
                    waits=[("dc", tok_consts)], inc="V")

        cast_toks = []
        for dc in range(DC):
            w = [("S", cast_toks[dc - 2])] if dc >= 2 else []
            sem = "da" if dc % 2 == 0 else "db"
            tok_l = P.emit("sync",
                           lambda E, dc=dc: E.dma_start(out=t_stage[:, (dc % 2) * 3072:(dc % 2) * 3072 + 3072],
                                                        in_=d_wqkv[dc * 128:(dc + 1) * 128, :]),
                           waits=w, inc=sem, n=16)
            tok = P.emit("scalar",
                         lambda E, dc=dc: E.copy(out=s_wall[:, dc * 3072:(dc + 1) * 3072],
                                                 in_=t_stage[:, (dc % 2) * 3072:(dc % 2) * 3072 + 3072]),
                         waits=[(sem, tok_l)], inc="S")
            cast_toks.append(tok)

        P.fence()
        for oc in range(16):
            for dc in range(DC):
                mm(P, pb[0][:, oc:oc + 1],
                   s_wall[:, dc * 3072 + oc * 128: dc * 3072 + (oc + 1) * 128],
                   s_betab[:, dc:dc + 1],
                   start=(dc == 0), stop=(dc == DC - 1),
                   waits=[("S", cast_toks[-1]), ("V", tv)])
        tok_T = P.emit("tensor", lambda E: E.nop(), inc="T")
        for half in range(2):
            for dc in range(DC):
                mm(P, pb[1 + half][0:1, 0:512],
                   s_betab[:, dc:dc + 1],
                   s_wall[:, dc * 3072 + 2048 + half * 512: dc * 3072 + 2048 + (half + 1) * 512],
                   start=(dc == 0), stop=(dc == DC - 1))
        tok_T = P.emit("tensor", lambda E: E.nop(), inc="T")
        tok_bw = P.emit("vector", lambda E: E.tensor_copy(out=s_bw[:], in_=pb[0][:, 0:16]),
                        waits=[("T", tok_T)], inc="V")
        tok_bw = P.emit("vector", lambda E: E.tensor_scalar_mul(s_bw[:, 0:8], s_bw[:, 0:8], 0.125),
                        inc="V")
        for half in range(2):
            tok_bwv = P.emit("vector",
                             lambda E, half=half: E.tensor_copy(out=s_bwv[0:1, half * 512:(half + 1) * 512],
                                                                in_=pb[1 + half][0:1, 0:512]),
                             waits=[("T", tok_T)], inc="V")
        fold_toks = []
        for dc in range(DC):
            P.emit("scalar",
                   lambda E, dc=dc: E.activation(out=s_wall[:, dc * 3072:dc * 3072 + 1024],
                                                 in_=s_wall[:, dc * 3072:dc * 3072 + 1024],
                                                 func=AF.Copy, scale=s_g8col[:, dc:dc + 1]),
                   waits=[("V", tok_bwv)])
            tok = P.emit("scalar",
                         lambda E, dc=dc: E.activation(out=s_wall[:, dc * 3072 + 1024:dc * 3072 + 3072],
                                                       in_=s_wall[:, dc * 3072 + 1024:dc * 3072 + 3072],
                                                       func=AF.Copy, scale=s_gcol[:, dc:dc + 1]),
                         inc="S")
            fold_toks.append(tok)

        P.fence()
        st_toks = [0, 0]
        cast_tk = [0, 0]
        for ic in range(DC):
            sem = "da" if ic % 2 == 0 else "db"
            ssem = "dwa" if ic % 2 == 0 else "dwb"
            tok_l = P.emit("sync",
                           lambda E, ic=ic: E.dma_start(out=t_stage[:, (ic % 2) * 4096:(ic % 2) * 4096 + 1024],
                                                        in_=d_wout[ic * 128:(ic + 1) * 128, :]),
                           waits=[("V", cast_tk[ic % 2])], inc=sem, n=16)
            tok_c = P.emit("vector",
                           lambda E, ic=ic: E.tensor_copy(out=t_panel[:, (ic % 2) * 1024:(ic % 2) * 1024 + 1024],
                                                          in_=t_stage[:, (ic % 2) * 4096:(ic % 2) * 4096 + 1024]),
                           waits=[(sem, tok_l), (ssem, st_toks[ic % 2])], inc="V")
            cast_tk[ic % 2] = tok_c
            st_toks[ic % 2] = P.emit(
                "gpsimd",
                lambda E, ic=ic: E.dma_start(out=d_woutb[ic * 128:(ic + 1) * 128, :],
                                             in_=t_panel[:, (ic % 2) * 1024:(ic % 2) * 1024 + 1024]),
                waits=[("V", tok_c)], inc=ssem, n=16)
        P.fence()

        tok_a = dma(P, "sync", t_arow[:, 0:1024], d_afc, "da")
        tva = P.emit("vector", lambda E: E.tensor_scalar(out=s_a2fc[:], in0=t_arow[:16, 0:1024],
                                                         scalar1=2.0, scalar2=None, op0=ALU.mult),
                     waits=[("da", tok_a)], inc="V")
        tok_a = P.emit("sync", lambda E: E.dma_start(out=t_arow[:], in_=d_apr),
                       waits=[("V", tva)], inc="da", n=16)
        tva = P.emit("vector", lambda E: E.tensor_scalar(out=s_a2pr[:], in0=t_arow[:16, :],
                                                         scalar1=2.0, scalar2=None, op0=ALU.mult),
                     waits=[("da", tok_a)], inc="V")
        P.fence()
        btr = [0, 0]
        bev = [0, 0]
        for c in range(HC):
            sem = "da" if c % 2 == 0 else "db"
            tok_l = P.emit("sync",
                           lambda E, c=c: E.dma_start(out=t_stage[:, (c % 2) * 4096:(c % 2) * 4096 + 16],
                                                      in_=d_bfcB[c * 128:(c + 1) * 128, :]),
                           waits=[("T", btr[c % 2])], inc=sem, n=16)
            tt = trans(P, pb[2 + c % 2][0:16, 0:128],
                       t_stage[:, (c % 2) * 4096:(c % 2) * 4096 + 16], s_identf[:],
                       waits=[(sem, tok_l), ("S", bev[c % 2])])
            btr[c % 2] = tt
            bev[c % 2] = P.emit("scalar",
                                lambda E, c=c: E.copy(out=s_btfc[:16, c * 128:(c + 1) * 128],
                                                      in_=pb[2 + c % 2][0:16, 0:128]),
                                waits=[("T", tt)], inc="S")
        P.fence()
        btr = [0, 0]
        bev = [0, 0]
        for c in range(DC):
            sem = "da" if c % 2 == 0 else "db"
            tok_l = P.emit("sync",
                           lambda E, c=c: E.dma_start(out=t_stage[:, (c % 2) * 4096:(c % 2) * 4096 + 16],
                                                      in_=d_bprB[c * 128:(c + 1) * 128, :]),
                           waits=[("T", btr[c % 2])], inc=sem, n=16)
            tt = trans(P, pb[2 + c % 2][0:16, 0:128],
                       t_stage[:, (c % 2) * 4096:(c % 2) * 4096 + 16], s_identf[:],
                       waits=[(sem, tok_l), ("S", bev[c % 2])])
            btr[c % 2] = tt
            bev[c % 2] = P.emit("scalar",
                                lambda E, c=c: E.copy(out=s_btpr[:16, c * 128:(c + 1) * 128],
                                                      in_=pb[2 + c % 2][0:16, 0:128]),
                                waits=[("T", tt)], inc="S")
        P.fence()

        def quant_weight(d_w, nchunk, free, s_scale, s_rscale, am_ncols, d_scrT, a2, bt,
                         panel_cols, tpos_cols):
            P.fence()
            red_tk = [0, 0]
            for c in range(nchunk):
                sl = (c % 2) * 4096
                sem = "da" if c % 2 == 0 else "db"
                tok_l = P.emit("sync",
                               lambda E, c=c, sl=sl: E.dma_start(out=t_stage[:, sl:sl + free],
                                                                 in_=d_w[c * 128:(c + 1) * 128, :]),
                               waits=[("V", red_tk[c % 2])], inc=sem, n=16)
                red_tk[c % 2] = P.emit("vector",
                                       lambda E, c=c, sl=sl: E.reduce_max(out=s_am[:, c:c + 1],
                                                                          in_=t_stage[:, sl:sl + free],
                                                                          axis=AX.X, apply_absolute_value=True),
                                       waits=[(sem, tok_l)], inc="V")
            P.fence()
            tok = P.emit("vector", lambda E: E.reduce_max(out=s_amc[:], in_=s_am[:, 0:am_ncols], axis=AX.X),
                         inc="V")
            tt = trans(P, pb[3][0:1, 0:128], s_amc[:], s_identf[:], waits=[("V", tok)])
            tok = P.emit("vector", lambda E: E.tensor_copy(out=s_amr[:], in_=pb[3][0:1, 0:128]),
                         waits=[("T", tt)], inc="V")
            tok = P.emit("vector", lambda E: E.reduce_max(out=s_sc1[:], in_=s_amr[:], axis=AX.X), inc="V")
            tok = P.emit("vector", lambda E: E.tensor_scalar_mul(s_sc1[:], s_sc1[:], float(np.float32(1.0) / np.float32(127.0))),
                         inc="V")
            tt = mm(P, pb[3][:, 0:1], s_onesf[:], s_sc1[:], start=True, stop=True,
                    waits=[("V", tok)])
            tok = P.emit("vector", lambda E: E.tensor_copy(out=s_scale[:], in_=pb[3][:, 0:1]),
                         waits=[("T", tt)], inc="V")
            tok = P.emit("vector", lambda E: E.reciprocal(out=s_rscale[:], in_=s_scale[:]), inc="V")
            P.fence()
            nsl = free // 1024
            tok_store = 0
            ev_bank = {}
            trans_last = 0
            tq_tk = [0, 0]
            gsl = [0]
            stt_tk = [0, 0]
            for c in range(nchunk):
                sl = (c % 2) * 4096
                sem = "da" if c % 2 == 0 else "db"
                tok_l = P.emit("sync",
                               lambda E, c=c, sl=sl: E.dma_start(out=t_stage[:, sl:sl + free],
                                                                 in_=d_w[c * 128:(c + 1) * 128, :]),
                               waits=[("S", tq_tk[c % 2])], inc=sem, n=16)
                for s in range(nsl):
                    slot = gsl[0] % 2
                    q = slot * 1024
                    gsl[0] += 1
                    src = t_stage[:, sl + s * 1024: sl + (s + 1) * 1024]
                    tq = P.emit("scalar",
                                lambda E, src=src, q=q: E.activation(out=t_q[:, q:q + 1024], in_=src,
                                                                     func=AF.Copy, scale=s_rscale[:]),
                                waits=[(sem, tok_l), ("V", stt_tk[slot])], inc="S")
                    tk = P.emit("vector",
                                lambda E, q=q: E.tensor_copy(out=t_k0[:, q:q + 1024], in_=t_q[:, q:q + 1024]),
                                waits=[("S", tq)], inc="V")
                    tk = P.emit("vector",
                                lambda E, q=q: E.tensor_copy(out=t_k0f[:, q:q + 1024], in_=t_k0[:, q:q + 1024]),
                                inc="V")
                    tk = P.emit("vector",
                                lambda E, q=q: E.tensor_tensor(out=t_d[:, q:q + 1024], in0=t_q[:, q:q + 1024],
                                                               in1=t_k0f[:, q:q + 1024], op=ALU.subtract),
                                inc="V")
                    t1 = P.emit("scalar",
                                lambda E, q=q: E.activation(out=t_s1[:, q:q + 1024], in_=t_d[:, q:q + 1024],
                                                            func=AF.Sign, bias=s_cm05[:]),
                                waits=[("V", tk)], inc="S")
                    t1 = P.emit("scalar",
                                lambda E, q=q: E.activation(out=t_s2[:, q:q + 1024], in_=t_d[:, q:q + 1024],
                                                            func=AF.Sign, bias=s_cp05[:]), inc="S")
                    tk = P.emit("vector",
                                lambda E, q=q: E.tensor_tensor(out=t_s1[:, q:q + 1024], in0=t_s1[:, q:q + 1024],
                                                               in1=t_s2[:, q:q + 1024], op=ALU.add),
                                waits=[("S", t1)], inc="V")
                    tk = P.emit("vector",
                                lambda E, q=q: E.scalar_tensor_tensor(out=t_d[:, q:q + 1024],
                                                                      in0=t_s1[:, q:q + 1024], scalar=0.5,
                                                                      in1=t_k0f[:, q:q + 1024],
                                                                      op0=ALU.mult, op1=ALU.add),
                                inc="V")
                    stt_tk[slot] = tk
                    tw = P.emit("scalar",
                                lambda E, s=s, q=q: E.activation(out=t_wqf[:, s * 1024:(s + 1) * 1024],
                                                                 in_=t_d[:, q:q + 1024],
                                                                 func=AF.Copy, scale=s_scale[:]),
                                waits=[("V", tk), ("T", trans_last)], inc="S")
                nblk = free // 128
                ev_tok = None
                for b in range(nblk):
                    bi = 4 + ((b // 4) % 2)
                    bank = pb[bi]
                    tt = trans(P, bank[:, (b % 4) * 128:(b % 4) * 128 + 128],
                               t_wqf[:, b * 128:(b + 1) * 128], s_identf[:],
                               start=True, stop=False,
                               waits=[("S", tw), ("S", ev_bank.get(bi, 0)),
                                      ("dwa", tok_store)])
                    tt = mm(P, bank[:, (b % 4) * 128:(b % 4) * 128 + 128],
                            a2[:16, b * 128:(b + 1) * 128],
                            bt[:16, c * 128:(c + 1) * 128],
                            start=False, stop=True)
                    if b % 4 == 3:
                        ev_tok = P.emit("scalar",
                                        lambda E, b=b, bank=bank: E.copy(out=t_panel[:, (b - 3) * 128:(b + 1) * 128],
                                                                         in_=bank[:]),
                                        waits=[("T", tt)], inc="S")
                        ev_bank[bi] = ev_tok
                trans_last = tt
                tq_tk[c % 2] = tq
                tok_store = P.emit("gpsimd",
                                   lambda E, c=c: E.dma_start(out=d_scrT[c * 128:(c + 1) * 128, :],
                                                              in_=t_panel[:, 0:free]),
                                   waits=[("S", ev_tok)], inc="dwa", n=16)
            P.fence()

        quant_weight(d_wfc, HC, 1024, s_scfc, s_rsfc, HC, d_wfcT, s_a2fc, s_btfc, 1024, None)
        quant_weight(d_wpr, DC, 4096, s_scpr, s_rspr, DC, d_wprT, s_a2pr, s_btpr, 4096, None)

        if dbg:
            P.fence()
            P.emit("sync", lambda E: E.nop(),
                   waits=[(k, cnt.get(k, 0)) for k in ("T", "V", "S")])
            dma(P, "sync", d_dbg["wall"], s_wall[:], "dc")
            dma(P, "sync", d_dbg["bw"], s_bw[:], "dc")
            dma(P, "sync", d_dbg["bwv"], s_bwv[:], "dc")
            P.emit("vector", lambda E: E.tensor_copy(out=s_am[:, 0:1], in_=s_scfc[:]))
            P.emit("vector", lambda E: E.tensor_copy(out=s_am[:, 1:2], in_=s_rsfc[:]))
            P.emit("vector", lambda E: E.tensor_copy(out=s_am[:, 2:3], in_=s_scpr[:]))
            tvd = P.emit("vector", lambda E: E.tensor_copy(out=s_am[:, 3:4], in_=s_rspr[:]), inc="V")
            dma(P, "sync", d_dbg["scl"], s_am[:, 0:8], "dc")
            dma(P, "sync", d_dbg["btfc"], s_btfc[:], "dc")
            dma(P, "sync", d_dbg["a2fc"], s_a2fc[:], "dc")
            dma(P, "sync", d_dbg["wfcT0"], d_wfcT[0:128, :], "dc")
            dma(P, "sync", d_dbg["wprT0"], d_wprT[0:128, :], "dc")
        for eng in ENGS:
            P.emit(eng, lambda E: E.nop(), inc="BAR")

        with nc.Block("setup") as b1:
            P.replay(b1, sems)

    s_x = e(nc.sbuf_tensor("xsb", [128, TILE // 128 * D], F32))
    s_xnb = e(nc.sbuf_tensor("xnb", [128, D], F32))
    s_xnfm = e(nc.sbuf_tensor("xnfm", [128, DC * TILE], BF16))
    s_q = e(nc.sbuf_tensor("qsb", [128, 8 * TILE], BF16))
    s_k = e(nc.sbuf_tensor("ksb", [128, 8 * TILE], BF16))
    s_v = e(nc.sbuf_tensor("vsb", [128, G * D], BF16))
    s_kprev = e(nc.sbuf_tensor("kprev", [128, 8 * 128], BF16))
    s_vprev = e(nc.sbuf_tensor("vprev", [128, D], BF16))
    s_ao = e(nc.sbuf_tensor("aosb", [128, 8 * TILE], BF16))
    s_h = e(nc.sbuf_tensor("hsb", [128, HC * TILE], BF16))
    s_sm = e(nc.sbuf_tensor("smbuf", [128, 4 * 256], F32))
    s_probs = e(nc.sbuf_tensor("probs", [128, 4 * 256], F32))
    s_pT = e(nc.sbuf_tensor("pT", [128, 4 * 256], BF16))
    s_stat = e(nc.sbuf_tensor("stat", [128, 64], F32))
    s_ofm = e(nc.sbuf_tensor("ofm", [128, 512], F32))
    s_wfcb = e(nc.sbuf_tensor("wfcb", [128, 2 * 1024], BF16))
    s_wprb = e(nc.sbuf_tensor("wprb", [128, 2 * 4096], BF16))
    s_woutb = e(nc.sbuf_tensor("woutb", [128, 2 * 1024], BF16))

    NEGMAX, ROWSUM, RECIP = 0, 4, 52
    SUM, SSQ, MU, MSQ, VARP, R1, V2P, ASC, NMA, T0 = 8, 12, 16, 20, 24, 28, 32, 36, 40, 44

    M = Prog(cnt)
    M.fence_snap = {}
    for eng in ENGS:
        M.emit(eng, lambda E: E.nop(), waits=[("BAR", cnt["BAR"])])

    tok_store = 0
    wout_cons = [0, 0]
    wfc_cons = [0, 0]
    wpr_cons = [0, 0]
    ofm_cons = [0, 0]
    ofm_trans_last = [0]
    xnb_cons = [0, 0]
    xnt_prev_ev = [0, 0]

    def emit_norm_transpose(M, a_col, nma_col):
        ev_last = None
        for g in range(G):
            sl = 0
            tok_x = M.emit("vector",
                           lambda E, g=g, sl=sl, a_col=a_col, nma_col=nma_col:
                           E.tensor_scalar(out=s_xnb[:, sl * D:(sl + 1) * D],
                                           in0=s_x[:, g * D:(g + 1) * D],
                                           scalar1=s_stat[:, a_col + g:a_col + g + 1],
                                           scalar2=s_stat[:, nma_col + g:nma_col + g + 1],
                                           op0=ALU.mult, op1=ALU.add),
                           waits=[("T", xnb_cons[sl])], inc="V")
            tt = None
            for half in range(2):
                bank = pb[half]
                for i in range(4):
                    dc = half * 4 + i
                    tt = trans(M, bank[:, i * 128:(i + 1) * 128],
                               s_xnb[:, sl * D + dc * 128: sl * D + (dc + 1) * 128],
                               s_identf[:],
                               waits=[("V", tok_x), ("S", xnt_prev_ev[half])])
                dst = bass.AP(s_xnfm, half * 4 * TILE + g * 128,
                              [[DC * TILE, 128], [TILE, 4], [1, 128]])
                tok = M.emit("scalar",
                             lambda E, bank=bank, dst=dst:
                             E.copy(out=dst, in_=bank[:].rearrange("p (a b) -> p a b", b=128)),
                             waits=[("T", tt)], inc="S")
                xnt_prev_ev[half] = tok
                ev_last = tok
            xnb_cons[sl] = tt
        return ev_last

    for t in range(NT):
        seq_start = (t % SEQ_TILES == 0)
        tok_x = dma(M, "sync", s_x[:].rearrange("p (g d) -> p g d", d=D),
                    d_x[t * TILE:(t + 1) * TILE, :].rearrange("(g p) d -> p g d", p=128),
                    "dx")

        def stats_and_norm(wait_tok, wait_sem, r_out_cols):
            tv = ts = None
            for g in range(G):
                xg = s_x[:, g * D:(g + 1) * D]
                tv = M.emit("vector",
                            lambda E, xg=xg, g=g: E.tensor_scalar(out=s_xnfm[:, 0:1024], in0=xg, scalar1=1.0,
                                                                  scalar2=None, op0=ALU.mult, op1=ALU.add,
                                                                  accum_out=s_stat[:, SUM + g:SUM + g + 1]),
                            waits=[(wait_sem, wait_tok)], inc="V")
                ts = M.emit("scalar",
                            lambda E, xg=xg, g=g: E.activation(out=s_xnfm[:, 1024:2048], in_=xg, func=AF.Square,
                                                               accum_out=s_stat[:, SSQ + g:SSQ + g + 1]),
                            waits=[(wait_sem, wait_tok)], inc="S")
            c = 1.0 / 1024.0
            tv = M.emit("vector", lambda E: E.tensor_scalar_mul(s_stat[:, MU:MU + 4], s_stat[:, SUM:SUM + 4], c),
                        waits=[("V", tv)], inc="V")
            tv = M.emit("vector", lambda E: E.tensor_scalar_mul(s_stat[:, MSQ:MSQ + 4], s_stat[:, SSQ:SSQ + 4], c),
                        waits=[("S", ts)], inc="V")
            tv = M.emit("vector", lambda E: E.tensor_tensor(out=s_stat[:, T0:T0 + 4], in0=s_stat[:, MU:MU + 4],
                                                            in1=s_stat[:, MU:MU + 4], op=ALU.mult), inc="V")
            tv = M.emit("vector", lambda E: E.tensor_tensor(out=s_stat[:, VARP:VARP + 4], in0=s_stat[:, MSQ:MSQ + 4],
                                                            in1=s_stat[:, T0:T0 + 4], op=ALU.subtract), inc="V")
            tv = M.emit("vector", lambda E: E.tensor_scalar_add(s_stat[:, VARP:VARP + 4], s_stat[:, VARP:VARP + 4],
                                                                1e-5), inc="V")
            ts = M.emit("scalar", lambda E: E.activation(out=s_stat[:, T0:T0 + 4], in_=s_stat[:, VARP:VARP + 4],
                                                         func=AF.Ln),
                        waits=[("V", tv)], inc="S")
            ts = M.emit("scalar", lambda E: E.activation(out=s_stat[:, R1:R1 + 4], in_=s_stat[:, T0:T0 + 4],
                                                         func=AF.Exp, scale=s_cm05[:]), inc="S")
            return tv, ts

        tv, ts = stats_and_norm(tok_x, "dx", R1)
        T1 = 48
        tv = M.emit("vector", lambda E: E.reciprocal(out=s_stat[:, T1:T1 + 4], in_=s_stat[:, VARP:VARP + 4]),
                    inc="V")
        tv = M.emit("vector", lambda E: E.tensor_scalar_add(s_stat[:, V2P:V2P + 4], s_stat[:, VARP:VARP + 4], -1e-5),
                    inc="V")
        tv = M.emit("vector", lambda E: E.tensor_tensor(out=s_stat[:, V2P:V2P + 4], in0=s_stat[:, V2P:V2P + 4],
                                                        in1=s_stat[:, T1:T1 + 4], op=ALU.mult), inc="V")
        tv = M.emit("vector", lambda E: E.tensor_scalar_add(s_stat[:, V2P:V2P + 4], s_stat[:, V2P:V2P + 4], 1e-5),
                    inc="V")
        ts = M.emit("scalar", lambda E: E.activation(out=s_stat[:, T1:T1 + 4], in_=s_stat[:, V2P:V2P + 4],
                                                     func=AF.Ln),
                    waits=[("V", tv), ("S", ts)], inc="S")
        ts = M.emit("scalar", lambda E: E.activation(out=s_stat[:, V2P:V2P + 4], in_=s_stat[:, T1:T1 + 4],
                                                     func=AF.Exp, scale=s_cm05[:]), inc="S")
        tv = M.emit("vector", lambda E: E.tensor_tensor(out=s_stat[:, ASC:ASC + 4], in0=s_stat[:, R1:R1 + 4],
                                                        in1=s_stat[:, V2P:V2P + 4], op=ALU.mult),
                    waits=[("S", ts)], inc="V")
        tv = M.emit("vector", lambda E: E.scalar_tensor_tensor(out=s_stat[:, NMA:NMA + 4], in0=s_stat[:, MU:MU + 4],
                                                               scalar=-1.0, in1=s_stat[:, ASC:ASC + 4],
                                                               op0=ALU.mult, op1=ALU.mult), inc="V")
        tok_xnfm = emit_norm_transpose(M, ASC, NMA)

        if dbg and t == 0:
            M.fence()
            M.emit("sync", lambda E: E.nop(), waits=[(k, cnt.get(k, 0)) for k in ("T", "V", "S")])
            dma(M, "sync", d_dbg["stat"], s_stat[:], "dc")
            dma(M, "sync", d_dbg["xnfm"], s_xnfm[:], "dc")
            for _e in ("vector", "scalar", "tensor", "gpsimd"):
                M.emit(_e, lambda E: E.nop(), waits=[("dc", cnt["dc"])])
        M.fence()
        qk_ev = []
        for oc in range(16):
            bank = pb[2 + (oc % 2)]
            tt = None
            for dc in range(DC):
                tt = mm(M, bank[:],
                        s_wall[:, dc * 3072 + oc * 128: dc * 3072 + (oc + 1) * 128],
                        s_xnfm[:, dc * TILE:(dc + 1) * TILE],
                        start=(dc == 0), stop=(dc == DC - 1),
                        waits=[("S", qk_ev[oc - 2] if oc >= 2 else 0)])
            dst = s_q if oc < 8 else s_k
            fc = oc % 8
            tok = M.emit("scalar",
                         lambda E, dst=dst, fc=fc, oc=oc, bank=bank:
                         E.activation(out=dst[:, fc * TILE:(fc + 1) * TILE], in_=bank[:],
                                      func=AF.Identity, bias=s_bw[:, oc:oc + 1]),
                         waits=[("T", tt)], inc="S")
            qk_ev.append(tok)
        v_ev = []
        for g in range(G):
            bh = [pb[4 + (g % 2) * 2], pb[5 + (g % 2) * 2]]
            tt = None
            for half in range(2):
                for dc in range(DC):
                    tt = mm(M, bh[half][:],
                            s_xnfm[:, dc * TILE + g * 128: dc * TILE + (g + 1) * 128],
                            s_wall[:, dc * 3072 + 2048 + half * 512: dc * 3072 + 2048 + (half + 1) * 512],
                            start=(dc == 0), stop=False,
                            waits=[("V", v_ev[g - 2] if g >= 2 else 0)])
                tt = mm(M, bh[half][:], s_onesb[:], s_bwv[0:1, half * 512:(half + 1) * 512],
                        start=False, stop=True)
            tok = None
            for half in range(2):
                tok = M.emit("vector",
                             lambda E, g=g, half=half, b=bh[half]:
                             E.tensor_copy(out=s_v[:, g * D + half * 512: g * D + (half + 1) * 512], in_=b[:]),
                             waits=[("T", tt)], inc="V")
            v_ev.append(tok)

        if dbg and t == 0:
            M.fence()
            M.emit("sync", lambda E: E.nop(), waits=[(k, cnt.get(k, 0)) for k in ("T", "V", "S")])
            dma(M, "sync", d_dbg["qd"], s_q[:], "dc")
            dma(M, "sync", d_dbg["kd"], s_k[:], "dc")
            dma(M, "sync", d_dbg["vd"], s_v[:], "dc")
            for _e in ("vector", "scalar", "tensor", "gpsimd"):
                M.emit(_e, lambda E: E.nop(), waits=[("dc", cnt["dc"])])
        M.fence()
        NI = G * H
        ctx = []
        for w in range(G):
            fw = (seq_start and w == 0)
            for h in range(H):
                ctx.append((w, h, fw))
        tokA = [0] * NI; tokB = [0] * NI; tokC = [0] * NI; tokD = [0] * NI
        tokE = [0] * NI; tokF = [0] * NI; tokG = [0] * NI; tokH = [0] * NI
        engH = [0] * NI

        def stA(i):
            w, h, fw = ctx[i]
            po = 64 * (h % 2); fc = h // 2
            qs = s_q[po:po + 64, fc * TILE + w * 128: fc * TILE + (w + 1) * 128]
            bankS = pb[i % 4]
            wts = [("S", qk_ev[8 + fc]), ("V", tokB[i - 4] if i >= 4 else 0)]
            if fw:
                tt = mm(M, bankS[:, 0:128], qs, s_k[po:po + 64, fc * TILE: fc * TILE + 128],
                        start=True, stop=True, tp=(po, 0), waits=wts)
            elif w == 0:
                tt = mm(M, bankS[:, 0:128], qs, s_kprev[po:po + 64, fc * 128:(fc + 1) * 128],
                        start=True, stop=True, tp=(po, 0), waits=wts)
                tt = mm(M, bankS[:, 128:256], qs, s_k[po:po + 64, fc * TILE: fc * TILE + 128],
                        start=True, stop=True, tp=(po, 0))
            else:
                tt = mm(M, bankS[:, 0:256], qs,
                        s_k[po:po + 64, fc * TILE + (w - 1) * 128: fc * TILE + (w + 1) * 128],
                        start=True, stop=True, tp=(po, 0), waits=wts)
            tokA[i] = tt

        def stB(i):
            w, h, fw = ctx[i]
            nk = 128 if fw else 256
            wh = i % 4
            msk = s_mask0 if fw else s_mask
            bankS = pb[wh]
            M.emit("vector",
                   lambda E, wh=wh, nk=nk, msk=msk, bankS=bankS:
                   E.tensor_tensor(out=s_sm[:, wh * 256: wh * 256 + nk], in0=bankS[:, 0:nk],
                                   in1=msk[:, 0:nk], op=ALU.add),
                   waits=[("T", tokA[i]), ("S", tokC[i - 4] if i >= 4 else 0)], inc="V")
            tokB[i] = M.emit("vector",
                             lambda E, wh=wh, nk=nk: E.reduce_max(out=s_stat[:, NEGMAX + wh:NEGMAX + wh + 1],
                                                                  in_=s_sm[:, wh * 256: wh * 256 + nk],
                                                                  axis=AX.X, negate=True),
                             inc="V")

        def stC(i):
            w, h, fw = ctx[i]
            nk = 128 if fw else 256
            wh = i % 4
            tokC[i] = M.emit("scalar",
                             lambda E, wh=wh, nk=nk: E.activation(out=s_probs[:, wh * 256: wh * 256 + nk],
                                                                  in_=s_sm[:, wh * 256: wh * 256 + nk],
                                                                  func=AF.Exp,
                                                                  bias=s_stat[:, NEGMAX + wh:NEGMAX + wh + 1],
                                                                  accum_out=s_stat[:, ROWSUM + wh:ROWSUM + wh + 1]),
                             waits=[("V", tokB[i]), ("T", tokE[i - 4] if i >= 4 else 0)], inc="S")

        def stD(i):
            w, h, fw = ctx[i]
            nk = 128 if fw else 256
            wh = i % 4
            M.emit("vector",
                   lambda E, wh=wh: E.reciprocal(out=s_stat[:, RECIP + wh:RECIP + wh + 1],
                                                 in_=s_stat[:, ROWSUM + wh:ROWSUM + wh + 1]),
                   waits=[("S", tokC[i])], inc="V")
            tokD[i] = M.emit("vector",
                             lambda E, wh=wh, nk=nk: E.tensor_scalar_mul(s_probs[:, wh * 256: wh * 256 + nk],
                                                                         s_probs[:, wh * 256: wh * 256 + nk],
                                                                         s_stat[:, RECIP + wh:RECIP + wh + 1]),
                             inc="V")

        def stE(i):
            w, h, fw = ctx[i]
            nk = 128 if fw else 256
            wh = i % 4
            bankT = pb[4 + i % 2]
            tt2 = None
            for kc in range(nk // 128):
                tt2 = trans(M, bankT[:, kc * 128:(kc + 1) * 128],
                            s_probs[:, wh * 256 + kc * 128: wh * 256 + (kc + 1) * 128],
                            s_identf[:],
                            waits=[("V", tokD[i]), ("S", tokF[i - 2] if i >= 2 else 0)])
            tokE[i] = tt2

        def stF(i):
            w, h, fw = ctx[i]
            nk = 128 if fw else 256
            wh = i % 4
            bankT = pb[4 + i % 2]
            tokF[i] = M.emit("scalar",
                             lambda E, wh=wh, nk=nk, bankT=bankT:
                             E.copy(out=s_pT[:, wh * 256: wh * 256 + nk], in_=bankT[:, 0:nk]),
                             waits=[("T", tokE[i])], inc="S")

        def stG(i):
            w, h, fw = ctx[i]
            wh = i % 4
            po = 64 * (h % 2)
            bankO = pb[6 + i % 2]
            pvw = [("S", tokF[i]), ("V", v_ev[w])]
            if i >= 2:
                pvw.append((("V", "S")[engH[i - 2]], tokH[i - 2]))
            if fw:
                tt3 = mm(M, bankO[po:po + 64, 0:128],
                         s_v[:, w * D + h * DH: w * D + (h + 1) * DH],
                         s_pT[:, wh * 256: wh * 256 + 128],
                         start=True, stop=True, tp=(0, po), waits=pvw)
            else:
                vprev_ap = (s_vprev[:, h * DH:(h + 1) * DH] if w == 0
                            else s_v[:, (w - 1) * D + h * DH: (w - 1) * D + h * DH + DH])
                tt3 = mm(M, bankO[po:po + 64, 0:128],
                         vprev_ap, s_pT[:, wh * 256: wh * 256 + 128],
                         start=True, stop=False, tp=(0, po), waits=pvw)
                tt3 = mm(M, bankO[po:po + 64, 0:128],
                         s_v[:, w * D + h * DH: w * D + (h + 1) * DH],
                         s_pT[:, wh * 256 + 128: wh * 256 + 256],
                         start=False, stop=True, tp=(0, po))
            tokG[i] = tt3

        def stH(i):
            w, h, fw = ctx[i]
            po = 64 * (h % 2); fc = h // 2
            bankO = pb[6 + i % 2]
            if i % 2 == 0:
                tokH[i] = M.emit("vector",
                                 lambda E, po=po, fc=fc, w=w, bankO=bankO:
                                 E.tensor_copy(out=s_ao[po:po + 64, fc * TILE + w * 128: fc * TILE + (w + 1) * 128],
                                               in_=bankO[po:po + 64, 0:128]),
                                 waits=[("T", tokG[i])], inc="V")
                engH[i] = 0
            else:
                tokH[i] = M.emit("scalar",
                                 lambda E, po=po, fc=fc, w=w, bankO=bankO:
                                 E.copy(out=s_ao[po:po + 64, fc * TILE + w * 128: fc * TILE + (w + 1) * 128],
                                        in_=bankO[po:po + 64, 0:128]),
                                 waits=[("T", tokG[i])], inc="S")
                engH[i] = 1

        stages = (stA, stB, stC, stD, stE, stF, stG, stH)
        for j in range(NI + len(stages) - 1):
            for k, st in enumerate(stages):
                i = j - k
                if 0 <= i < NI:
                    st(i)
        last_i = NI - 1
        pv_ev_last = M.emit("vector", lambda E: E.memset(s_stat[:, 60:61], 0.0),
                            waits=[(("V", "S")[engH[last_i]], tokH[last_i]),
                                   (("V", "S")[engH[last_i - 1]], tokH[last_i - 1])], inc="V")

        if (t + 1) % SEQ_TILES != 0:
            kp = M.emit("vector",
                        lambda E: E.tensor_copy(
                            out=s_kprev[:].rearrange("p (f c) -> p f c", c=128),
                            in_=s_k[:].rearrange("p (f t) -> p f t", t=TILE)[:, :, TILE - 128:TILE]),
                        waits=[("S", qk_ev[-1])], inc="V")
            vp = M.emit("vector",
                        lambda E: E.tensor_copy(out=s_vprev[:], in_=s_v[:, (G - 1) * D: G * D]),
                        waits=[("V", v_ev[-1])], inc="V")

        if dbg and t == 0:
            M.fence()
            M.emit("sync", lambda E: E.nop(), waits=[(k, cnt.get(k, 0)) for k in ("T", "V", "S")])
            dma(M, "sync", d_dbg["ao"], s_ao[:], "dc")
            for _e in ("vector", "scalar", "tensor", "gpsimd"):
                M.emit(_e, lambda E: E.nop(), waits=[("dc", cnt["dc"])])
        M.fence()
        res_toks = []
        for ic in range(DC):
            sl = ic % 2
            tok_w = M.emit("gpsimd",
                           lambda E, ic=ic, sl=sl: E.dma_start(out=s_woutb[:, sl * 1024:(sl + 1) * 1024],
                                                               in_=d_woutb[ic * 128:(ic + 1) * 128, :]),
                           waits=[("T", wout_cons[sl])], inc="wo%d" % sl, n=16)
            tt = None
            for g in range(G):
                for half in range(2):
                    tt = mm(M, pb[2 * g + half][:],
                            s_ao[:, ic * TILE + g * 128: ic * TILE + (g + 1) * 128],
                            s_woutb[:, sl * 1024 + half * 512: sl * 1024 + (half + 1) * 512],
                            start=(ic == 0), stop=(ic == DC - 1),
                            waits=[("wo%d" % sl, tok_w), ("V", pv_ev_last)])
            wout_cons[sl] = tt
        tok_T = M.emit("tensor", lambda E: E.nop(), inc="T")
        tvlast = None
        for g in range(G):
            for half in range(2):
                tvlast = M.emit("vector",
                                lambda E, g=g, half=half:
                                E.scalar_tensor_tensor(out=s_x[:, g * D + half * 512: g * D + (half + 1) * 512],
                                                       in0=pb[2 * g + half][:], scalar=1.0,
                                                       in1=s_x[:, g * D + half * 512: g * D + (half + 1) * 512],
                                                       op0=ALU.mult, op1=ALU.add),
                                waits=[("T", tok_T)], inc="V")

        if dbg and t == 0:
            M.fence()
            M.emit("sync", lambda E: E.nop(), waits=[(k, cnt.get(k, 0)) for k in ("T", "V", "S")])
            dma(M, "sync", d_dbg["x1"], s_x[:], "dc")
            for _e in ("vector", "scalar", "tensor", "gpsimd"):
                M.emit(_e, lambda E: E.nop(), waits=[("dc", cnt["dc"])])
        M.fence()
        tv, ts = stats_and_norm(tvlast, "V", R1)
        xnb_toks = []
        for g in range(G):
            tok = M.emit("vector",
                         lambda E, g=g: E.scalar_tensor_tensor(out=s_stat[:, NMA + g:NMA + g + 1],
                                                               in0=s_stat[:, MU + g:MU + g + 1], scalar=-1.0,
                                                               in1=s_stat[:, R1 + g:R1 + g + 1],
                                                               op0=ALU.mult, op1=ALU.mult),
                         waits=[("S", ts)], inc="V")
        tok_xnfm2 = emit_norm_transpose(M, R1, NMA)
        ev_tr = [tok_xnfm2]

        M.fence()
        h_toks = []
        for c in range(HC):
            sl = c % 2
            tok_w = M.emit("gpsimd",
                           lambda E, c=c, sl=sl: E.dma_start(out=s_wfcb[:, sl * 1024:(sl + 1) * 1024],
                                                             in_=d_wfcT[c * 128:(c + 1) * 128, :]),
                           waits=[("T", wfc_cons[sl])], inc="wf%d" % sl, n=16)
            bank = pb[2 + sl]
            tt = None
            for dc in range(DC):
                tt = mm(M, bank[:],
                        s_wfcb[:, sl * 1024 + dc * 128: sl * 1024 + (dc + 1) * 128],
                        s_xnfm[:, dc * TILE:(dc + 1) * TILE],
                        start=(dc == 0), stop=(dc == DC - 1),
                        waits=[("wf%d" % sl, tok_w), ("S", ev_tr[-1]),
                               ("S", h_toks[c - 2] if c >= 2 else 0)])
            wfc_cons[sl] = tt
            tok = M.emit("scalar",
                         lambda E, c=c, bank=bank: E.activation(out=s_h[:, c * TILE:(c + 1) * TILE],
                                                                in_=bank[:], func=AF.Gelu,
                                                                bias=s_bfc[:, c:c + 1]),
                         waits=[("T", tt)], inc="S")
            h_toks.append(tok)

        if dbg and t == 0:
            M.fence()
            M.emit("sync", lambda E: E.nop(), waits=[(k, cnt.get(k, 0)) for k in ("T", "V", "S")])
            dma(M, "sync", d_dbg["hd"], s_h[:], "dc")
            for _e in ("vector", "scalar", "tensor", "gpsimd"):
                M.emit(_e, lambda E: E.nop(), waits=[("dc", cnt["dc"])])
        M.fence()
        last_res = None
        for do in range(DC):
            sl = do % 2
            tok_w = M.emit("gpsimd",
                           lambda E, do=do, sl=sl: E.dma_start(out=s_wprb[:, sl * 4096:(sl + 1) * 4096],
                                                               in_=d_wprT[do * 128:(do + 1) * 128, :]),
                           waits=[("T", wpr_cons[sl])], inc="wp%d" % sl, n=16)
            bank = pb[0 + sl]
            tt = None
            for c in range(HC):
                tt = mm(M, bank[:],
                        s_wprb[:, sl * 4096 + c * 128: sl * 4096 + (c + 1) * 128],
                        s_h[:, c * TILE:(c + 1) * TILE],
                        start=(c == 0), stop=(c == HC - 1),
                        waits=[("wp%d" % sl, tok_w), ("S", h_toks[-1]),
                               ("V", ofm_cons[sl])])
            wpr_cons[sl] = tt
            tok_ev = M.emit("scalar",
                            lambda E, bank=bank, do=do:
                            E.activation(out=s_ofm[:, 0:512], in_=bank[:],
                                         func=AF.Identity, bias=s_bpr[:, do:do + 1]),
                            waits=[("T", tt), ("T", ofm_trans_last[0])], inc="S")
            bank2 = pb[4 + sl]
            tt2 = None
            for g in range(G):
                tt2 = trans(M, bank2[:, g * 128:(g + 1) * 128],
                            s_ofm[:, g * 128:(g + 1) * 128], s_identf[:],
                            waits=[("S", tok_ev), ("V", ofm_cons[sl])])
            ofm_trans_last[0] = tt2
            tok_res = M.emit("vector",
                             lambda E, do=do, bank2=bank2:
                             E.scalar_tensor_tensor(
                                 out=s_x[:].rearrange("p (g d) -> p g d", d=D)[:, :, do * 128:(do + 1) * 128],
                                 in0=bank2[:].rearrange("p (g c) -> p g c", c=128), scalar=1.0,
                                 in1=s_x[:].rearrange("p (g d) -> p g d", d=D)[:, :, do * 128:(do + 1) * 128],
                                 op0=ALU.mult, op1=ALU.add),
                             waits=[("T", tt2)], inc="V")
            ofm_cons[sl] = tok_res
            last_res = tok_res

        tok_store = M.emit("sync",
                           lambda E, t=t: E.dma_start(
                               out=d_y[t * TILE:(t + 1) * TILE, :].rearrange("(g p) d -> p g d", p=128),
                               in_=s_x[:].rearrange("p (g d) -> p g d", d=D)),
                           waits=[("V", last_res)], inc="dy", n=16)
        M.fence()
        M.emit("sync", lambda E: E.nop(), waits=[("dy", tok_store)])

    M.emit("sync", lambda E: E.nop(), waits=[("dy", cnt["dy"])])

    with nc.Block("main") as b2:
        M.replay(b2, sems)

    stack.close()
    return nc


_NC_CACHE = None


def kernel(L0_attn_x, L0_attn_ln_gamma, L0_attn_ln_beta, L0_attn_w_qkv, L0_attn_w_out,
           L0_ffn_W_fc, L0_ffn_b_fc, L0_ffn_A_fc, L0_ffn_B_fc,
           L0_ffn_W_proj, L0_ffn_b_proj, L0_ffn_A_proj, L0_ffn_B_proj):
    global _NC_CACHE
    x = np.ascontiguousarray(np.asarray(L0_attn_x, dtype=np.float32))
    B, N, Dm = x.shape
    ncores = 8
    per = B // ncores

    if _NC_CACHE is None:
        _NC_CACHE = build_program(dbg=DBG)
    nc = _NC_CACHE

    ident_f = np.eye(128, dtype=np.float32)
    ii = np.arange(128)[:, None]
    jj = np.arange(256)[None, :]
    prev = jj < 128
    jjm = np.where(prev, jj, jj - 128)
    bad = np.where(prev, jjm < ii, jjm > ii)
    mask_m = np.where(bad, np.float32(NEG), np.float32(0.0)).astype(BFNP)
    mask_0 = np.where(np.arange(128)[None, :] > ii, np.float32(NEG), np.float32(0.0)).astype(BFNP)
    ones_f = np.ones((1, 128), dtype=np.float32)

    common = dict(
        gamma=np.asarray(L0_attn_ln_gamma, np.float32),
        beta=np.asarray(L0_attn_ln_beta, np.float32),
        wqkv=np.asarray(L0_attn_w_qkv, np.float32),
        wout=np.asarray(L0_attn_w_out, np.float32),
        wfc=np.asarray(L0_ffn_W_fc, np.float32),
        bfc=np.asarray(L0_ffn_b_fc, np.float32),
        afc=np.asarray(L0_ffn_A_fc, np.float32),
        bfcB=np.asarray(L0_ffn_B_fc, np.float32),
        wpr=np.asarray(L0_ffn_W_proj, np.float32),
        bpr=np.asarray(L0_ffn_b_proj, np.float32),
        apr=np.asarray(L0_ffn_A_proj, np.float32),
        bprB=np.asarray(L0_ffn_B_proj, np.float32),
        identf=ident_f, maskm=mask_m, mask0=mask_0, onesf=ones_f,
        onesb=np.ones((1, 128), dtype=BFNP),
    )
    in_maps = []
    for c in range(ncores):
        m = dict(common)
        m["x"] = np.ascontiguousarray(x[c * per:(c + 1) * per].reshape(TOK, Dm))
        in_maps.append(m)

    res = run_bass_kernel_spmd(nc, in_maps, core_ids=list(range(ncores)))
    out = np.empty((B, N, Dm), np.float32)
    for c in range(ncores):
        out[c * per:(c + 1) * per] = res.results[c]["y"].reshape(per, N, Dm)
    if DBG:
        global DBG_RES
        DBG_RES = res.results
    return out


# version 12
# speedup vs baseline: 1.4236x; 1.0124x over previous
import numpy as np
import ml_dtypes
from contextlib import ExitStack

import concourse.bass as bass
import concourse.mybir as mybir
from concourse.bass_utils import run_bass_kernel_spmd

F32 = mybir.dt.float32
BF16 = mybir.dt.bfloat16
I32 = mybir.dt.int32
AF = mybir.ActivationFunctionType
ALU = mybir.AluOpType
AX = mybir.AxisListType
BFNP = ml_dtypes.bfloat16

D = 1024
TOK = 4096
TILE = 512
NT = TOK // TILE
G = TILE // 128
DC = D // 128
H = 16
DH = 64
FFH = 4096
HC = FFH // 128
NEG = -1.0e30
SEQ_TILES = 2048 // TILE

ENGS = ("sync", "tensor", "vector", "scalar", "gpsimd")


class Prog:

    def __init__(self, cnt):
        self.ops = {e: [] for e in ENGS}
        self.cnt = cnt
        self.last_wait = {e: {} for e in ENGS}
        self.fence_snap = {}

    def emit(self, eng, fn, waits=(), inc=None, n=1):
        own = _OWN.get(eng)
        merged = dict(self.fence_snap)
        if own in ("V", "S"):
            if inc is None:
                inc = own
            merged[own] = self.cnt.get(own, 0)
        for s, v in waits:
            if v > merged.get(s, 0):
                merged[s] = v
        w = []
        for s, v in merged.items():
            if s == "T" and eng == "tensor":
                continue
            if v > self.last_wait[eng].get(s, 0):
                self.last_wait[eng][s] = v
                w.append((s, v))
        self.ops[eng].append((fn, w, inc, n))
        if inc is not None:
            self.cnt[inc] = self.cnt.get(inc, 0) + n
            return self.cnt[inc]
        return None

    def fence(self):
        self.fence_snap = {s: self.cnt.get(s, 0) for s in ("T", "V", "S")}

    def replay(self, block, sems):
        for ename in ENGS:
            ops = self.ops[ename]

            def body(eng, ops=ops):
                for fn, w, inc, n in ops:
                    for s, v in w:
                        eng.wait_ge(sems[s], v)
                    instr = fn(eng)
                    if inc is not None:
                        instr.then_inc(sems[inc], n)
            getattr(block, ename)(body)


_OWN = {"tensor": "T", "vector": "V", "scalar": "S"}


DBG = False


def build_program(dbg=False):
    nc = bass.Bass("TRN2", target_bir_lowering=False, detect_race_conditions=False)

    d_x = nc.dram_tensor("x", [TOK, D], F32, kind="ExternalInput").ap()
    d_gamma = nc.dram_tensor("gamma", [D], F32, kind="ExternalInput").ap()
    d_beta = nc.dram_tensor("beta", [D], F32, kind="ExternalInput").ap()
    d_wqkv = nc.dram_tensor("wqkv", [D, 3 * D], F32, kind="ExternalInput").ap()
    d_wout = nc.dram_tensor("wout", [D, D], F32, kind="ExternalInput").ap()
    d_wfc = nc.dram_tensor("wfc", [FFH, D], F32, kind="ExternalInput").ap()
    d_bfc = nc.dram_tensor("bfc", [FFH], F32, kind="ExternalInput").ap()
    d_afc = nc.dram_tensor("afc", [16, D], F32, kind="ExternalInput").ap()
    d_bfcB = nc.dram_tensor("bfcB", [FFH, 16], F32, kind="ExternalInput").ap()
    d_wpr = nc.dram_tensor("wpr", [D, FFH], F32, kind="ExternalInput").ap()
    d_bpr = nc.dram_tensor("bpr", [D], F32, kind="ExternalInput").ap()
    d_apr = nc.dram_tensor("apr", [16, FFH], F32, kind="ExternalInput").ap()
    d_bprB = nc.dram_tensor("bprB", [D, 16], F32, kind="ExternalInput").ap()
    d_identf = nc.dram_tensor("identf", [128, 128], F32, kind="ExternalInput").ap()
    d_mask = nc.dram_tensor("maskm", [128, 256], BF16, kind="ExternalInput").ap()
    d_mask0 = nc.dram_tensor("mask0", [128, 128], BF16, kind="ExternalInput").ap()
    d_onesf = nc.dram_tensor("onesf", [1, 128], F32, kind="ExternalInput").ap()
    d_onesb = nc.dram_tensor("onesb", [1, 128], BF16, kind="ExternalInput").ap()

    d_y = nc.dram_tensor("y", [TOK, D], F32, kind="ExternalOutput").ap()
    d_dbg = {}
    if dbg:
        for name, shape, dt in [("wall", [128, 24576], BF16), ("bw", [128, 16], F32),
                                ("bwv", [1, 1024], F32), ("scl", [128, 8], F32),
                                ("btfc", [16, FFH], BF16), ("a2fc", [16, D], BF16),
                                ("stat", [128, 64], F32), ("xnfm", [128, 4096], BF16),
                                ("qd", [128, 4096], BF16), ("kd", [128, 4096], BF16),
                                ("vd", [128, 4096], BF16), ("ao", [128, 4096], BF16),
                                ("x1", [128, 4096], F32), ("hd", [128, HC * TILE], BF16),
                                ("wfcT0", [128, 1024], BF16), ("wprT0", [128, 4096], BF16)]:
            d_dbg[name] = nc.dram_tensor("dbg_" + name, shape, dt, kind="ExternalOutput").ap()

    d_wfcT = nc.dram_tensor("wfcT_scr", [FFH, D], BF16, kind="Internal").ap()
    d_wprT = nc.dram_tensor("wprT_scr", [D, FFH], BF16, kind="Internal").ap()
    d_woutb = nc.dram_tensor("wout_scr", [D, D], BF16, kind="Internal").ap()

    cnt = {}
    stack = ExitStack()
    e = stack.enter_context

    sems = {}
    for s in ("dc", "da", "db", "dwa", "dwb", "dx", "dy",
              "wo0", "wo1", "wf0", "wf1", "wp0", "wp1", "T", "V", "S", "BAR"):
        sems[s] = e(nc.semaphore(s))

    pb = [e(nc.psum_tensor(f"pb{i}", [128, 512], F32)) for i in range(8)]

    s_identf = e(nc.sbuf_tensor("sbident_f", [128, 128], F32))
    s_mask = e(nc.sbuf_tensor("sb_maskm", [128, 256], BF16))
    s_mask0 = e(nc.sbuf_tensor("sb_mask0", [128, 128], BF16))
    s_onesf = e(nc.sbuf_tensor("sb_onesf", [1, 128], F32))
    s_gcol = e(nc.sbuf_tensor("gcol", [128, 8], F32))
    s_g8col = e(nc.sbuf_tensor("g8col", [128, 8], F32))
    s_betab = e(nc.sbuf_tensor("betab", [128, 8], BF16))
    s_bw = e(nc.sbuf_tensor("bw", [128, 16], F32))
    s_bwv = e(nc.sbuf_tensor("bwv", [1, 1024], BF16))
    s_onesb = e(nc.sbuf_tensor("sb_onesb", [1, 128], BF16))
    s_bfc = e(nc.sbuf_tensor("bfccol", [128, 32], F32))
    s_bpr = e(nc.sbuf_tensor("bprcol", [128, 8], F32))
    s_cm05 = e(nc.sbuf_tensor("cm05", [128, 1], F32))
    s_cp05 = e(nc.sbuf_tensor("cp05", [128, 1], F32))
    s_scfc = e(nc.sbuf_tensor("scfc", [128, 1], F32))
    s_rsfc = e(nc.sbuf_tensor("rsfc", [128, 1], F32))
    s_scpr = e(nc.sbuf_tensor("scpr", [128, 1], F32))
    s_rspr = e(nc.sbuf_tensor("rspr", [128, 1], F32))
    s_am = e(nc.sbuf_tensor("amcols", [128, 32], F32))
    s_hsfc = e(nc.sbuf_tensor("hsfc", [128, 1], F32))
    s_hspr = e(nc.sbuf_tensor("hspr", [128, 1], F32))
    s_amc = e(nc.sbuf_tensor("amc", [128, 1], F32))
    s_amr = e(nc.sbuf_tensor("amr", [1, 128], F32))
    s_sc1 = e(nc.sbuf_tensor("sc1", [1, 1], F32))

    s_wall = e(nc.sbuf_tensor("wall", [128, DC * 3 * D], BF16))

    s_a2fc = e(nc.sbuf_tensor("a2fc", [16, D], BF16))
    s_btfc = e(nc.sbuf_tensor("btfc", [16, FFH], BF16))
    s_a2pr = e(nc.sbuf_tensor("a2pr", [16, FFH], BF16))
    s_btpr = e(nc.sbuf_tensor("btpr", [16, D], BF16))

    P = Prog(cnt)

    def dma(prog, eng, out_ap, in_ap, sem, slow=False):
        if slow:
            return prog.emit(eng, lambda E, o=out_ap, i=in_ap: E.dma_start(out=o, in_=i, allow_slow_non_contiguous=True), inc=sem, n=16)
        return prog.emit(eng, lambda E, o=out_ap, i=in_ap: E.dma_start(out=o, in_=i), inc=sem, n=16)

    def mm(prog, out_ap, lhsT, rhs, start, stop, tp=None, waits=()):
        def f(E, o=out_ap, l=lhsT, r=rhs):
            return E.matmul(o, l, r, start=start, stop=stop, tile_position=tp,
                            skip_group_check=True)
        return prog.emit("tensor", f, waits=waits, inc="T")

    def trans(prog, out_ap, in_ap, ident, start=True, stop=True, waits=()):
        def f(E, o=out_ap, i=in_ap, idn=ident):
            return E.matmul(o, i, idn, start=start, stop=stop, is_transpose=True,
                            skip_group_check=True)
        return prog.emit("tensor", f, waits=waits, inc="T")

    with ExitStack() as arena:
        a = arena.enter_context
        t_stage = a(nc.sbuf_tensor("stage", [128, 2 * FFH], F32))
        t_q = a(nc.sbuf_tensor("tq", [128, 2048], F32))
        t_k0 = a(nc.sbuf_tensor("tk0", [128, 2048], I32))
        t_k0f = a(nc.sbuf_tensor("tk0f", [128, 2048], F32))
        t_d = a(nc.sbuf_tensor("td", [128, 2048], F32))
        t_s1 = a(nc.sbuf_tensor("ts1", [128, 2048], F32))
        t_s2 = a(nc.sbuf_tensor("ts2", [128, 2048], F32))
        t_wqf = a(nc.sbuf_tensor("twqf", [128, FFH], F32))
        t_panel = a(nc.sbuf_tensor("tpanel", [128, FFH], BF16))
        t_small = a(nc.sbuf_tensor("tsmall", [128, 16], F32))
        t_arow = a(nc.sbuf_tensor("tarow", [16, FFH], F32))

        tok_dx = 0
        tok_dx = dma(P, "sync", s_identf[:], d_identf, "dc")
        tok_dx = dma(P, "sync", s_mask[:], d_mask, "dc")
        tok_dx = dma(P, "sync", s_mask0[:], d_mask0, "dc")
        tok_dx = dma(P, "sync", s_onesf[:], d_onesf, "dc")
        tok_dx = dma(P, "sync", s_onesb[:], d_onesb, "dc")
        tok_dx = dma(P, "sync", s_gcol[:], d_gamma.rearrange("(c p) -> p c", p=128), "dc", slow=True)
        tok_dx = dma(P, "sync", t_small[:, 0:8], d_beta.rearrange("(c p) -> p c", p=128), "dc", slow=True)
        tok_dx = dma(P, "sync", s_bfc[:], d_bfc.rearrange("(c p) -> p c", p=128), "dc", slow=True)
        tok_dx = dma(P, "sync", s_bpr[:], d_bpr.rearrange("(c p) -> p c", p=128), "dc", slow=True)
        tok_consts = tok_dx
        P.emit("vector", lambda E: E.memset(s_cm05[:], -0.5))
        P.emit("vector", lambda E: E.memset(s_cp05[:], 0.5))
        P.emit("vector", lambda E: E.tensor_scalar_mul(s_g8col[:], s_gcol[:], 0.125),
               waits=[("dc", tok_consts)])
        tv = P.emit("vector", lambda E: E.tensor_copy(out=s_betab[:], in_=t_small[:, 0:8]),
                    waits=[("dc", tok_consts)], inc="V")

        cast_toks = []
        for dc in range(DC):
            w = [("S", cast_toks[dc - 2])] if dc >= 2 else []
            sem = "da" if dc % 2 == 0 else "db"
            tok_l = P.emit("sync",
                           lambda E, dc=dc: E.dma_start(out=t_stage[:, (dc % 2) * 3072:(dc % 2) * 3072 + 3072],
                                                        in_=d_wqkv[dc * 128:(dc + 1) * 128, :]),
                           waits=w, inc=sem, n=16)
            tok = P.emit("scalar",
                         lambda E, dc=dc: E.copy(out=s_wall[:, dc * 3072:(dc + 1) * 3072],
                                                 in_=t_stage[:, (dc % 2) * 3072:(dc % 2) * 3072 + 3072]),
                         waits=[(sem, tok_l)], inc="S")
            cast_toks.append(tok)

        P.fence()
        for oc in range(16):
            for dc in range(DC):
                mm(P, pb[0][:, oc:oc + 1],
                   s_wall[:, dc * 3072 + oc * 128: dc * 3072 + (oc + 1) * 128],
                   s_betab[:, dc:dc + 1],
                   start=(dc == 0), stop=(dc == DC - 1),
                   waits=[("S", cast_toks[-1]), ("V", tv)])
        tok_T = P.emit("tensor", lambda E: E.nop(), inc="T")
        for half in range(2):
            for dc in range(DC):
                mm(P, pb[1 + half][0:1, 0:512],
                   s_betab[:, dc:dc + 1],
                   s_wall[:, dc * 3072 + 2048 + half * 512: dc * 3072 + 2048 + (half + 1) * 512],
                   start=(dc == 0), stop=(dc == DC - 1))
        tok_T = P.emit("tensor", lambda E: E.nop(), inc="T")
        tok_bw = P.emit("vector", lambda E: E.tensor_copy(out=s_bw[:], in_=pb[0][:, 0:16]),
                        waits=[("T", tok_T)], inc="V")
        tok_bw = P.emit("vector", lambda E: E.tensor_scalar_mul(s_bw[:, 0:8], s_bw[:, 0:8], 0.125),
                        inc="V")
        for half in range(2):
            tok_bwv = P.emit("vector",
                             lambda E, half=half: E.tensor_copy(out=s_bwv[0:1, half * 512:(half + 1) * 512],
                                                                in_=pb[1 + half][0:1, 0:512]),
                             waits=[("T", tok_T)], inc="V")
        fold_toks = []
        for dc in range(DC):
            P.emit("scalar",
                   lambda E, dc=dc: E.activation(out=s_wall[:, dc * 3072:dc * 3072 + 1024],
                                                 in_=s_wall[:, dc * 3072:dc * 3072 + 1024],
                                                 func=AF.Copy, scale=s_g8col[:, dc:dc + 1]),
                   waits=[("V", tok_bwv)])
            tok = P.emit("scalar",
                         lambda E, dc=dc: E.activation(out=s_wall[:, dc * 3072 + 1024:dc * 3072 + 3072],
                                                       in_=s_wall[:, dc * 3072 + 1024:dc * 3072 + 3072],
                                                       func=AF.Copy, scale=s_gcol[:, dc:dc + 1]),
                         inc="S")
            fold_toks.append(tok)

        P.fence()
        st_toks = [0, 0]
        cast_tk = [0, 0]
        for ic in range(DC):
            sem = "da" if ic % 2 == 0 else "db"
            ssem = "dwa" if ic % 2 == 0 else "dwb"
            tok_l = P.emit("sync",
                           lambda E, ic=ic: E.dma_start(out=t_stage[:, (ic % 2) * 4096:(ic % 2) * 4096 + 1024],
                                                        in_=d_wout[ic * 128:(ic + 1) * 128, :]),
                           waits=[("V", cast_tk[ic % 2])], inc=sem, n=16)
            tok_c = P.emit("vector",
                           lambda E, ic=ic: E.tensor_copy(out=t_panel[:, (ic % 2) * 1024:(ic % 2) * 1024 + 1024],
                                                          in_=t_stage[:, (ic % 2) * 4096:(ic % 2) * 4096 + 1024]),
                           waits=[(sem, tok_l), (ssem, st_toks[ic % 2])], inc="V")
            cast_tk[ic % 2] = tok_c
            st_toks[ic % 2] = P.emit(
                "gpsimd",
                lambda E, ic=ic: E.dma_start(out=d_woutb[ic * 128:(ic + 1) * 128, :],
                                             in_=t_panel[:, (ic % 2) * 1024:(ic % 2) * 1024 + 1024]),
                waits=[("V", tok_c)], inc=ssem, n=16)
        P.fence()

        tok_a = dma(P, "sync", t_arow[:, 0:1024], d_afc, "da")
        tva = P.emit("vector", lambda E: E.tensor_scalar(out=s_a2fc[:], in0=t_arow[:16, 0:1024],
                                                         scalar1=2.0, scalar2=None, op0=ALU.mult),
                     waits=[("da", tok_a)], inc="V")
        tok_a = P.emit("sync", lambda E: E.dma_start(out=t_arow[:], in_=d_apr),
                       waits=[("V", tva)], inc="da", n=16)
        tva = P.emit("vector", lambda E: E.tensor_scalar(out=s_a2pr[:], in0=t_arow[:16, :],
                                                         scalar1=2.0, scalar2=None, op0=ALU.mult),
                     waits=[("da", tok_a)], inc="V")
        P.fence()
        btr = [0, 0]
        bev = [0, 0]
        for c in range(HC):
            sem = "da" if c % 2 == 0 else "db"
            tok_l = P.emit("sync",
                           lambda E, c=c: E.dma_start(out=t_stage[:, (c % 2) * 4096:(c % 2) * 4096 + 16],
                                                      in_=d_bfcB[c * 128:(c + 1) * 128, :]),
                           waits=[("T", btr[c % 2])], inc=sem, n=16)
            tt = trans(P, pb[2 + c % 2][0:16, 0:128],
                       t_stage[:, (c % 2) * 4096:(c % 2) * 4096 + 16], s_identf[:],
                       waits=[(sem, tok_l), ("S", bev[c % 2])])
            btr[c % 2] = tt
            bev[c % 2] = P.emit("scalar",
                                lambda E, c=c: E.copy(out=s_btfc[:16, c * 128:(c + 1) * 128],
                                                      in_=pb[2 + c % 2][0:16, 0:128]),
                                waits=[("T", tt)], inc="S")
        P.fence()
        btr = [0, 0]
        bev = [0, 0]
        for c in range(DC):
            sem = "da" if c % 2 == 0 else "db"
            tok_l = P.emit("sync",
                           lambda E, c=c: E.dma_start(out=t_stage[:, (c % 2) * 4096:(c % 2) * 4096 + 16],
                                                      in_=d_bprB[c * 128:(c + 1) * 128, :]),
                           waits=[("T", btr[c % 2])], inc=sem, n=16)
            tt = trans(P, pb[2 + c % 2][0:16, 0:128],
                       t_stage[:, (c % 2) * 4096:(c % 2) * 4096 + 16], s_identf[:],
                       waits=[(sem, tok_l), ("S", bev[c % 2])])
            btr[c % 2] = tt
            bev[c % 2] = P.emit("scalar",
                                lambda E, c=c: E.copy(out=s_btpr[:16, c * 128:(c + 1) * 128],
                                                      in_=pb[2 + c % 2][0:16, 0:128]),
                                waits=[("T", tt)], inc="S")
        P.fence()

        def quant_weight(d_w, nchunk, free, s_scale, s_rscale, s_hscale, am_ncols, d_scrT, a2, bt,
                         panel_cols, tpos_cols):
            P.fence()
            red_tk = [0, 0]
            for c in range(nchunk):
                sl = (c % 2) * 4096
                sem = "da" if c % 2 == 0 else "db"
                tok_l = P.emit("sync",
                               lambda E, c=c, sl=sl: E.dma_start(out=t_stage[:, sl:sl + free],
                                                                 in_=d_w[c * 128:(c + 1) * 128, :]),
                               waits=[("V", red_tk[c % 2])], inc=sem, n=16)
                red_tk[c % 2] = P.emit("vector",
                                       lambda E, c=c, sl=sl: E.reduce_max(out=s_am[:, c:c + 1],
                                                                          in_=t_stage[:, sl:sl + free],
                                                                          axis=AX.X, apply_absolute_value=True),
                                       waits=[(sem, tok_l)], inc="V")
            P.fence()
            tok = P.emit("vector", lambda E: E.reduce_max(out=s_amc[:], in_=s_am[:, 0:am_ncols], axis=AX.X),
                         inc="V")
            tt = trans(P, pb[3][0:1, 0:128], s_amc[:], s_identf[:], waits=[("V", tok)])
            tok = P.emit("vector", lambda E: E.tensor_copy(out=s_amr[:], in_=pb[3][0:1, 0:128]),
                         waits=[("T", tt)], inc="V")
            tok = P.emit("vector", lambda E: E.reduce_max(out=s_sc1[:], in_=s_amr[:], axis=AX.X), inc="V")
            tok = P.emit("vector", lambda E: E.tensor_scalar_mul(s_sc1[:], s_sc1[:], float(np.float32(1.0) / np.float32(127.0))),
                         inc="V")
            tt = mm(P, pb[3][:, 0:1], s_onesf[:], s_sc1[:], start=True, stop=True,
                    waits=[("V", tok)])
            tok = P.emit("vector", lambda E: E.tensor_copy(out=s_scale[:], in_=pb[3][:, 0:1]),
                         waits=[("T", tt)], inc="V")
            tok = P.emit("vector", lambda E: E.reciprocal(out=s_rscale[:], in_=s_scale[:]), inc="V")
            tok = P.emit("vector", lambda E: E.tensor_scalar_mul(s_hscale[:], s_scale[:], 0.5), inc="V")
            P.fence()
            nsl = free // 1024
            tok_store = 0
            ev_bank = {}
            trans_last = 0
            tq_tk = [0, 0]
            gsl = [0]
            stt_tk = [0, 0]
            pend_tw = [None]
            tw_done = [0]
            for c in range(nchunk):
                sl = (c % 2) * 4096
                sem = "da" if c % 2 == 0 else "db"
                tok_l = P.emit("sync",
                               lambda E, c=c, sl=sl: E.dma_start(out=t_stage[:, sl:sl + free],
                                                                 in_=d_w[c * 128:(c + 1) * 128, :]),
                               waits=[("S", tq_tk[c % 2])], inc=sem, n=16)
                for s in range(nsl):
                    slot = gsl[0] % 2
                    q = slot * 1024
                    gsl[0] += 1
                    src = t_stage[:, sl + s * 1024: sl + (s + 1) * 1024]
                    tq = P.emit("scalar",
                                lambda E, src=src, q=q: E.activation(out=t_q[:, q:q + 1024], in_=src,
                                                                     func=AF.Copy, scale=s_rscale[:]),
                                waits=[(sem, tok_l), ("V", stt_tk[slot])], inc="S")
                    if pend_tw[0] is not None:
                        pf, pq, ptk = pend_tw[0]
                        tw = P.emit("scalar", pf, waits=[("V", ptk), ("T", trans_last)], inc="S")
                        tw_done[0] = tw
                        pend_tw[0] = None
                    tk = P.emit("vector",
                                lambda E, q=q: E.tensor_copy(out=t_k0[:, q:q + 1024], in_=t_q[:, q:q + 1024]),
                                waits=[("S", tq)], inc="V")
                    tk = P.emit("vector",
                                lambda E, q=q: E.tensor_tensor(out=t_d[:, q:q + 1024], in0=t_q[:, q:q + 1024],
                                                               in1=t_k0[:, q:q + 1024], op=ALU.subtract),
                                inc="V")
                    t1 = P.emit("scalar",
                                lambda E, q=q: E.activation(out=t_s1[:, q:q + 1024], in_=t_d[:, q:q + 1024],
                                                            func=AF.Sign, bias=s_cm05[:]),
                                waits=[("V", tk)], inc="S")
                    t1 = P.emit("scalar",
                                lambda E, q=q: E.activation(out=t_s2[:, q:q + 1024], in_=t_d[:, q:q + 1024],
                                                            func=AF.Sign, bias=s_cp05[:]), inc="S")
                    tk = P.emit("vector",
                                lambda E, q=q: E.scalar_tensor_tensor(out=t_s1[:, q:q + 1024],
                                                                      in0=t_k0[:, q:q + 1024], scalar=2.0,
                                                                      in1=t_s1[:, q:q + 1024],
                                                                      op0=ALU.mult, op1=ALU.add),
                                waits=[("S", t1)], inc="V")
                    tk = P.emit("vector",
                                lambda E, q=q: E.tensor_tensor(out=t_d[:, q:q + 1024], in0=t_s1[:, q:q + 1024],
                                                               in1=t_s2[:, q:q + 1024], op=ALU.add),
                                inc="V")
                    stt_tk[slot] = tk
                    def mk_tw(s=s, q=q):
                        return lambda E: E.activation(out=t_wqf[:, s * 1024:(s + 1) * 1024],
                                                      in_=t_d[:, q:q + 1024],
                                                      func=AF.Copy, scale=s_hscale[:])
                    pend_tw[0] = (mk_tw(), q, tk)
                if pend_tw[0] is not None:
                    pf, pq, ptk = pend_tw[0]
                    tw = P.emit("scalar", pf, waits=[("V", ptk), ("T", trans_last)], inc="S")
                    tw_done[0] = tw
                    pend_tw[0] = None
                nblk = free // 128
                ev_tok = None
                for b in range(nblk):
                    bi = 4 + ((b // 4) % 2)
                    bank = pb[bi]
                    tt = trans(P, bank[:, (b % 4) * 128:(b % 4) * 128 + 128],
                               t_wqf[:, b * 128:(b + 1) * 128], s_identf[:],
                               start=True, stop=False,
                               waits=[("S", tw), ("S", ev_bank.get(bi, 0)),
                                      ("dwa", tok_store)])
                    tt = mm(P, bank[:, (b % 4) * 128:(b % 4) * 128 + 128],
                            a2[:16, b * 128:(b + 1) * 128],
                            bt[:16, c * 128:(c + 1) * 128],
                            start=False, stop=True)
                    if b % 4 == 3:
                        ev_tok = P.emit("scalar",
                                        lambda E, b=b, bank=bank: E.copy(out=t_panel[:, (b - 3) * 128:(b + 1) * 128],
                                                                         in_=bank[:]),
                                        waits=[("T", tt)], inc="S")
                        ev_bank[bi] = ev_tok
                trans_last = tt
                tq_tk[c % 2] = tq
                tok_store = P.emit("gpsimd",
                                   lambda E, c=c: E.dma_start(out=d_scrT[c * 128:(c + 1) * 128, :],
                                                              in_=t_panel[:, 0:free]),
                                   waits=[("S", ev_tok)], inc="dwa", n=16)
            P.fence()

        quant_weight(d_wfc, HC, 1024, s_scfc, s_rsfc, s_hsfc, HC, d_wfcT, s_a2fc, s_btfc, 1024, None)
        quant_weight(d_wpr, DC, 4096, s_scpr, s_rspr, s_hspr, DC, d_wprT, s_a2pr, s_btpr, 4096, None)

        if dbg:
            P.fence()
            P.emit("sync", lambda E: E.nop(),
                   waits=[(k, cnt.get(k, 0)) for k in ("T", "V", "S")])
            dma(P, "sync", d_dbg["wall"], s_wall[:], "dc")
            dma(P, "sync", d_dbg["bw"], s_bw[:], "dc")
            dma(P, "sync", d_dbg["bwv"], s_bwv[:], "dc")
            P.emit("vector", lambda E: E.tensor_copy(out=s_am[:, 0:1], in_=s_scfc[:]))
            P.emit("vector", lambda E: E.tensor_copy(out=s_am[:, 1:2], in_=s_rsfc[:]))
            P.emit("vector", lambda E: E.tensor_copy(out=s_am[:, 2:3], in_=s_scpr[:]))
            tvd = P.emit("vector", lambda E: E.tensor_copy(out=s_am[:, 3:4], in_=s_rspr[:]), inc="V")
            dma(P, "sync", d_dbg["scl"], s_am[:, 0:8], "dc")
            dma(P, "sync", d_dbg["btfc"], s_btfc[:], "dc")
            dma(P, "sync", d_dbg["a2fc"], s_a2fc[:], "dc")
            dma(P, "sync", d_dbg["wfcT0"], d_wfcT[0:128, :], "dc")
            dma(P, "sync", d_dbg["wprT0"], d_wprT[0:128, :], "dc")
        for eng in ENGS:
            P.emit(eng, lambda E: E.nop(), inc="BAR")

        with nc.Block("setup") as b1:
            P.replay(b1, sems)

    s_x = e(nc.sbuf_tensor("xsb", [128, TILE // 128 * D], F32))
    s_xnb = e(nc.sbuf_tensor("xnb", [128, D], F32))
    s_xnfm = e(nc.sbuf_tensor("xnfm", [128, DC * TILE], BF16))
    s_q = e(nc.sbuf_tensor("qsb", [128, 8 * TILE], BF16))
    s_k = e(nc.sbuf_tensor("ksb", [128, 8 * TILE], BF16))
    s_v = e(nc.sbuf_tensor("vsb", [128, G * D], BF16))
    s_kprev = e(nc.sbuf_tensor("kprev", [128, 8 * 128], BF16))
    s_vprev = e(nc.sbuf_tensor("vprev", [128, D], BF16))
    s_ao = e(nc.sbuf_tensor("aosb", [128, 8 * TILE], BF16))
    s_h = e(nc.sbuf_tensor("hsb", [128, HC * TILE], BF16))
    s_sm = e(nc.sbuf_tensor("smbuf", [128, 4 * 256], F32))
    s_probs = e(nc.sbuf_tensor("probs", [128, 4 * 256], F32))
    s_pT = e(nc.sbuf_tensor("pT", [128, 4 * 256], BF16))
    s_stat = e(nc.sbuf_tensor("stat", [128, 64], F32))
    s_ofm = e(nc.sbuf_tensor("ofm", [128, 512], F32))
    s_wfcb = e(nc.sbuf_tensor("wfcb", [128, 2 * 1024], BF16))
    s_wprb = e(nc.sbuf_tensor("wprb", [128, 2 * 4096], BF16))
    s_woutb = e(nc.sbuf_tensor("woutb", [128, 2 * 1024], BF16))

    NEGMAX, ROWSUM, RECIP = 0, 4, 52
    SUM, SSQ, MU, MSQ, VARP, R1, V2P, ASC, NMA, T0 = 8, 12, 16, 20, 24, 28, 32, 36, 40, 44

    M = Prog(cnt)
    M.fence_snap = {}
    for eng in ENGS:
        M.emit(eng, lambda E: E.nop(), waits=[("BAR", cnt["BAR"])])

    tok_store = 0
    wout_cons = [0, 0]
    wfc_cons = [0, 0]
    wpr_cons = [0, 0]
    ofm_cons = [0, 0]
    ofm_trans_last = [0]
    xnb_cons = [0, 0]
    xnt_prev_ev = [0, 0]

    def emit_norm_transpose(M, a_col, nma_col):
        ev_last = None
        for g in range(G):
            sl = 0
            tok_x = M.emit("vector",
                           lambda E, g=g, sl=sl, a_col=a_col, nma_col=nma_col:
                           E.tensor_scalar(out=s_xnb[:, sl * D:(sl + 1) * D],
                                           in0=s_x[:, g * D:(g + 1) * D],
                                           scalar1=s_stat[:, a_col + g:a_col + g + 1],
                                           scalar2=s_stat[:, nma_col + g:nma_col + g + 1],
                                           op0=ALU.mult, op1=ALU.add),
                           waits=[("T", xnb_cons[sl])], inc="V")
            tt = None
            for half in range(2):
                bank = pb[half]
                for i in range(4):
                    dc = half * 4 + i
                    tt = trans(M, bank[:, i * 128:(i + 1) * 128],
                               s_xnb[:, sl * D + dc * 128: sl * D + (dc + 1) * 128],
                               s_identf[:],
                               waits=[("V", tok_x), ("S", xnt_prev_ev[half])])
                dst = bass.AP(s_xnfm, half * 4 * TILE + g * 128,
                              [[DC * TILE, 128], [TILE, 4], [1, 128]])
                tok = M.emit("scalar",
                             lambda E, bank=bank, dst=dst:
                             E.copy(out=dst, in_=bank[:].rearrange("p (a b) -> p a b", b=128)),
                             waits=[("T", tt)], inc="S")
                xnt_prev_ev[half] = tok
                ev_last = tok
            xnb_cons[sl] = tt
        return ev_last

    for t in range(NT):
        seq_start = (t % SEQ_TILES == 0)
        tok_x = dma(M, "sync", s_x[:].rearrange("p (g d) -> p g d", d=D),
                    d_x[t * TILE:(t + 1) * TILE, :].rearrange("(g p) d -> p g d", p=128),
                    "dx")

        def stats_and_norm(wait_tok, wait_sem, r_out_cols):
            tv = ts = None
            for g in range(G):
                xg = s_x[:, g * D:(g + 1) * D]
                tv = M.emit("vector",
                            lambda E, xg=xg, g=g: E.tensor_scalar(out=s_xnfm[:, 0:1024], in0=xg, scalar1=1.0,
                                                                  scalar2=None, op0=ALU.mult, op1=ALU.add,
                                                                  accum_out=s_stat[:, SUM + g:SUM + g + 1]),
                            waits=[(wait_sem, wait_tok)], inc="V")
                ts = M.emit("scalar",
                            lambda E, xg=xg, g=g: E.activation(out=s_xnfm[:, 1024:2048], in_=xg, func=AF.Square,
                                                               accum_out=s_stat[:, SSQ + g:SSQ + g + 1]),
                            waits=[(wait_sem, wait_tok)], inc="S")
            c = 1.0 / 1024.0
            tv = M.emit("vector", lambda E: E.tensor_scalar_mul(s_stat[:, MU:MU + 4], s_stat[:, SUM:SUM + 4], c),
                        waits=[("V", tv)], inc="V")
            tv = M.emit("vector", lambda E: E.tensor_scalar_mul(s_stat[:, MSQ:MSQ + 4], s_stat[:, SSQ:SSQ + 4], c),
                        waits=[("S", ts)], inc="V")
            tv = M.emit("vector", lambda E: E.tensor_tensor(out=s_stat[:, T0:T0 + 4], in0=s_stat[:, MU:MU + 4],
                                                            in1=s_stat[:, MU:MU + 4], op=ALU.mult), inc="V")
            tv = M.emit("vector", lambda E: E.tensor_tensor(out=s_stat[:, VARP:VARP + 4], in0=s_stat[:, MSQ:MSQ + 4],
                                                            in1=s_stat[:, T0:T0 + 4], op=ALU.subtract), inc="V")
            tv = M.emit("vector", lambda E: E.tensor_scalar_add(s_stat[:, VARP:VARP + 4], s_stat[:, VARP:VARP + 4],
                                                                1e-5), inc="V")
            ts = M.emit("scalar", lambda E: E.activation(out=s_stat[:, T0:T0 + 4], in_=s_stat[:, VARP:VARP + 4],
                                                         func=AF.Ln),
                        waits=[("V", tv)], inc="S")
            ts = M.emit("scalar", lambda E: E.activation(out=s_stat[:, R1:R1 + 4], in_=s_stat[:, T0:T0 + 4],
                                                         func=AF.Exp, scale=s_cm05[:]), inc="S")
            return tv, ts

        tv, ts = stats_and_norm(tok_x, "dx", R1)
        T1 = 48
        tv = M.emit("vector", lambda E: E.reciprocal(out=s_stat[:, T1:T1 + 4], in_=s_stat[:, VARP:VARP + 4]),
                    inc="V")
        tv = M.emit("vector", lambda E: E.tensor_scalar_add(s_stat[:, V2P:V2P + 4], s_stat[:, VARP:VARP + 4], -1e-5),
                    inc="V")
        tv = M.emit("vector", lambda E: E.tensor_tensor(out=s_stat[:, V2P:V2P + 4], in0=s_stat[:, V2P:V2P + 4],
                                                        in1=s_stat[:, T1:T1 + 4], op=ALU.mult), inc="V")
        tv = M.emit("vector", lambda E: E.tensor_scalar_add(s_stat[:, V2P:V2P + 4], s_stat[:, V2P:V2P + 4], 1e-5),
                    inc="V")
        ts = M.emit("scalar", lambda E: E.activation(out=s_stat[:, T1:T1 + 4], in_=s_stat[:, V2P:V2P + 4],
                                                     func=AF.Ln),
                    waits=[("V", tv), ("S", ts)], inc="S")
        ts = M.emit("scalar", lambda E: E.activation(out=s_stat[:, V2P:V2P + 4], in_=s_stat[:, T1:T1 + 4],
                                                     func=AF.Exp, scale=s_cm05[:]), inc="S")
        tv = M.emit("vector", lambda E: E.tensor_tensor(out=s_stat[:, ASC:ASC + 4], in0=s_stat[:, R1:R1 + 4],
                                                        in1=s_stat[:, V2P:V2P + 4], op=ALU.mult),
                    waits=[("S", ts)], inc="V")
        tv = M.emit("vector", lambda E: E.scalar_tensor_tensor(out=s_stat[:, NMA:NMA + 4], in0=s_stat[:, MU:MU + 4],
                                                               scalar=-1.0, in1=s_stat[:, ASC:ASC + 4],
                                                               op0=ALU.mult, op1=ALU.mult), inc="V")
        tok_xnfm = emit_norm_transpose(M, ASC, NMA)

        if dbg and t == 0:
            M.fence()
            M.emit("sync", lambda E: E.nop(), waits=[(k, cnt.get(k, 0)) for k in ("T", "V", "S")])
            dma(M, "sync", d_dbg["stat"], s_stat[:], "dc")
            dma(M, "sync", d_dbg["xnfm"], s_xnfm[:], "dc")
            for _e in ("vector", "scalar", "tensor", "gpsimd"):
                M.emit(_e, lambda E: E.nop(), waits=[("dc", cnt["dc"])])
        M.fence()
        qk_ev = []
        for oc in range(16):
            bank = pb[2 + (oc % 2)]
            tt = None
            for dc in range(DC):
                tt = mm(M, bank[:],
                        s_wall[:, dc * 3072 + oc * 128: dc * 3072 + (oc + 1) * 128],
                        s_xnfm[:, dc * TILE:(dc + 1) * TILE],
                        start=(dc == 0), stop=(dc == DC - 1),
                        waits=[("S", qk_ev[oc - 2] if oc >= 2 else 0)])
            dst = s_q if oc < 8 else s_k
            fc = oc % 8
            tok = M.emit("scalar",
                         lambda E, dst=dst, fc=fc, oc=oc, bank=bank:
                         E.activation(out=dst[:, fc * TILE:(fc + 1) * TILE], in_=bank[:],
                                      func=AF.Identity, bias=s_bw[:, oc:oc + 1]),
                         waits=[("T", tt)], inc="S")
            qk_ev.append(tok)
        v_ev = []
        for g in range(G):
            bh = [pb[4 + (g % 2) * 2], pb[5 + (g % 2) * 2]]
            tt = None
            for half in range(2):
                for dc in range(DC):
                    tt = mm(M, bh[half][:],
                            s_xnfm[:, dc * TILE + g * 128: dc * TILE + (g + 1) * 128],
                            s_wall[:, dc * 3072 + 2048 + half * 512: dc * 3072 + 2048 + (half + 1) * 512],
                            start=(dc == 0), stop=False,
                            waits=[("V", v_ev[g - 2] if g >= 2 else 0)])
                tt = mm(M, bh[half][:], s_onesb[:], s_bwv[0:1, half * 512:(half + 1) * 512],
                        start=False, stop=True)
            tok = None
            for half in range(2):
                tok = M.emit("vector",
                             lambda E, g=g, half=half, b=bh[half]:
                             E.tensor_copy(out=s_v[:, g * D + half * 512: g * D + (half + 1) * 512], in_=b[:]),
                             waits=[("T", tt)], inc="V")
            v_ev.append(tok)

        if dbg and t == 0:
            M.fence()
            M.emit("sync", lambda E: E.nop(), waits=[(k, cnt.get(k, 0)) for k in ("T", "V", "S")])
            dma(M, "sync", d_dbg["qd"], s_q[:], "dc")
            dma(M, "sync", d_dbg["kd"], s_k[:], "dc")
            dma(M, "sync", d_dbg["vd"], s_v[:], "dc")
            for _e in ("vector", "scalar", "tensor", "gpsimd"):
                M.emit(_e, lambda E: E.nop(), waits=[("dc", cnt["dc"])])
        M.fence()
        NI = G * H
        ctx = []
        for w in range(G):
            fw = (seq_start and w == 0)
            for h in range(H):
                ctx.append((w, h, fw))
        tokA = [0] * NI; tokB = [0] * NI; tokC = [0] * NI; tokD = [0] * NI
        tokE = [0] * NI; tokF = [0] * NI; tokG = [0] * NI; tokH = [0] * NI
        engH = [0] * NI

        def stA(i):
            w, h, fw = ctx[i]
            po = 64 * (h % 2); fc = h // 2
            qs = s_q[po:po + 64, fc * TILE + w * 128: fc * TILE + (w + 1) * 128]
            bankS = pb[i % 4]
            wts = [("S", qk_ev[8 + fc]), ("V", tokB[i - 4] if i >= 4 else 0)]
            if fw:
                tt = mm(M, bankS[:, 0:128], qs, s_k[po:po + 64, fc * TILE: fc * TILE + 128],
                        start=True, stop=True, tp=(po, 0), waits=wts)
            elif w == 0:
                tt = mm(M, bankS[:, 0:128], qs, s_kprev[po:po + 64, fc * 128:(fc + 1) * 128],
                        start=True, stop=True, tp=(po, 0), waits=wts)
                tt = mm(M, bankS[:, 128:256], qs, s_k[po:po + 64, fc * TILE: fc * TILE + 128],
                        start=True, stop=True, tp=(po, 0))
            else:
                tt = mm(M, bankS[:, 0:256], qs,
                        s_k[po:po + 64, fc * TILE + (w - 1) * 128: fc * TILE + (w + 1) * 128],
                        start=True, stop=True, tp=(po, 0), waits=wts)
            tokA[i] = tt

        def stB(i):
            w, h, fw = ctx[i]
            nk = 128 if fw else 256
            wh = i % 4
            msk = s_mask0 if fw else s_mask
            bankS = pb[wh]
            M.emit("vector",
                   lambda E, wh=wh, nk=nk, msk=msk, bankS=bankS:
                   E.tensor_tensor(out=s_sm[:, wh * 256: wh * 256 + nk], in0=bankS[:, 0:nk],
                                   in1=msk[:, 0:nk], op=ALU.add),
                   waits=[("T", tokA[i]), ("S", tokC[i - 4] if i >= 4 else 0)], inc="V")
            tokB[i] = M.emit("vector",
                             lambda E, wh=wh, nk=nk: E.reduce_max(out=s_stat[:, NEGMAX + wh:NEGMAX + wh + 1],
                                                                  in_=s_sm[:, wh * 256: wh * 256 + nk],
                                                                  axis=AX.X, negate=True),
                             inc="V")

        def stC(i):
            w, h, fw = ctx[i]
            nk = 128 if fw else 256
            wh = i % 4
            tokC[i] = M.emit("scalar",
                             lambda E, wh=wh, nk=nk: E.activation(out=s_probs[:, wh * 256: wh * 256 + nk],
                                                                  in_=s_sm[:, wh * 256: wh * 256 + nk],
                                                                  func=AF.Exp,
                                                                  bias=s_stat[:, NEGMAX + wh:NEGMAX + wh + 1],
                                                                  accum_out=s_stat[:, ROWSUM + wh:ROWSUM + wh + 1]),
                             waits=[("V", tokB[i]), ("T", tokE[i - 4] if i >= 4 else 0)], inc="S")

        def stD(i):
            w, h, fw = ctx[i]
            nk = 128 if fw else 256
            wh = i % 4
            M.emit("vector",
                   lambda E, wh=wh: E.reciprocal(out=s_stat[:, RECIP + wh:RECIP + wh + 1],
                                                 in_=s_stat[:, ROWSUM + wh:ROWSUM + wh + 1]),
                   waits=[("S", tokC[i])], inc="V")
            tokD[i] = M.emit("vector",
                             lambda E, wh=wh, nk=nk: E.tensor_scalar_mul(s_probs[:, wh * 256: wh * 256 + nk],
                                                                         s_probs[:, wh * 256: wh * 256 + nk],
                                                                         s_stat[:, RECIP + wh:RECIP + wh + 1]),
                             inc="V")

        def stE(i):
            w, h, fw = ctx[i]
            nk = 128 if fw else 256
            wh = i % 4
            bankT = pb[4 + i % 2]
            tt2 = None
            for kc in range(nk // 128):
                tt2 = trans(M, bankT[:, kc * 128:(kc + 1) * 128],
                            s_probs[:, wh * 256 + kc * 128: wh * 256 + (kc + 1) * 128],
                            s_identf[:],
                            waits=[("V", tokD[i]), ("S", tokF[i - 2] if i >= 2 else 0)])
            tokE[i] = tt2

        def stF(i):
            w, h, fw = ctx[i]
            nk = 128 if fw else 256
            wh = i % 4
            bankT = pb[4 + i % 2]
            tokF[i] = M.emit("scalar",
                             lambda E, wh=wh, nk=nk, bankT=bankT:
                             E.copy(out=s_pT[:, wh * 256: wh * 256 + nk], in_=bankT[:, 0:nk]),
                             waits=[("T", tokE[i])], inc="S")

        def stG(i):
            w, h, fw = ctx[i]
            wh = i % 4
            po = 64 * (h % 2)
            bankO = pb[6 + i % 2]
            pvw = [("S", tokF[i]), ("V", v_ev[w])]
            if i >= 2:
                pvw.append((("V", "S")[engH[i - 2]], tokH[i - 2]))
            if fw:
                tt3 = mm(M, bankO[po:po + 64, 0:128],
                         s_v[:, w * D + h * DH: w * D + (h + 1) * DH],
                         s_pT[:, wh * 256: wh * 256 + 128],
                         start=True, stop=True, tp=(0, po), waits=pvw)
            else:
                vprev_ap = (s_vprev[:, h * DH:(h + 1) * DH] if w == 0
                            else s_v[:, (w - 1) * D + h * DH: (w - 1) * D + h * DH + DH])
                tt3 = mm(M, bankO[po:po + 64, 0:128],
                         vprev_ap, s_pT[:, wh * 256: wh * 256 + 128],
                         start=True, stop=False, tp=(0, po), waits=pvw)
                tt3 = mm(M, bankO[po:po + 64, 0:128],
                         s_v[:, w * D + h * DH: w * D + (h + 1) * DH],
                         s_pT[:, wh * 256 + 128: wh * 256 + 256],
                         start=False, stop=True, tp=(0, po))
            tokG[i] = tt3

        def stH(i):
            w, h, fw = ctx[i]
            po = 64 * (h % 2); fc = h // 2
            bankO = pb[6 + i % 2]
            if i % 2 == 0:
                tokH[i] = M.emit("vector",
                                 lambda E, po=po, fc=fc, w=w, bankO=bankO:
                                 E.tensor_copy(out=s_ao[po:po + 64, fc * TILE + w * 128: fc * TILE + (w + 1) * 128],
                                               in_=bankO[po:po + 64, 0:128]),
                                 waits=[("T", tokG[i])], inc="V")
                engH[i] = 0
            else:
                tokH[i] = M.emit("scalar",
                                 lambda E, po=po, fc=fc, w=w, bankO=bankO:
                                 E.copy(out=s_ao[po:po + 64, fc * TILE + w * 128: fc * TILE + (w + 1) * 128],
                                        in_=bankO[po:po + 64, 0:128]),
                                 waits=[("T", tokG[i])], inc="S")
                engH[i] = 1

        stages = (stA, stB, stC, stD, stE, stF, stG, stH)
        for j in range(NI + len(stages) - 1):
            for k, st in enumerate(stages):
                i = j - k
                if 0 <= i < NI:
                    st(i)
        last_i = NI - 1
        pv_ev_last = M.emit("vector", lambda E: E.memset(s_stat[:, 60:61], 0.0),
                            waits=[(("V", "S")[engH[last_i]], tokH[last_i]),
                                   (("V", "S")[engH[last_i - 1]], tokH[last_i - 1])], inc="V")

        if (t + 1) % SEQ_TILES != 0:
            kp = M.emit("vector",
                        lambda E: E.tensor_copy(
                            out=s_kprev[:].rearrange("p (f c) -> p f c", c=128),
                            in_=s_k[:].rearrange("p (f t) -> p f t", t=TILE)[:, :, TILE - 128:TILE]),
                        waits=[("S", qk_ev[-1])], inc="V")
            vp = M.emit("vector",
                        lambda E: E.tensor_copy(out=s_vprev[:], in_=s_v[:, (G - 1) * D: G * D]),
                        waits=[("V", v_ev[-1])], inc="V")

        if dbg and t == 0:
            M.fence()
            M.emit("sync", lambda E: E.nop(), waits=[(k, cnt.get(k, 0)) for k in ("T", "V", "S")])
            dma(M, "sync", d_dbg["ao"], s_ao[:], "dc")
            for _e in ("vector", "scalar", "tensor", "gpsimd"):
                M.emit(_e, lambda E: E.nop(), waits=[("dc", cnt["dc"])])
        M.fence()
        res_toks = []
        for ic in range(DC):
            sl = ic % 2
            tok_w = M.emit("gpsimd",
                           lambda E, ic=ic, sl=sl: E.dma_start(out=s_woutb[:, sl * 1024:(sl + 1) * 1024],
                                                               in_=d_woutb[ic * 128:(ic + 1) * 128, :]),
                           waits=[("T", wout_cons[sl])], inc="wo%d" % sl, n=16)
            tt = None
            for g in range(G):
                for half in range(2):
                    tt = mm(M, pb[2 * g + half][:],
                            s_ao[:, ic * TILE + g * 128: ic * TILE + (g + 1) * 128],
                            s_woutb[:, sl * 1024 + half * 512: sl * 1024 + (half + 1) * 512],
                            start=(ic == 0), stop=(ic == DC - 1),
                            waits=[("wo%d" % sl, tok_w), ("V", pv_ev_last)])
            wout_cons[sl] = tt
        tok_T = M.emit("tensor", lambda E: E.nop(), inc="T")
        tvlast = None
        for g in range(G):
            for half in range(2):
                tvlast = M.emit("vector",
                                lambda E, g=g, half=half:
                                E.scalar_tensor_tensor(out=s_x[:, g * D + half * 512: g * D + (half + 1) * 512],
                                                       in0=pb[2 * g + half][:], scalar=1.0,
                                                       in1=s_x[:, g * D + half * 512: g * D + (half + 1) * 512],
                                                       op0=ALU.mult, op1=ALU.add),
                                waits=[("T", tok_T)], inc="V")

        if dbg and t == 0:
            M.fence()
            M.emit("sync", lambda E: E.nop(), waits=[(k, cnt.get(k, 0)) for k in ("T", "V", "S")])
            dma(M, "sync", d_dbg["x1"], s_x[:], "dc")
            for _e in ("vector", "scalar", "tensor", "gpsimd"):
                M.emit(_e, lambda E: E.nop(), waits=[("dc", cnt["dc"])])
        M.fence()
        tv, ts = stats_and_norm(tvlast, "V", R1)
        xnb_toks = []
        for g in range(G):
            tok = M.emit("vector",
                         lambda E, g=g: E.scalar_tensor_tensor(out=s_stat[:, NMA + g:NMA + g + 1],
                                                               in0=s_stat[:, MU + g:MU + g + 1], scalar=-1.0,
                                                               in1=s_stat[:, R1 + g:R1 + g + 1],
                                                               op0=ALU.mult, op1=ALU.mult),
                         waits=[("S", ts)], inc="V")
        tok_xnfm2 = emit_norm_transpose(M, R1, NMA)
        ev_tr = [tok_xnfm2]

        M.fence()
        h_toks = []
        for c in range(HC):
            sl = c % 2
            tok_w = M.emit("gpsimd",
                           lambda E, c=c, sl=sl: E.dma_start(out=s_wfcb[:, sl * 1024:(sl + 1) * 1024],
                                                             in_=d_wfcT[c * 128:(c + 1) * 128, :]),
                           waits=[("T", wfc_cons[sl])], inc="wf%d" % sl, n=16)
            bank = pb[2 + sl]
            tt = None
            for dc in range(DC):
                tt = mm(M, bank[:],
                        s_wfcb[:, sl * 1024 + dc * 128: sl * 1024 + (dc + 1) * 128],
                        s_xnfm[:, dc * TILE:(dc + 1) * TILE],
                        start=(dc == 0), stop=(dc == DC - 1),
                        waits=[("wf%d" % sl, tok_w), ("S", ev_tr[-1]),
                               ("S", h_toks[c - 2] if c >= 2 else 0)])
            wfc_cons[sl] = tt
            tok = M.emit("scalar",
                         lambda E, c=c, bank=bank: E.activation(out=s_h[:, c * TILE:(c + 1) * TILE],
                                                                in_=bank[:], func=AF.Gelu,
                                                                bias=s_bfc[:, c:c + 1]),
                         waits=[("T", tt)], inc="S")
            h_toks.append(tok)

        if dbg and t == 0:
            M.fence()
            M.emit("sync", lambda E: E.nop(), waits=[(k, cnt.get(k, 0)) for k in ("T", "V", "S")])
            dma(M, "sync", d_dbg["hd"], s_h[:], "dc")
            for _e in ("vector", "scalar", "tensor", "gpsimd"):
                M.emit(_e, lambda E: E.nop(), waits=[("dc", cnt["dc"])])
        M.fence()
        last_res = None
        for do in range(DC):
            sl = do % 2
            tok_w = M.emit("gpsimd",
                           lambda E, do=do, sl=sl: E.dma_start(out=s_wprb[:, sl * 4096:(sl + 1) * 4096],
                                                               in_=d_wprT[do * 128:(do + 1) * 128, :]),
                           waits=[("T", wpr_cons[sl])], inc="wp%d" % sl, n=16)
            bank = pb[0 + sl]
            tt = None
            for c in range(HC):
                tt = mm(M, bank[:],
                        s_wprb[:, sl * 4096 + c * 128: sl * 4096 + (c + 1) * 128],
                        s_h[:, c * TILE:(c + 1) * TILE],
                        start=(c == 0), stop=(c == HC - 1),
                        waits=[("wp%d" % sl, tok_w), ("S", h_toks[-1]),
                               ("V", ofm_cons[sl])])
            wpr_cons[sl] = tt
            tok_ev = M.emit("scalar",
                            lambda E, bank=bank, do=do:
                            E.activation(out=s_ofm[:, 0:512], in_=bank[:],
                                         func=AF.Identity, bias=s_bpr[:, do:do + 1]),
                            waits=[("T", tt), ("T", ofm_trans_last[0])], inc="S")
            bank2 = pb[4 + sl]
            tt2 = None
            for g in range(G):
                tt2 = trans(M, bank2[:, g * 128:(g + 1) * 128],
                            s_ofm[:, g * 128:(g + 1) * 128], s_identf[:],
                            waits=[("S", tok_ev), ("V", ofm_cons[sl])])
            ofm_trans_last[0] = tt2
            tok_res = M.emit("vector",
                             lambda E, do=do, bank2=bank2:
                             E.scalar_tensor_tensor(
                                 out=s_x[:].rearrange("p (g d) -> p g d", d=D)[:, :, do * 128:(do + 1) * 128],
                                 in0=bank2[:].rearrange("p (g c) -> p g c", c=128), scalar=1.0,
                                 in1=s_x[:].rearrange("p (g d) -> p g d", d=D)[:, :, do * 128:(do + 1) * 128],
                                 op0=ALU.mult, op1=ALU.add),
                             waits=[("T", tt2)], inc="V")
            ofm_cons[sl] = tok_res
            last_res = tok_res

        tok_store = M.emit("sync",
                           lambda E, t=t: E.dma_start(
                               out=d_y[t * TILE:(t + 1) * TILE, :].rearrange("(g p) d -> p g d", p=128),
                               in_=s_x[:].rearrange("p (g d) -> p g d", d=D)),
                           waits=[("V", last_res)], inc="dy", n=16)
        M.fence()
        M.emit("sync", lambda E: E.nop(), waits=[("dy", tok_store)])

    M.emit("sync", lambda E: E.nop(), waits=[("dy", cnt["dy"])])

    with nc.Block("main") as b2:
        M.replay(b2, sems)

    stack.close()
    return nc


_NC_CACHE = None


def kernel(L0_attn_x, L0_attn_ln_gamma, L0_attn_ln_beta, L0_attn_w_qkv, L0_attn_w_out,
           L0_ffn_W_fc, L0_ffn_b_fc, L0_ffn_A_fc, L0_ffn_B_fc,
           L0_ffn_W_proj, L0_ffn_b_proj, L0_ffn_A_proj, L0_ffn_B_proj):
    global _NC_CACHE
    x = np.ascontiguousarray(np.asarray(L0_attn_x, dtype=np.float32))
    B, N, Dm = x.shape
    ncores = 8
    per = B // ncores

    if _NC_CACHE is None:
        _NC_CACHE = build_program(dbg=DBG)
    nc = _NC_CACHE

    ident_f = np.eye(128, dtype=np.float32)
    ii = np.arange(128)[:, None]
    jj = np.arange(256)[None, :]
    prev = jj < 128
    jjm = np.where(prev, jj, jj - 128)
    bad = np.where(prev, jjm < ii, jjm > ii)
    mask_m = np.where(bad, np.float32(NEG), np.float32(0.0)).astype(BFNP)
    mask_0 = np.where(np.arange(128)[None, :] > ii, np.float32(NEG), np.float32(0.0)).astype(BFNP)
    ones_f = np.ones((1, 128), dtype=np.float32)

    common = dict(
        gamma=np.asarray(L0_attn_ln_gamma, np.float32),
        beta=np.asarray(L0_attn_ln_beta, np.float32),
        wqkv=np.asarray(L0_attn_w_qkv, np.float32),
        wout=np.asarray(L0_attn_w_out, np.float32),
        wfc=np.asarray(L0_ffn_W_fc, np.float32),
        bfc=np.asarray(L0_ffn_b_fc, np.float32),
        afc=np.asarray(L0_ffn_A_fc, np.float32),
        bfcB=np.asarray(L0_ffn_B_fc, np.float32),
        wpr=np.asarray(L0_ffn_W_proj, np.float32),
        bpr=np.asarray(L0_ffn_b_proj, np.float32),
        apr=np.asarray(L0_ffn_A_proj, np.float32),
        bprB=np.asarray(L0_ffn_B_proj, np.float32),
        identf=ident_f, maskm=mask_m, mask0=mask_0, onesf=ones_f,
        onesb=np.ones((1, 128), dtype=BFNP),
    )
    in_maps = []
    for c in range(ncores):
        m = dict(common)
        m["x"] = np.ascontiguousarray(x[c * per:(c + 1) * per].reshape(TOK, Dm))
        in_maps.append(m)

    res = run_bass_kernel_spmd(nc, in_maps, core_ids=list(range(ncores)))
    out = np.empty((B, N, Dm), np.float32)
    for c in range(ncores):
        out[c * per:(c + 1) * per] = res.results[c]["y"].reshape(per, N, Dm)
    if DBG:
        global DBG_RES
        DBG_RES = res.results
    return out


# version 13
# speedup vs baseline: 1.4389x; 1.0108x over previous
import numpy as np
import ml_dtypes
from contextlib import ExitStack

import concourse.bass as bass
import concourse.mybir as mybir
from concourse.bass_utils import run_bass_kernel_spmd

F32 = mybir.dt.float32
BF16 = mybir.dt.bfloat16
I32 = mybir.dt.int32
AF = mybir.ActivationFunctionType
ALU = mybir.AluOpType
AX = mybir.AxisListType
BFNP = ml_dtypes.bfloat16

D = 1024
TOK = 4096
TILE = 512
NT = TOK // TILE
G = TILE // 128
DC = D // 128
H = 16
DH = 64
FFH = 4096
HC = FFH // 128
NEG = -1.0e30
SEQ_TILES = 2048 // TILE

ENGS = ("sync", "tensor", "vector", "scalar", "gpsimd")


class Prog:

    def __init__(self, cnt):
        self.ops = {e: [] for e in ENGS}
        self.cnt = cnt
        self.last_wait = {e: {} for e in ENGS}
        self.fence_snap = {}

    def emit(self, eng, fn, waits=(), inc=None, n=1):
        own = _OWN.get(eng)
        merged = dict(self.fence_snap)
        if own in ("V", "S"):
            if inc is None:
                inc = own
            merged[own] = self.cnt.get(own, 0)
        for s, v in waits:
            if v > merged.get(s, 0):
                merged[s] = v
        w = []
        for s, v in merged.items():
            if s == "T" and eng == "tensor":
                continue
            if v > self.last_wait[eng].get(s, 0):
                self.last_wait[eng][s] = v
                w.append((s, v))
        self.ops[eng].append((fn, w, inc, n))
        if inc is not None:
            self.cnt[inc] = self.cnt.get(inc, 0) + n
            return self.cnt[inc]
        return None

    def fence(self):
        self.fence_snap = {s: self.cnt.get(s, 0) for s in ("T", "V", "S")}

    def replay(self, block, sems):
        for ename in ENGS:
            ops = self.ops[ename]

            def body(eng, ops=ops):
                for fn, w, inc, n in ops:
                    for s, v in w:
                        eng.wait_ge(sems[s], v)
                    instr = fn(eng)
                    if inc is not None:
                        instr.then_inc(sems[inc], n)
            getattr(block, ename)(body)


_OWN = {"tensor": "T", "vector": "V", "scalar": "S"}


DBG = False


def build_program(dbg=False):
    nc = bass.Bass("TRN2", target_bir_lowering=False, detect_race_conditions=False)

    d_x = nc.dram_tensor("x", [TOK, D], F32, kind="ExternalInput").ap()
    d_gamma = nc.dram_tensor("gamma", [D], F32, kind="ExternalInput").ap()
    d_beta = nc.dram_tensor("beta", [D], F32, kind="ExternalInput").ap()
    d_wqkv = nc.dram_tensor("wqkv", [D, 3 * D], F32, kind="ExternalInput").ap()
    d_wout = nc.dram_tensor("wout", [D, D], F32, kind="ExternalInput").ap()
    d_wfc = nc.dram_tensor("wfc", [FFH, D], F32, kind="ExternalInput").ap()
    d_bfc = nc.dram_tensor("bfc", [FFH], F32, kind="ExternalInput").ap()
    d_afc = nc.dram_tensor("afc", [16, D], F32, kind="ExternalInput").ap()
    d_bfcB = nc.dram_tensor("bfcB", [FFH, 16], F32, kind="ExternalInput").ap()
    d_wpr = nc.dram_tensor("wpr", [D, FFH], F32, kind="ExternalInput").ap()
    d_bpr = nc.dram_tensor("bpr", [D], F32, kind="ExternalInput").ap()
    d_apr = nc.dram_tensor("apr", [16, FFH], F32, kind="ExternalInput").ap()
    d_bprB = nc.dram_tensor("bprB", [D, 16], F32, kind="ExternalInput").ap()
    d_identf = nc.dram_tensor("identf", [128, 128], F32, kind="ExternalInput").ap()
    d_mask = nc.dram_tensor("maskm", [128, 256], BF16, kind="ExternalInput").ap()
    d_mask0 = nc.dram_tensor("mask0", [128, 128], BF16, kind="ExternalInput").ap()
    d_onesf = nc.dram_tensor("onesf", [1, 128], F32, kind="ExternalInput").ap()
    d_onesb = nc.dram_tensor("onesb", [1, 128], BF16, kind="ExternalInput").ap()

    d_y = nc.dram_tensor("y", [TOK, D], F32, kind="ExternalOutput").ap()
    d_dbg = {}
    if dbg:
        for name, shape, dt in [("wall", [128, 24576], BF16), ("bw", [128, 16], F32),
                                ("bwv", [1, 1024], F32), ("scl", [128, 8], F32),
                                ("btfc", [16, FFH], BF16), ("a2fc", [16, D], BF16),
                                ("stat", [128, 64], F32), ("xnfm", [128, 4096], BF16),
                                ("qd", [128, 4096], BF16), ("kd", [128, 4096], BF16),
                                ("vd", [128, 4096], BF16), ("ao", [128, 4096], BF16),
                                ("x1", [128, 4096], F32), ("hd", [128, HC * TILE], BF16),
                                ("wfcT0", [128, 1024], BF16), ("wprT0", [128, 4096], BF16)]:
            d_dbg[name] = nc.dram_tensor("dbg_" + name, shape, dt, kind="ExternalOutput").ap()

    d_wfcT = nc.dram_tensor("wfcT_scr", [FFH, D], BF16, kind="Internal").ap()
    d_wprT = nc.dram_tensor("wprT_scr", [D, FFH], BF16, kind="Internal").ap()
    d_woutb = nc.dram_tensor("wout_scr", [D, D], BF16, kind="Internal").ap()

    cnt = {}
    stack = ExitStack()
    e = stack.enter_context

    sems = {}
    for s in ("dc", "da", "db", "dwa", "dwb", "dx", "dy",
              "wo0", "wo1", "wf0", "wf1", "wp0", "wp1", "T", "V", "S", "BAR"):
        sems[s] = e(nc.semaphore(s))

    pb = [e(nc.psum_tensor(f"pb{i}", [128, 512], F32)) for i in range(8)]

    s_identf = e(nc.sbuf_tensor("sbident_f", [128, 128], F32))
    s_mask = e(nc.sbuf_tensor("sb_maskm", [128, 256], BF16))
    s_mask0 = e(nc.sbuf_tensor("sb_mask0", [128, 128], BF16))
    s_onesf = e(nc.sbuf_tensor("sb_onesf", [1, 128], F32))
    s_gcol = e(nc.sbuf_tensor("gcol", [128, 8], F32))
    s_g8col = e(nc.sbuf_tensor("g8col", [128, 8], F32))
    s_betab = e(nc.sbuf_tensor("betab", [128, 8], BF16))
    s_bw = e(nc.sbuf_tensor("bw", [128, 16], F32))
    s_bwv = e(nc.sbuf_tensor("bwv", [1, 1024], BF16))
    s_onesb = e(nc.sbuf_tensor("sb_onesb", [1, 128], BF16))
    s_bfc = e(nc.sbuf_tensor("bfccol", [128, 32], F32))
    s_bpr = e(nc.sbuf_tensor("bprcol", [128, 8], F32))
    s_cm05 = e(nc.sbuf_tensor("cm05", [128, 1], F32))
    s_cp05 = e(nc.sbuf_tensor("cp05", [128, 1], F32))
    s_scfc = e(nc.sbuf_tensor("scfc", [128, 1], F32))
    s_rsfc = e(nc.sbuf_tensor("rsfc", [128, 1], F32))
    s_scpr = e(nc.sbuf_tensor("scpr", [128, 1], F32))
    s_rspr = e(nc.sbuf_tensor("rspr", [128, 1], F32))
    s_am = e(nc.sbuf_tensor("amcols", [128, 32], F32))
    s_hsfc = e(nc.sbuf_tensor("hsfc", [128, 1], F32))
    s_hspr = e(nc.sbuf_tensor("hspr", [128, 1], F32))
    s_amc = e(nc.sbuf_tensor("amc", [128, 1], F32))
    s_amr = e(nc.sbuf_tensor("amr", [1, 128], F32))
    s_sc1 = e(nc.sbuf_tensor("sc1", [1, 1], F32))

    s_wall = e(nc.sbuf_tensor("wall", [128, DC * 3 * D], BF16))

    s_a2fc = e(nc.sbuf_tensor("a2fc", [16, D], BF16))
    s_btfc = e(nc.sbuf_tensor("btfc", [16, FFH], BF16))
    s_a2pr = e(nc.sbuf_tensor("a2pr", [16, FFH], BF16))
    s_btpr = e(nc.sbuf_tensor("btpr", [16, D], BF16))

    P = Prog(cnt)

    def dma(prog, eng, out_ap, in_ap, sem, slow=False):
        if slow:
            return prog.emit(eng, lambda E, o=out_ap, i=in_ap: E.dma_start(out=o, in_=i, allow_slow_non_contiguous=True), inc=sem, n=16)
        return prog.emit(eng, lambda E, o=out_ap, i=in_ap: E.dma_start(out=o, in_=i), inc=sem, n=16)

    def mm(prog, out_ap, lhsT, rhs, start, stop, tp=None, waits=()):
        def f(E, o=out_ap, l=lhsT, r=rhs):
            return E.matmul(o, l, r, start=start, stop=stop, tile_position=tp,
                            skip_group_check=True)
        return prog.emit("tensor", f, waits=waits, inc="T")

    def trans(prog, out_ap, in_ap, ident, start=True, stop=True, waits=()):
        def f(E, o=out_ap, i=in_ap, idn=ident):
            return E.matmul(o, i, idn, start=start, stop=stop, is_transpose=True,
                            skip_group_check=True)
        return prog.emit("tensor", f, waits=waits, inc="T")

    with ExitStack() as arena:
        a = arena.enter_context
        t_stage = a(nc.sbuf_tensor("stage", [128, 2 * FFH], F32))
        t_q = a(nc.sbuf_tensor("tq", [128, 2048], F32))
        t_k0 = a(nc.sbuf_tensor("tk0", [128, 2048], I32))
        t_d = a(nc.sbuf_tensor("td", [128, 2048], F32))
        t_s1 = a(nc.sbuf_tensor("ts1", [128, 2048], F32))
        t_s2 = a(nc.sbuf_tensor("ts2", [128, 2048], F32))
        t_wqf = a(nc.sbuf_tensor("twqf", [128, 2 * FFH], F32))
        t_panel = a(nc.sbuf_tensor("tpanel", [128, 2 * FFH], BF16))
        t_small = a(nc.sbuf_tensor("tsmall", [128, 16], F32))

        tok_dx = 0
        tok_dx = dma(P, "sync", s_identf[:], d_identf, "dc")
        tok_dx = dma(P, "sync", s_mask[:], d_mask, "dc")
        tok_dx = dma(P, "sync", s_mask0[:], d_mask0, "dc")
        tok_dx = dma(P, "sync", s_onesf[:], d_onesf, "dc")
        tok_dx = dma(P, "sync", s_onesb[:], d_onesb, "dc")
        tok_dx = dma(P, "sync", s_gcol[:], d_gamma.rearrange("(c p) -> p c", p=128), "dc", slow=True)
        tok_dx = dma(P, "sync", t_small[:, 0:8], d_beta.rearrange("(c p) -> p c", p=128), "dc", slow=True)
        tok_dx = dma(P, "sync", s_bfc[:], d_bfc.rearrange("(c p) -> p c", p=128), "dc", slow=True)
        tok_dx = dma(P, "sync", s_bpr[:], d_bpr.rearrange("(c p) -> p c", p=128), "dc", slow=True)
        tok_consts = tok_dx
        P.emit("vector", lambda E: E.memset(s_cm05[:], -0.5))
        P.emit("vector", lambda E: E.memset(s_cp05[:], 0.5))
        P.emit("vector", lambda E: E.tensor_scalar_mul(s_g8col[:], s_gcol[:], 0.125),
               waits=[("dc", tok_consts)])
        tv = P.emit("vector", lambda E: E.tensor_copy(out=s_betab[:], in_=t_small[:, 0:8]),
                    waits=[("dc", tok_consts)], inc="V")

        cast_toks = []
        for dc in range(DC):
            w = [("S", cast_toks[dc - 2])] if dc >= 2 else []
            sem = "da" if dc % 2 == 0 else "db"
            tok_l = P.emit("sync",
                           lambda E, dc=dc: E.dma_start(out=t_stage[:, (dc % 2) * 3072:(dc % 2) * 3072 + 3072],
                                                        in_=d_wqkv[dc * 128:(dc + 1) * 128, :]),
                           waits=w, inc=sem, n=16)
            tok = P.emit("scalar",
                         lambda E, dc=dc: E.copy(out=s_wall[:, dc * 3072:(dc + 1) * 3072],
                                                 in_=t_stage[:, (dc % 2) * 3072:(dc % 2) * 3072 + 3072]),
                         waits=[(sem, tok_l)], inc="S")
            cast_toks.append(tok)

        P.fence()
        for oc in range(16):
            for dc in range(DC):
                mm(P, pb[0][:, oc:oc + 1],
                   s_wall[:, dc * 3072 + oc * 128: dc * 3072 + (oc + 1) * 128],
                   s_betab[:, dc:dc + 1],
                   start=(dc == 0), stop=(dc == DC - 1),
                   waits=[("S", cast_toks[-1]), ("V", tv)])
        tok_T = P.emit("tensor", lambda E: E.nop(), inc="T")
        for half in range(2):
            for dc in range(DC):
                mm(P, pb[1 + half][0:1, 0:512],
                   s_betab[:, dc:dc + 1],
                   s_wall[:, dc * 3072 + 2048 + half * 512: dc * 3072 + 2048 + (half + 1) * 512],
                   start=(dc == 0), stop=(dc == DC - 1))
        tok_T = P.emit("tensor", lambda E: E.nop(), inc="T")
        tok_bw = P.emit("vector", lambda E: E.tensor_copy(out=s_bw[:], in_=pb[0][:, 0:16]),
                        waits=[("T", tok_T)], inc="V")
        tok_bw = P.emit("vector", lambda E: E.tensor_scalar_mul(s_bw[:, 0:8], s_bw[:, 0:8], 0.125),
                        inc="V")
        for half in range(2):
            tok_bwv = P.emit("vector",
                             lambda E, half=half: E.tensor_copy(out=s_bwv[0:1, half * 512:(half + 1) * 512],
                                                                in_=pb[1 + half][0:1, 0:512]),
                             waits=[("T", tok_T)], inc="V")
        fold_toks = []
        for dc in range(DC):
            P.emit("scalar",
                   lambda E, dc=dc: E.activation(out=s_wall[:, dc * 3072:dc * 3072 + 1024],
                                                 in_=s_wall[:, dc * 3072:dc * 3072 + 1024],
                                                 func=AF.Copy, scale=s_g8col[:, dc:dc + 1]),
                   waits=[("V", tok_bwv)])
            tok = P.emit("scalar",
                         lambda E, dc=dc: E.activation(out=s_wall[:, dc * 3072 + 1024:dc * 3072 + 3072],
                                                       in_=s_wall[:, dc * 3072 + 1024:dc * 3072 + 3072],
                                                       func=AF.Copy, scale=s_gcol[:, dc:dc + 1]),
                         inc="S")
            fold_toks.append(tok)

        P.fence()
        st_toks = [0, 0]
        cast_tk = [0, 0]
        for ic in range(DC):
            sem = "da" if ic % 2 == 0 else "db"
            ssem = "dwa" if ic % 2 == 0 else "dwb"
            tok_l = P.emit("sync",
                           lambda E, ic=ic: E.dma_start(out=t_stage[:, (ic % 2) * 4096:(ic % 2) * 4096 + 1024],
                                                        in_=d_wout[ic * 128:(ic + 1) * 128, :]),
                           waits=[("V", cast_tk[ic % 2])], inc=sem, n=16)
            tok_c = P.emit("vector",
                           lambda E, ic=ic: E.tensor_copy(out=t_panel[:, (ic % 2) * 1024:(ic % 2) * 1024 + 1024],
                                                          in_=t_stage[:, (ic % 2) * 4096:(ic % 2) * 4096 + 1024]),
                           waits=[(sem, tok_l), (ssem, st_toks[ic % 2])], inc="V")
            cast_tk[ic % 2] = tok_c
            st_toks[ic % 2] = P.emit(
                "gpsimd",
                lambda E, ic=ic: E.dma_start(out=d_woutb[ic * 128:(ic + 1) * 128, :],
                                             in_=t_panel[:, (ic % 2) * 1024:(ic % 2) * 1024 + 1024]),
                waits=[("V", tok_c)], inc=ssem, n=16)
        P.fence()

        tok_a = dma(P, "sync", t_stage[:16, 0:1024], d_afc, "da")
        tva = P.emit("vector", lambda E: E.tensor_scalar(out=s_a2fc[:], in0=t_stage[:16, 0:1024],
                                                         scalar1=2.0, scalar2=None, op0=ALU.mult),
                     waits=[("da", tok_a)], inc="V")
        tok_a = P.emit("sync", lambda E: E.dma_start(out=t_stage[:16, 0:4096], in_=d_apr),
                       waits=[("V", tva)], inc="da", n=16)
        tva = P.emit("vector", lambda E: E.tensor_scalar(out=s_a2pr[:], in0=t_stage[:16, 0:4096],
                                                         scalar1=2.0, scalar2=None, op0=ALU.mult),
                     waits=[("da", tok_a)], inc="V")
        P.fence()
        btr = [0, 0]
        bev = [0, 0]
        for c in range(HC):
            sem = "da" if c % 2 == 0 else "db"
            tok_l = P.emit("sync",
                           lambda E, c=c: E.dma_start(out=t_stage[:, (c % 2) * 4096:(c % 2) * 4096 + 16],
                                                      in_=d_bfcB[c * 128:(c + 1) * 128, :]),
                           waits=[("T", btr[c % 2])], inc=sem, n=16)
            tt = trans(P, pb[2 + c % 2][0:16, 0:128],
                       t_stage[:, (c % 2) * 4096:(c % 2) * 4096 + 16], s_identf[:],
                       waits=[(sem, tok_l), ("S", bev[c % 2])])
            btr[c % 2] = tt
            bev[c % 2] = P.emit("scalar",
                                lambda E, c=c: E.copy(out=s_btfc[:16, c * 128:(c + 1) * 128],
                                                      in_=pb[2 + c % 2][0:16, 0:128]),
                                waits=[("T", tt)], inc="S")
        P.fence()
        btr = [0, 0]
        bev = [0, 0]
        for c in range(DC):
            sem = "da" if c % 2 == 0 else "db"
            tok_l = P.emit("sync",
                           lambda E, c=c: E.dma_start(out=t_stage[:, (c % 2) * 4096:(c % 2) * 4096 + 16],
                                                      in_=d_bprB[c * 128:(c + 1) * 128, :]),
                           waits=[("T", btr[c % 2])], inc=sem, n=16)
            tt = trans(P, pb[2 + c % 2][0:16, 0:128],
                       t_stage[:, (c % 2) * 4096:(c % 2) * 4096 + 16], s_identf[:],
                       waits=[(sem, tok_l), ("S", bev[c % 2])])
            btr[c % 2] = tt
            bev[c % 2] = P.emit("scalar",
                                lambda E, c=c: E.copy(out=s_btpr[:16, c * 128:(c + 1) * 128],
                                                      in_=pb[2 + c % 2][0:16, 0:128]),
                                waits=[("T", tt)], inc="S")
        P.fence()

        def quant_weight(d_w, nchunk, free, s_scale, s_rscale, s_hscale, am_ncols, d_scrT, a2, bt,
                         panel_cols, tpos_cols):
            P.fence()
            red_tk = [0, 0]
            for c in range(nchunk):
                sl = (c % 2) * 4096
                sem = "da" if c % 2 == 0 else "db"
                tok_l = P.emit("sync",
                               lambda E, c=c, sl=sl: E.dma_start(out=t_stage[:, sl:sl + free],
                                                                 in_=d_w[c * 128:(c + 1) * 128, :]),
                               waits=[("V", red_tk[c % 2])], inc=sem, n=16)
                red_tk[c % 2] = P.emit("vector",
                                       lambda E, c=c, sl=sl: E.reduce_max(out=s_am[:, c:c + 1],
                                                                          in_=t_stage[:, sl:sl + free],
                                                                          axis=AX.X, apply_absolute_value=True),
                                       waits=[(sem, tok_l)], inc="V")
            P.fence()
            tok = P.emit("vector", lambda E: E.reduce_max(out=s_amc[:], in_=s_am[:, 0:am_ncols], axis=AX.X),
                         inc="V")
            tt = trans(P, pb[3][0:1, 0:128], s_amc[:], s_identf[:], waits=[("V", tok)])
            tok = P.emit("vector", lambda E: E.tensor_copy(out=s_amr[:], in_=pb[3][0:1, 0:128]),
                         waits=[("T", tt)], inc="V")
            tok = P.emit("vector", lambda E: E.reduce_max(out=s_sc1[:], in_=s_amr[:], axis=AX.X), inc="V")
            tok = P.emit("vector", lambda E: E.tensor_scalar_mul(s_sc1[:], s_sc1[:], float(np.float32(1.0) / np.float32(127.0))),
                         inc="V")
            tt = mm(P, pb[3][:, 0:1], s_onesf[:], s_sc1[:], start=True, stop=True,
                    waits=[("V", tok)])
            tok = P.emit("vector", lambda E: E.tensor_copy(out=s_scale[:], in_=pb[3][:, 0:1]),
                         waits=[("T", tt)], inc="V")
            tok = P.emit("vector", lambda E: E.reciprocal(out=s_rscale[:], in_=s_scale[:]), inc="V")
            tok = P.emit("vector", lambda E: E.tensor_scalar_mul(s_hscale[:], s_scale[:], 0.5), inc="V")
            P.fence()
            nsl = free // 1024
            tok_store = 0
            ev_bank = {}
            trans_last = 0
            tq_tk = [0, 0]
            gsl = [0]
            stt_tk = [0, 0]
            pend_tw = [None]
            twc = [0] * nchunk
            trans_c = [0] * nchunk
            ev_bank = {}
            store_tk = [0, 0]

            def flush_tw(prog):
                if pend_tw[0] is not None:
                    pf, pc, ptk, ptl = pend_tw[0]
                    tw = prog.emit("scalar", pf, waits=[("V", ptk), ("T", ptl)], inc="S")
                    twc[pc] = tw
                    pend_tw[0] = None

            def quant_slices(c):
                sl = (c % 2) * 4096
                wsl = (c % 2) * 4096
                sem = "da" if c % 2 == 0 else "db"
                tok_l = P.emit("sync",
                               lambda E, c=c, sl=sl: E.dma_start(out=t_stage[:, sl:sl + free],
                                                                 in_=d_w[c * 128:(c + 1) * 128, :]),
                               waits=[("S", tq_tk[c % 2])], inc=sem, n=16)
                tq = 0
                for s in range(nsl):
                    slot = gsl[0] % 2
                    q = slot * 1024
                    gsl[0] += 1
                    src = t_stage[:, sl + s * 1024: sl + (s + 1) * 1024]
                    tq = P.emit("scalar",
                                lambda E, src=src, q=q: E.activation(out=t_q[:, q:q + 1024], in_=src,
                                                                     func=AF.Copy, scale=s_rscale[:]),
                                waits=[(sem, tok_l), ("V", stt_tk[slot])], inc="S")
                    flush_tw(P)
                    tk = P.emit("vector",
                                lambda E, q=q: E.tensor_copy(out=t_k0[:, q:q + 1024], in_=t_q[:, q:q + 1024]),
                                waits=[("S", tq)], inc="V")
                    tk = P.emit("vector",
                                lambda E, q=q: E.tensor_tensor(out=t_d[:, q:q + 1024], in0=t_q[:, q:q + 1024],
                                                               in1=t_k0[:, q:q + 1024], op=ALU.subtract),
                                inc="V")
                    t1 = P.emit("scalar",
                                lambda E, q=q: E.activation(out=t_s1[:, q:q + 1024], in_=t_d[:, q:q + 1024],
                                                            func=AF.Sign, bias=s_cm05[:]),
                                waits=[("V", tk)], inc="S")
                    t1 = P.emit("scalar",
                                lambda E, q=q: E.activation(out=t_s2[:, q:q + 1024], in_=t_d[:, q:q + 1024],
                                                            func=AF.Sign, bias=s_cp05[:]), inc="S")
                    tk = P.emit("vector",
                                lambda E, q=q: E.scalar_tensor_tensor(out=t_s1[:, q:q + 1024],
                                                                      in0=t_k0[:, q:q + 1024], scalar=2.0,
                                                                      in1=t_s1[:, q:q + 1024],
                                                                      op0=ALU.mult, op1=ALU.add),
                                waits=[("S", t1)], inc="V")
                    tk = P.emit("vector",
                                lambda E, q=q: E.tensor_tensor(out=t_d[:, q:q + 1024], in0=t_s1[:, q:q + 1024],
                                                               in1=t_s2[:, q:q + 1024], op=ALU.add),
                                inc="V")
                    stt_tk[slot] = tk
                    def mk_tw(s=s, q=q, wsl=wsl):
                        return lambda E: E.activation(out=t_wqf[:, wsl + s * 1024: wsl + (s + 1) * 1024],
                                                      in_=t_d[:, q:q + 1024],
                                                      func=AF.Copy, scale=s_hscale[:])
                    pend_tw[0] = (mk_tw(), c, tk, trans_c[c - 2] if c >= 2 else 0)
                tq_tk[c % 2] = tq

            def trans_store(c):
                wsl = (c % 2) * 4096
                psl = (c % 2) * free
                nblk = free // 128
                ev_tok = None
                tt = None
                for b in range(nblk):
                    bi = 4 + ((b // 4) % 2)
                    bank = pb[bi]
                    tt = trans(P, bank[:, (b % 4) * 128:(b % 4) * 128 + 128],
                               t_wqf[:, wsl + b * 128: wsl + (b + 1) * 128], s_identf[:],
                               start=True, stop=False,
                               waits=[("S", twc[c]), ("S", ev_bank.get(bi, 0))])
                    tt = mm(P, bank[:, (b % 4) * 128:(b % 4) * 128 + 128],
                            a2[:16, b * 128:(b + 1) * 128],
                            bt[:16, c * 128:(c + 1) * 128],
                            start=False, stop=True)
                    if b % 4 == 3:
                        ev_tok = P.emit("scalar",
                                        lambda E, b=b, bank=bank, psl=psl:
                                        E.copy(out=t_panel[:, psl + (b - 3) * 128: psl + (b + 1) * 128],
                                               in_=bank[:]),
                                        waits=[("T", tt), ("dwa" if c % 2 == 0 else "dwb", store_tk[c % 2])],
                                        inc="S")
                        ev_bank[bi] = ev_tok
                trans_c[c] = tt
                store_tk[c % 2] = P.emit("gpsimd",
                                         lambda E, c=c, psl=psl:
                                         E.dma_start(out=d_scrT[c * 128:(c + 1) * 128, :],
                                                     in_=t_panel[:, psl:psl + free]),
                                         waits=[("S", ev_tok)],
                                         inc="dwa" if c % 2 == 0 else "dwb", n=16)

            for c in range(nchunk):
                quant_slices(c)
                flush_tw(P)
                if c >= 1:
                    trans_store(c - 1)
            trans_store(nchunk - 1)
            P.fence()

        quant_weight(d_wfc, HC, 1024, s_scfc, s_rsfc, s_hsfc, HC, d_wfcT, s_a2fc, s_btfc, 1024, None)
        quant_weight(d_wpr, DC, 4096, s_scpr, s_rspr, s_hspr, DC, d_wprT, s_a2pr, s_btpr, 4096, None)

        if dbg:
            P.fence()
            P.emit("sync", lambda E: E.nop(),
                   waits=[(k, cnt.get(k, 0)) for k in ("T", "V", "S")])
            dma(P, "sync", d_dbg["wall"], s_wall[:], "dc")
            dma(P, "sync", d_dbg["bw"], s_bw[:], "dc")
            dma(P, "sync", d_dbg["bwv"], s_bwv[:], "dc")
            P.emit("vector", lambda E: E.tensor_copy(out=s_am[:, 0:1], in_=s_scfc[:]))
            P.emit("vector", lambda E: E.tensor_copy(out=s_am[:, 1:2], in_=s_rsfc[:]))
            P.emit("vector", lambda E: E.tensor_copy(out=s_am[:, 2:3], in_=s_scpr[:]))
            tvd = P.emit("vector", lambda E: E.tensor_copy(out=s_am[:, 3:4], in_=s_rspr[:]), inc="V")
            dma(P, "sync", d_dbg["scl"], s_am[:, 0:8], "dc")
            dma(P, "sync", d_dbg["btfc"], s_btfc[:], "dc")
            dma(P, "sync", d_dbg["a2fc"], s_a2fc[:], "dc")
            dma(P, "sync", d_dbg["wfcT0"], d_wfcT[0:128, :], "dc")
            dma(P, "sync", d_dbg["wprT0"], d_wprT[0:128, :], "dc")
        for eng in ENGS:
            P.emit(eng, lambda E: E.nop(), inc="BAR")

        with nc.Block("setup") as b1:
            P.replay(b1, sems)

    s_x = e(nc.sbuf_tensor("xsb", [128, TILE // 128 * D], F32))
    s_xnb = e(nc.sbuf_tensor("xnb", [128, D], F32))
    s_xnfm = e(nc.sbuf_tensor("xnfm", [128, DC * TILE], BF16))
    s_q = e(nc.sbuf_tensor("qsb", [128, 8 * TILE], BF16))
    s_k = e(nc.sbuf_tensor("ksb", [128, 8 * TILE], BF16))
    s_v = e(nc.sbuf_tensor("vsb", [128, G * D], BF16))
    s_kprev = e(nc.sbuf_tensor("kprev", [128, 8 * 128], BF16))
    s_vprev = e(nc.sbuf_tensor("vprev", [128, D], BF16))
    s_ao = e(nc.sbuf_tensor("aosb", [128, 8 * TILE], BF16))
    s_h = e(nc.sbuf_tensor("hsb", [128, HC * TILE], BF16))
    s_sm = e(nc.sbuf_tensor("smbuf", [128, 4 * 256], F32))
    s_probs = e(nc.sbuf_tensor("probs", [128, 4 * 256], F32))
    s_pT = e(nc.sbuf_tensor("pT", [128, 4 * 256], BF16))
    s_stat = e(nc.sbuf_tensor("stat", [128, 64], F32))
    s_ofm = e(nc.sbuf_tensor("ofm", [128, 512], F32))
    s_wfcb = e(nc.sbuf_tensor("wfcb", [128, 2 * 1024], BF16))
    s_wprb = e(nc.sbuf_tensor("wprb", [128, 2 * 4096], BF16))
    s_woutb = e(nc.sbuf_tensor("woutb", [128, 2 * 1024], BF16))

    NEGMAX, ROWSUM, RECIP = 0, 4, 52
    SUM, SSQ, MU, MSQ, VARP, R1, V2P, ASC, NMA, T0 = 8, 12, 16, 20, 24, 28, 32, 36, 40, 44

    M = Prog(cnt)
    M.fence_snap = {}
    for eng in ENGS:
        M.emit(eng, lambda E: E.nop(), waits=[("BAR", cnt["BAR"])])

    tok_store = 0
    wout_cons = [0, 0]
    wfc_cons = [0, 0]
    wpr_cons = [0, 0]
    ofm_cons = [0, 0]
    ofm_trans_last = [0]
    xnb_cons = [0, 0]
    xnt_prev_ev = [0, 0]

    def emit_norm_transpose(M, a_col, nma_col):
        ev_last = None
        for g in range(G):
            sl = 0
            tok_x = M.emit("vector",
                           lambda E, g=g, sl=sl, a_col=a_col, nma_col=nma_col:
                           E.tensor_scalar(out=s_xnb[:, sl * D:(sl + 1) * D],
                                           in0=s_x[:, g * D:(g + 1) * D],
                                           scalar1=s_stat[:, a_col + g:a_col + g + 1],
                                           scalar2=s_stat[:, nma_col + g:nma_col + g + 1],
                                           op0=ALU.mult, op1=ALU.add),
                           waits=[("T", xnb_cons[sl])], inc="V")
            tt = None
            for half in range(2):
                bank = pb[half]
                for i in range(4):
                    dc = half * 4 + i
                    tt = trans(M, bank[:, i * 128:(i + 1) * 128],
                               s_xnb[:, sl * D + dc * 128: sl * D + (dc + 1) * 128],
                               s_identf[:],
                               waits=[("V", tok_x), ("S", xnt_prev_ev[half])])
                dst = bass.AP(s_xnfm, half * 4 * TILE + g * 128,
                              [[DC * TILE, 128], [TILE, 4], [1, 128]])
                tok = M.emit("scalar",
                             lambda E, bank=bank, dst=dst:
                             E.copy(out=dst, in_=bank[:].rearrange("p (a b) -> p a b", b=128)),
                             waits=[("T", tt)], inc="S")
                xnt_prev_ev[half] = tok
                ev_last = tok
            xnb_cons[sl] = tt
        return ev_last

    for t in range(NT):
        seq_start = (t % SEQ_TILES == 0)
        tok_x = dma(M, "sync", s_x[:].rearrange("p (g d) -> p g d", d=D),
                    d_x[t * TILE:(t + 1) * TILE, :].rearrange("(g p) d -> p g d", p=128),
                    "dx")

        def stats_and_norm(wait_tok, wait_sem, r_out_cols):
            tv = ts = None
            for g in range(G):
                xg = s_x[:, g * D:(g + 1) * D]
                tv = M.emit("vector",
                            lambda E, xg=xg, g=g: E.tensor_scalar(out=s_xnfm[:, 0:1024], in0=xg, scalar1=1.0,
                                                                  scalar2=None, op0=ALU.mult, op1=ALU.add,
                                                                  accum_out=s_stat[:, SUM + g:SUM + g + 1]),
                            waits=[(wait_sem, wait_tok)], inc="V")
                ts = M.emit("scalar",
                            lambda E, xg=xg, g=g: E.activation(out=s_xnfm[:, 1024:2048], in_=xg, func=AF.Square,
                                                               accum_out=s_stat[:, SSQ + g:SSQ + g + 1]),
                            waits=[(wait_sem, wait_tok)], inc="S")
            c = 1.0 / 1024.0
            tv = M.emit("vector", lambda E: E.tensor_scalar_mul(s_stat[:, MU:MU + 4], s_stat[:, SUM:SUM + 4], c),
                        waits=[("V", tv)], inc="V")
            tv = M.emit("vector", lambda E: E.tensor_scalar_mul(s_stat[:, MSQ:MSQ + 4], s_stat[:, SSQ:SSQ + 4], c),
                        waits=[("S", ts)], inc="V")
            tv = M.emit("vector", lambda E: E.tensor_tensor(out=s_stat[:, T0:T0 + 4], in0=s_stat[:, MU:MU + 4],
                                                            in1=s_stat[:, MU:MU + 4], op=ALU.mult), inc="V")
            tv = M.emit("vector", lambda E: E.tensor_tensor(out=s_stat[:, VARP:VARP + 4], in0=s_stat[:, MSQ:MSQ + 4],
                                                            in1=s_stat[:, T0:T0 + 4], op=ALU.subtract), inc="V")
            tv = M.emit("vector", lambda E: E.tensor_scalar_add(s_stat[:, VARP:VARP + 4], s_stat[:, VARP:VARP + 4],
                                                                1e-5), inc="V")
            ts = M.emit("scalar", lambda E: E.activation(out=s_stat[:, T0:T0 + 4], in_=s_stat[:, VARP:VARP + 4],
                                                         func=AF.Ln),
                        waits=[("V", tv)], inc="S")
            ts = M.emit("scalar", lambda E: E.activation(out=s_stat[:, R1:R1 + 4], in_=s_stat[:, T0:T0 + 4],
                                                         func=AF.Exp, scale=s_cm05[:]), inc="S")
            return tv, ts

        tv, ts = stats_and_norm(tok_x, "dx", R1)
        T1 = 48
        tv = M.emit("vector", lambda E: E.reciprocal(out=s_stat[:, T1:T1 + 4], in_=s_stat[:, VARP:VARP + 4]),
                    inc="V")
        tv = M.emit("vector", lambda E: E.tensor_scalar_add(s_stat[:, V2P:V2P + 4], s_stat[:, VARP:VARP + 4], -1e-5),
                    inc="V")
        tv = M.emit("vector", lambda E: E.tensor_tensor(out=s_stat[:, V2P:V2P + 4], in0=s_stat[:, V2P:V2P + 4],
                                                        in1=s_stat[:, T1:T1 + 4], op=ALU.mult), inc="V")
        tv = M.emit("vector", lambda E: E.tensor_scalar_add(s_stat[:, V2P:V2P + 4], s_stat[:, V2P:V2P + 4], 1e-5),
                    inc="V")
        ts = M.emit("scalar", lambda E: E.activation(out=s_stat[:, T1:T1 + 4], in_=s_stat[:, V2P:V2P + 4],
                                                     func=AF.Ln),
                    waits=[("V", tv), ("S", ts)], inc="S")
        ts = M.emit("scalar", lambda E: E.activation(out=s_stat[:, V2P:V2P + 4], in_=s_stat[:, T1:T1 + 4],
                                                     func=AF.Exp, scale=s_cm05[:]), inc="S")
        tv = M.emit("vector", lambda E: E.tensor_tensor(out=s_stat[:, ASC:ASC + 4], in0=s_stat[:, R1:R1 + 4],
                                                        in1=s_stat[:, V2P:V2P + 4], op=ALU.mult),
                    waits=[("S", ts)], inc="V")
        tv = M.emit("vector", lambda E: E.scalar_tensor_tensor(out=s_stat[:, NMA:NMA + 4], in0=s_stat[:, MU:MU + 4],
                                                               scalar=-1.0, in1=s_stat[:, ASC:ASC + 4],
                                                               op0=ALU.mult, op1=ALU.mult), inc="V")
        tok_xnfm = emit_norm_transpose(M, ASC, NMA)

        if dbg and t == 0:
            M.fence()
            M.emit("sync", lambda E: E.nop(), waits=[(k, cnt.get(k, 0)) for k in ("T", "V", "S")])
            dma(M, "sync", d_dbg["stat"], s_stat[:], "dc")
            dma(M, "sync", d_dbg["xnfm"], s_xnfm[:], "dc")
            for _e in ("vector", "scalar", "tensor", "gpsimd"):
                M.emit(_e, lambda E: E.nop(), waits=[("dc", cnt["dc"])])
        M.fence()
        qk_ev = []
        for oc in range(16):
            bank = pb[2 + (oc % 2)]
            tt = None
            for dc in range(DC):
                tt = mm(M, bank[:],
                        s_wall[:, dc * 3072 + oc * 128: dc * 3072 + (oc + 1) * 128],
                        s_xnfm[:, dc * TILE:(dc + 1) * TILE],
                        start=(dc == 0), stop=(dc == DC - 1),
                        waits=[("S", qk_ev[oc - 2] if oc >= 2 else 0)])
            dst = s_q if oc < 8 else s_k
            fc = oc % 8
            tok = M.emit("scalar",
                         lambda E, dst=dst, fc=fc, oc=oc, bank=bank:
                         E.activation(out=dst[:, fc * TILE:(fc + 1) * TILE], in_=bank[:],
                                      func=AF.Identity, bias=s_bw[:, oc:oc + 1]),
                         waits=[("T", tt)], inc="S")
            qk_ev.append(tok)
        v_ev = []
        for g in range(G):
            bh = [pb[4 + (g % 2) * 2], pb[5 + (g % 2) * 2]]
            tt = None
            for half in range(2):
                for dc in range(DC):
                    tt = mm(M, bh[half][:],
                            s_xnfm[:, dc * TILE + g * 128: dc * TILE + (g + 1) * 128],
                            s_wall[:, dc * 3072 + 2048 + half * 512: dc * 3072 + 2048 + (half + 1) * 512],
                            start=(dc == 0), stop=False,
                            waits=[("V", v_ev[g - 2] if g >= 2 else 0)])
                tt = mm(M, bh[half][:], s_onesb[:], s_bwv[0:1, half * 512:(half + 1) * 512],
                        start=False, stop=True)
            tok = None
            for half in range(2):
                tok = M.emit("vector",
                             lambda E, g=g, half=half, b=bh[half]:
                             E.tensor_copy(out=s_v[:, g * D + half * 512: g * D + (half + 1) * 512], in_=b[:]),
                             waits=[("T", tt)], inc="V")
            v_ev.append(tok)

        if dbg and t == 0:
            M.fence()
            M.emit("sync", lambda E: E.nop(), waits=[(k, cnt.get(k, 0)) for k in ("T", "V", "S")])
            dma(M, "sync", d_dbg["qd"], s_q[:], "dc")
            dma(M, "sync", d_dbg["kd"], s_k[:], "dc")
            dma(M, "sync", d_dbg["vd"], s_v[:], "dc")
            for _e in ("vector", "scalar", "tensor", "gpsimd"):
                M.emit(_e, lambda E: E.nop(), waits=[("dc", cnt["dc"])])
        M.fence()
        NI = G * H
        ctx = []
        for w in range(G):
            fw = (seq_start and w == 0)
            for h in range(H):
                ctx.append((w, h, fw))
        tokA = [0] * NI; tokB = [0] * NI; tokC = [0] * NI; tokD = [0] * NI
        tokE = [0] * NI; tokF = [0] * NI; tokG = [0] * NI; tokH = [0] * NI
        engH = [0] * NI

        def stA(i):
            w, h, fw = ctx[i]
            po = 64 * (h % 2); fc = h // 2
            qs = s_q[po:po + 64, fc * TILE + w * 128: fc * TILE + (w + 1) * 128]
            bankS = pb[i % 4]
            wts = [("S", qk_ev[8 + fc]), ("V", tokB[i - 4] if i >= 4 else 0)]
            if fw:
                tt = mm(M, bankS[:, 0:128], qs, s_k[po:po + 64, fc * TILE: fc * TILE + 128],
                        start=True, stop=True, tp=(po, 0), waits=wts)
            elif w == 0:
                tt = mm(M, bankS[:, 0:128], qs, s_kprev[po:po + 64, fc * 128:(fc + 1) * 128],
                        start=True, stop=True, tp=(po, 0), waits=wts)
                tt = mm(M, bankS[:, 128:256], qs, s_k[po:po + 64, fc * TILE: fc * TILE + 128],
                        start=True, stop=True, tp=(po, 0))
            else:
                tt = mm(M, bankS[:, 0:256], qs,
                        s_k[po:po + 64, fc * TILE + (w - 1) * 128: fc * TILE + (w + 1) * 128],
                        start=True, stop=True, tp=(po, 0), waits=wts)
            tokA[i] = tt

        def stB(i):
            w, h, fw = ctx[i]
            nk = 128 if fw else 256
            wh = i % 4
            msk = s_mask0 if fw else s_mask
            bankS = pb[wh]
            M.emit("vector",
                   lambda E, wh=wh, nk=nk, msk=msk, bankS=bankS:
                   E.tensor_tensor(out=s_sm[:, wh * 256: wh * 256 + nk], in0=bankS[:, 0:nk],
                                   in1=msk[:, 0:nk], op=ALU.add),
                   waits=[("T", tokA[i]), ("S", tokC[i - 4] if i >= 4 else 0)], inc="V")
            tokB[i] = M.emit("vector",
                             lambda E, wh=wh, nk=nk: E.reduce_max(out=s_stat[:, NEGMAX + wh:NEGMAX + wh + 1],
                                                                  in_=s_sm[:, wh * 256: wh * 256 + nk],
                                                                  axis=AX.X, negate=True),
                             inc="V")

        def stC(i):
            w, h, fw = ctx[i]
            nk = 128 if fw else 256
            wh = i % 4
            tokC[i] = M.emit("scalar",
                             lambda E, wh=wh, nk=nk: E.activation(out=s_probs[:, wh * 256: wh * 256 + nk],
                                                                  in_=s_sm[:, wh * 256: wh * 256 + nk],
                                                                  func=AF.Exp,
                                                                  bias=s_stat[:, NEGMAX + wh:NEGMAX + wh + 1],
                                                                  accum_out=s_stat[:, ROWSUM + wh:ROWSUM + wh + 1]),
                             waits=[("V", tokB[i]), ("T", tokE[i - 4] if i >= 4 else 0)], inc="S")

        def stD(i):
            w, h, fw = ctx[i]
            nk = 128 if fw else 256
            wh = i % 4
            M.emit("vector",
                   lambda E, wh=wh: E.reciprocal(out=s_stat[:, RECIP + wh:RECIP + wh + 1],
                                                 in_=s_stat[:, ROWSUM + wh:ROWSUM + wh + 1]),
                   waits=[("S", tokC[i])], inc="V")
            tokD[i] = M.emit("vector",
                             lambda E, wh=wh, nk=nk: E.tensor_scalar_mul(s_probs[:, wh * 256: wh * 256 + nk],
                                                                         s_probs[:, wh * 256: wh * 256 + nk],
                                                                         s_stat[:, RECIP + wh:RECIP + wh + 1]),
                             inc="V")

        def stE(i):
            w, h, fw = ctx[i]
            nk = 128 if fw else 256
            wh = i % 4
            bankT = pb[4 + i % 2]
            tt2 = None
            for kc in range(nk // 128):
                tt2 = trans(M, bankT[:, kc * 128:(kc + 1) * 128],
                            s_probs[:, wh * 256 + kc * 128: wh * 256 + (kc + 1) * 128],
                            s_identf[:],
                            waits=[("V", tokD[i]), ("S", tokF[i - 2] if i >= 2 else 0)])
            tokE[i] = tt2

        def stF(i):
            w, h, fw = ctx[i]
            nk = 128 if fw else 256
            wh = i % 4
            bankT = pb[4 + i % 2]
            tokF[i] = M.emit("scalar",
                             lambda E, wh=wh, nk=nk, bankT=bankT:
                             E.copy(out=s_pT[:, wh * 256: wh * 256 + nk], in_=bankT[:, 0:nk]),
                             waits=[("T", tokE[i])], inc="S")

        def stG(i):
            w, h, fw = ctx[i]
            wh = i % 4
            po = 64 * (h % 2)
            bankO = pb[6 + i % 2]
            pvw = [("S", tokF[i]), ("V", v_ev[w])]
            if i >= 2:
                pvw.append((("V", "S")[engH[i - 2]], tokH[i - 2]))
            if fw:
                tt3 = mm(M, bankO[po:po + 64, 0:128],
                         s_v[:, w * D + h * DH: w * D + (h + 1) * DH],
                         s_pT[:, wh * 256: wh * 256 + 128],
                         start=True, stop=True, tp=(0, po), waits=pvw)
            else:
                vprev_ap = (s_vprev[:, h * DH:(h + 1) * DH] if w == 0
                            else s_v[:, (w - 1) * D + h * DH: (w - 1) * D + h * DH + DH])
                tt3 = mm(M, bankO[po:po + 64, 0:128],
                         vprev_ap, s_pT[:, wh * 256: wh * 256 + 128],
                         start=True, stop=False, tp=(0, po), waits=pvw)
                tt3 = mm(M, bankO[po:po + 64, 0:128],
                         s_v[:, w * D + h * DH: w * D + (h + 1) * DH],
                         s_pT[:, wh * 256 + 128: wh * 256 + 256],
                         start=False, stop=True, tp=(0, po))
            tokG[i] = tt3

        def stH(i):
            w, h, fw = ctx[i]
            po = 64 * (h % 2); fc = h // 2
            bankO = pb[6 + i % 2]
            if i % 2 == 0:
                tokH[i] = M.emit("vector",
                                 lambda E, po=po, fc=fc, w=w, bankO=bankO:
                                 E.tensor_copy(out=s_ao[po:po + 64, fc * TILE + w * 128: fc * TILE + (w + 1) * 128],
                                               in_=bankO[po:po + 64, 0:128]),
                                 waits=[("T", tokG[i])], inc="V")
                engH[i] = 0
            else:
                tokH[i] = M.emit("scalar",
                                 lambda E, po=po, fc=fc, w=w, bankO=bankO:
                                 E.copy(out=s_ao[po:po + 64, fc * TILE + w * 128: fc * TILE + (w + 1) * 128],
                                        in_=bankO[po:po + 64, 0:128]),
                                 waits=[("T", tokG[i])], inc="S")
                engH[i] = 1

        stages = (stA, stB, stC, stD, stE, stF, stG, stH)
        for j in range(NI + len(stages) - 1):
            for k, st in enumerate(stages):
                i = j - k
                if 0 <= i < NI:
                    st(i)
        last_i = NI - 1
        pv_ev_last = M.emit("vector", lambda E: E.memset(s_stat[:, 60:61], 0.0),
                            waits=[(("V", "S")[engH[last_i]], tokH[last_i]),
                                   (("V", "S")[engH[last_i - 1]], tokH[last_i - 1])], inc="V")

        if (t + 1) % SEQ_TILES != 0:
            kp = M.emit("vector",
                        lambda E: E.tensor_copy(
                            out=s_kprev[:].rearrange("p (f c) -> p f c", c=128),
                            in_=s_k[:].rearrange("p (f t) -> p f t", t=TILE)[:, :, TILE - 128:TILE]),
                        waits=[("S", qk_ev[-1])], inc="V")
            vp = M.emit("vector",
                        lambda E: E.tensor_copy(out=s_vprev[:], in_=s_v[:, (G - 1) * D: G * D]),
                        waits=[("V", v_ev[-1])], inc="V")

        if dbg and t == 0:
            M.fence()
            M.emit("sync", lambda E: E.nop(), waits=[(k, cnt.get(k, 0)) for k in ("T", "V", "S")])
            dma(M, "sync", d_dbg["ao"], s_ao[:], "dc")
            for _e in ("vector", "scalar", "tensor", "gpsimd"):
                M.emit(_e, lambda E: E.nop(), waits=[("dc", cnt["dc"])])
        M.fence()
        res_toks = []
        for ic in range(DC):
            sl = ic % 2
            tok_w = M.emit("gpsimd",
                           lambda E, ic=ic, sl=sl: E.dma_start(out=s_woutb[:, sl * 1024:(sl + 1) * 1024],
                                                               in_=d_woutb[ic * 128:(ic + 1) * 128, :]),
                           waits=[("T", wout_cons[sl])], inc="wo%d" % sl, n=16)
            tt = None
            for g in range(G):
                for half in range(2):
                    tt = mm(M, pb[2 * g + half][:],
                            s_ao[:, ic * TILE + g * 128: ic * TILE + (g + 1) * 128],
                            s_woutb[:, sl * 1024 + half * 512: sl * 1024 + (half + 1) * 512],
                            start=(ic == 0), stop=(ic == DC - 1),
                            waits=[("wo%d" % sl, tok_w), ("V", pv_ev_last)])
            wout_cons[sl] = tt
        tok_T = M.emit("tensor", lambda E: E.nop(), inc="T")
        tvlast = None
        for g in range(G):
            for half in range(2):
                tvlast = M.emit("vector",
                                lambda E, g=g, half=half:
                                E.scalar_tensor_tensor(out=s_x[:, g * D + half * 512: g * D + (half + 1) * 512],
                                                       in0=pb[2 * g + half][:], scalar=1.0,
                                                       in1=s_x[:, g * D + half * 512: g * D + (half + 1) * 512],
                                                       op0=ALU.mult, op1=ALU.add),
                                waits=[("T", tok_T)], inc="V")

        if dbg and t == 0:
            M.fence()
            M.emit("sync", lambda E: E.nop(), waits=[(k, cnt.get(k, 0)) for k in ("T", "V", "S")])
            dma(M, "sync", d_dbg["x1"], s_x[:], "dc")
            for _e in ("vector", "scalar", "tensor", "gpsimd"):
                M.emit(_e, lambda E: E.nop(), waits=[("dc", cnt["dc"])])
        M.fence()
        tv, ts = stats_and_norm(tvlast, "V", R1)
        xnb_toks = []
        for g in range(G):
            tok = M.emit("vector",
                         lambda E, g=g: E.scalar_tensor_tensor(out=s_stat[:, NMA + g:NMA + g + 1],
                                                               in0=s_stat[:, MU + g:MU + g + 1], scalar=-1.0,
                                                               in1=s_stat[:, R1 + g:R1 + g + 1],
                                                               op0=ALU.mult, op1=ALU.mult),
                         waits=[("S", ts)], inc="V")
        tok_xnfm2 = emit_norm_transpose(M, R1, NMA)
        ev_tr = [tok_xnfm2]

        M.fence()
        h_toks = []
        for c in range(HC):
            sl = c % 2
            tok_w = M.emit("gpsimd",
                           lambda E, c=c, sl=sl: E.dma_start(out=s_wfcb[:, sl * 1024:(sl + 1) * 1024],
                                                             in_=d_wfcT[c * 128:(c + 1) * 128, :]),
                           waits=[("T", wfc_cons[sl])], inc="wf%d" % sl, n=16)
            bank = pb[2 + sl]
            tt = None
            for dc in range(DC):
                tt = mm(M, bank[:],
                        s_wfcb[:, sl * 1024 + dc * 128: sl * 1024 + (dc + 1) * 128],
                        s_xnfm[:, dc * TILE:(dc + 1) * TILE],
                        start=(dc == 0), stop=(dc == DC - 1),
                        waits=[("wf%d" % sl, tok_w), ("S", ev_tr[-1]),
                               ("S", h_toks[c - 2] if c >= 2 else 0)])
            wfc_cons[sl] = tt
            tok = M.emit("scalar",
                         lambda E, c=c, bank=bank: E.activation(out=s_h[:, c * TILE:(c + 1) * TILE],
                                                                in_=bank[:], func=AF.Gelu,
                                                                bias=s_bfc[:, c:c + 1]),
                         waits=[("T", tt)], inc="S")
            h_toks.append(tok)

        if dbg and t == 0:
            M.fence()
            M.emit("sync", lambda E: E.nop(), waits=[(k, cnt.get(k, 0)) for k in ("T", "V", "S")])
            dma(M, "sync", d_dbg["hd"], s_h[:], "dc")
            for _e in ("vector", "scalar", "tensor", "gpsimd"):
                M.emit(_e, lambda E: E.nop(), waits=[("dc", cnt["dc"])])
        M.fence()
        last_res = None
        for do in range(DC):
            sl = do % 2
            tok_w = M.emit("gpsimd",
                           lambda E, do=do, sl=sl: E.dma_start(out=s_wprb[:, sl * 4096:(sl + 1) * 4096],
                                                               in_=d_wprT[do * 128:(do + 1) * 128, :]),
                           waits=[("T", wpr_cons[sl])], inc="wp%d" % sl, n=16)
            bank = pb[0 + sl]
            tt = None
            for c in range(HC):
                tt = mm(M, bank[:],
                        s_wprb[:, sl * 4096 + c * 128: sl * 4096 + (c + 1) * 128],
                        s_h[:, c * TILE:(c + 1) * TILE],
                        start=(c == 0), stop=(c == HC - 1),
                        waits=[("wp%d" % sl, tok_w), ("S", h_toks[-1]),
                               ("V", ofm_cons[sl])])
            wpr_cons[sl] = tt
            tok_ev = M.emit("scalar",
                            lambda E, bank=bank, do=do:
                            E.activation(out=s_ofm[:, 0:512], in_=bank[:],
                                         func=AF.Identity, bias=s_bpr[:, do:do + 1]),
                            waits=[("T", tt), ("T", ofm_trans_last[0])], inc="S")
            bank2 = pb[4 + sl]
            tt2 = None
            for g in range(G):
                tt2 = trans(M, bank2[:, g * 128:(g + 1) * 128],
                            s_ofm[:, g * 128:(g + 1) * 128], s_identf[:],
                            waits=[("S", tok_ev), ("V", ofm_cons[sl])])
            ofm_trans_last[0] = tt2
            tok_res = M.emit("vector",
                             lambda E, do=do, bank2=bank2:
                             E.scalar_tensor_tensor(
                                 out=s_x[:].rearrange("p (g d) -> p g d", d=D)[:, :, do * 128:(do + 1) * 128],
                                 in0=bank2[:].rearrange("p (g c) -> p g c", c=128), scalar=1.0,
                                 in1=s_x[:].rearrange("p (g d) -> p g d", d=D)[:, :, do * 128:(do + 1) * 128],
                                 op0=ALU.mult, op1=ALU.add),
                             waits=[("T", tt2)], inc="V")
            ofm_cons[sl] = tok_res
            last_res = tok_res

        tok_store = M.emit("sync",
                           lambda E, t=t: E.dma_start(
                               out=d_y[t * TILE:(t + 1) * TILE, :].rearrange("(g p) d -> p g d", p=128),
                               in_=s_x[:].rearrange("p (g d) -> p g d", d=D)),
                           waits=[("V", last_res)], inc="dy", n=16)
        M.fence()
        M.emit("sync", lambda E: E.nop(), waits=[("dy", tok_store)])

    M.emit("sync", lambda E: E.nop(), waits=[("dy", cnt["dy"])])

    with nc.Block("main") as b2:
        M.replay(b2, sems)

    stack.close()
    return nc


_NC_CACHE = None


def kernel(L0_attn_x, L0_attn_ln_gamma, L0_attn_ln_beta, L0_attn_w_qkv, L0_attn_w_out,
           L0_ffn_W_fc, L0_ffn_b_fc, L0_ffn_A_fc, L0_ffn_B_fc,
           L0_ffn_W_proj, L0_ffn_b_proj, L0_ffn_A_proj, L0_ffn_B_proj):
    global _NC_CACHE
    x = np.ascontiguousarray(np.asarray(L0_attn_x, dtype=np.float32))
    B, N, Dm = x.shape
    ncores = 8
    per = B // ncores

    if _NC_CACHE is None:
        _NC_CACHE = build_program(dbg=DBG)
    nc = _NC_CACHE

    ident_f = np.eye(128, dtype=np.float32)
    ii = np.arange(128)[:, None]
    jj = np.arange(256)[None, :]
    prev = jj < 128
    jjm = np.where(prev, jj, jj - 128)
    bad = np.where(prev, jjm < ii, jjm > ii)
    mask_m = np.where(bad, np.float32(NEG), np.float32(0.0)).astype(BFNP)
    mask_0 = np.where(np.arange(128)[None, :] > ii, np.float32(NEG), np.float32(0.0)).astype(BFNP)
    ones_f = np.ones((1, 128), dtype=np.float32)

    common = dict(
        gamma=np.asarray(L0_attn_ln_gamma, np.float32),
        beta=np.asarray(L0_attn_ln_beta, np.float32),
        wqkv=np.asarray(L0_attn_w_qkv, np.float32),
        wout=np.asarray(L0_attn_w_out, np.float32),
        wfc=np.asarray(L0_ffn_W_fc, np.float32),
        bfc=np.asarray(L0_ffn_b_fc, np.float32),
        afc=np.asarray(L0_ffn_A_fc, np.float32),
        bfcB=np.asarray(L0_ffn_B_fc, np.float32),
        wpr=np.asarray(L0_ffn_W_proj, np.float32),
        bpr=np.asarray(L0_ffn_b_proj, np.float32),
        apr=np.asarray(L0_ffn_A_proj, np.float32),
        bprB=np.asarray(L0_ffn_B_proj, np.float32),
        identf=ident_f, maskm=mask_m, mask0=mask_0, onesf=ones_f,
        onesb=np.ones((1, 128), dtype=BFNP),
    )
    in_maps = []
    for c in range(ncores):
        m = dict(common)
        m["x"] = np.ascontiguousarray(x[c * per:(c + 1) * per].reshape(TOK, Dm))
        in_maps.append(m)

    res = run_bass_kernel_spmd(nc, in_maps, core_ids=list(range(ncores)))
    out = np.empty((B, N, Dm), np.float32)
    for c in range(ncores):
        out[c * per:(c + 1) * per] = res.results[c]["y"].reshape(per, N, Dm)
    if DBG:
        global DBG_RES
        DBG_RES = res.results
    return out
